# version 06 (final)
# speedup vs baseline: 1.1483x; 1.1422x over previous
import contextlib
import math
import numpy as np

import concourse.bass as bass
import concourse.mybir as mybir
from concourse.bass_utils import run_bass_kernel_spmd

F32 = mybir.dt.float32
BF16 = mybir.dt.bfloat16
U32 = mybir.dt.uint32
I32 = mybir.dt.int32
AF = mybir.ActivationFunctionType
ALU = mybir.AluOpType
AX = mybir.AxisListType

D = 1024
T = 2048
KC = 8
TT = 512
NTT = T // TT
LN_EPS = 1e-5


class Sem:
    def __init__(self, handle):
        self.h = handle
        self.n = 0

    def inc(self, inst, k=1):
        inst.then_inc(self.h, k)
        self.n += k
        return self.n

    def dma(self, inst):
        return self.inc(inst, 16)


class Prog:

    def __init__(self, shared=None):
        if shared is None:
            self.nc = bass.Bass("TRN2", target_bir_lowering=False,
                                detect_race_conditions=False)
            self.sems = {}
            self.prefix = ""
            self.x_ap = None
            self.y_ap = None
            self.root_ctx = contextlib.ExitStack()
        else:
            self.nc = shared["nc"]
            self.sems = shared["sems"]
            self.prefix = shared["prefix"]
            self.x_ap = shared["x_ap"]
            self.y_ap = shared["y_ap"]
            self.root_ctx = shared["root_ctx"]
        self.ctx = contextlib.ExitStack()
        self.block = None
        self.bufs = {}

    def dram_in(self, name, shape, dtype=F32):
        if name == "x" and self.x_ap is not None:
            return self.x_ap
        return self.nc.dram_tensor(self.prefix + name, list(shape), dtype,
                                   kind="ExternalInput").ap()

    def dram_out(self, name, shape, dtype=F32):
        if name == "y" and self.y_ap is not None:
            return self.y_ap
        return self.nc.dram_tensor(self.prefix + name, list(shape), dtype,
                                   kind="ExternalOutput").ap()

    def sbuf(self, name, shape, dtype):
        h = self.ctx.enter_context(
            self.nc.sbuf_tensor(self.prefix + name, list(shape), dtype))
        self.bufs[name] = h
        return h

    def psum(self, name, shape, dtype=F32):
        h = self.ctx.enter_context(
            self.nc.psum_tensor(self.prefix + name, list(shape), dtype))
        self.bufs[name] = h
        return h

    def sem(self, name):
        if name not in self.sems:
            self.sems[name] = Sem(self.root_ctx.enter_context(self.nc.semaphore(name)))
        return self.sems[name]

    def start_block(self):
        self.block = self.ctx.enter_context(self.nc.Block())

    def close(self):
        self.ctx.close()


def fm_w(w_T):
    w_T = np.ascontiguousarray(w_T)
    din, dout = w_T.shape
    return np.ascontiguousarray(w_T.reshape(din // 128, 128, dout).transpose(1, 0, 2))


def fm_vec(v):
    v = np.asarray(v)
    return np.ascontiguousarray(v.reshape(-1, 128).T)


def fm_x(x):
    if x is None:
        return None
    return np.ascontiguousarray(np.asarray(x).T.reshape(KC, 128, T).transpose(1, 0, 2))


def fm_x_inv(x_fm):
    return np.ascontiguousarray(x_fm.transpose(1, 0, 2).reshape(D, T).T)


def host_consts_common():
    c = {}
    psw = np.zeros((128, 128), np.float32)
    for k in range(128):
        psw[k, k ^ 1] = 1.0
    c["p_swap"] = psw
    j = np.arange(896)[None, :]
    kr = np.arange(128)[:, None]
    c["causal_strip"] = (j >= kr + 384).astype(np.float32)
    return c


def host_consts_rope():
    dh = 64
    inv_freq = 1.0 / (10000.0 ** (np.arange(0, dh, 2, dtype=np.float64) / dh))
    t = np.arange(T, dtype=np.float64)
    ang = np.outer(inv_freq, t)
    cos_i = np.cos(ang)
    sin_i = np.sin(ang)
    p = np.arange(128)
    i_of_p = (p % 64) // 2
    C = cos_i[i_of_p, :].astype(np.float32)
    sgn = np.where(p % 2 == 0, 1.0, -1.0)
    S2 = (sin_i[i_of_p, :] * sgn[:, None]).astype(np.float32)
    return {"rope_C": C, "rope_S2": S2}



def emit_ln_stream(p, x_hbm, xn_sb, ld_dma_engine="sync", f32_cb=None):
    block = p.block
    ts, vs, ss, ld = p.sem("ts"), p.sem("vs"), p.sem("ss"), p.sem("ld")
    x_t = p.bufs["x_t"]
    xbf = p.bufs["xbf_scr"]
    sq = p.bufs["sq_scr"]
    ones_bf = p.bufs["ones_bf"]
    onesrow_f32 = p.bufs["onesrow_f32"]
    scr = p.bufs["scr_f32"]
    row = p.bufs["row_scr"]
    ps_a, ps_b, ps_c = p.bufs["ps_a"], p.bufs["ps_b"], p.bufs["ps_c"]
    ln_t1 = scr[:, 0:TT]

    for tt in range(NTT):
        t0 = tt * TT
        apply_done_prev = vs.n

        @block.sync
        def _(s):
            s.wait_ge(vs.h, apply_done_prev)
            ld.dma(s.dma_start(out=x_t[:, :, :], in_=x_hbm[:, :, t0:t0 + TT]))

        x_ld = ld.n
        stats_done_prev = ts.n

        @block.vector
        def _(v):
            v.wait_ge(ld.h, x_ld)
            v.wait_ge(ts.h, stats_done_prev)
            for kc in range(KC):
                inst = v.tensor_copy(xbf[:, kc, :], x_t[:, kc, :])
            vs.inc(inst)

        @block.scalar
        def _(s):
            s.wait_ge(ld.h, x_ld)
            s.wait_ge(ts.h, stats_done_prev)
            for kc in range(KC):
                inst = s.activation(sq[:, kc, :], x_t[:, kc, :], AF.Square)
            ss.inc(inst)

        v_ready, s_ready = vs.n, ss.n

        @block.tensor
        def _(t):
            t.wait_ge(vs.h, v_ready)
            t.wait_ge(ss.h, s_ready)
            for kc in range(KC):
                t.matmul(ps_c[0:1, :], ones_bf[:, 0:1], xbf[:, kc, :],
                         start=(kc == 0), stop=(kc == KC - 1))
            for kc in range(KC):
                inst = t.matmul(ps_c[32:33, :], ones_bf[:, 0:1], sq[:, kc, :],
                                start=(kc == 0), stop=(kc == KC - 1))
            ts.inc(inst)

        t_ready = ts.n
        mu = row[0:1, 0:TT]
        var = row[0:1, TT:2 * TT]
        rr = row[0:1, 2 * TT:3 * TT]
        mur = row[0:1, 3 * TT:4 * TT]

        @block.vector
        def _(v):
            v.wait_ge(ts.h, t_ready)
            v.tensor_scalar_mul(mu, ps_c[0:1, :], 1.0 / D)
            v.tensor_scalar_mul(var, ps_c[32:33, :], 1.0 / D)
            v.tensor_tensor(rr, mu, mu, ALU.mult)
            v.tensor_tensor(var, var, rr, ALU.subtract)
            inst = v.tensor_scalar_add(var, var, LN_EPS)
            vs.inc(inst)

        vv = vs.n

        @block.scalar
        def _(s):
            s.wait_ge(vs.h, vv)
            inst = s.activation(rr, var, AF.Sqrt)
            ss.inc(inst)

        sv = ss.n

        @block.vector
        def _(v):
            v.wait_ge(ss.h, sv)
            inst = v.reciprocal(rr, rr)
            vs.inc(inst)
            v.wait_ge(vs.h, vs.n)
            inst = v.tensor_tensor(mur, mu, rr, ALU.mult)
            vs.inc(inst)

        vv2 = vs.n

        @block.tensor
        def _(t):
            t.wait_ge(vs.h, vv2)
            t.matmul(ps_a[:, :], onesrow_f32[0:1, :], rr, start=True, stop=True)
            inst = t.matmul(ps_b[:, :], onesrow_f32[0:1, :], mur, start=True, stop=True)
            ts.inc(inst)

        tv2 = ts.n

        if f32_cb is None:
            @block.vector
            def _(v):
                v.wait_ge(ts.h, tv2)
                for kc in range(KC):
                    v.tensor_tensor(ln_t1, x_t[:, kc, :], ps_a[:, :], ALU.mult)
                    inst = v.tensor_tensor(xn_sb[:, kc, t0:t0 + TT], ln_t1,
                                           ps_b[:, :], ALU.subtract)
                vs.inc(inst)
        else:
            ln_f32 = p.bufs["ln_f32"]
            for kc in range(KC):
                cb_prev = ts.n

                @block.vector
                def _(v):
                    if kc == 0:
                        v.wait_ge(ts.h, tv2)
                    v.wait_ge(ts.h, cb_prev)
                    v.tensor_tensor(ln_t1, x_t[:, kc, :], ps_a[:, :], ALU.mult)
                    v.tensor_tensor(ln_f32[:, :], ln_t1, ps_b[:, :], ALU.subtract)
                    inst = v.tensor_copy(xn_sb[:, kc, t0:t0 + TT], ln_f32[:, :])
                    vs.inc(inst)

                f32_cb(kc, tt, vs.n)


def emit_proj_fm(p, src_sb, w_sb, ps, epilogue, n_kc=KC, oc_list=None, tt_list=None,
                 pre_wait_v=0, pre_wait_ld=0, ld_sem=None):
    block = p.block
    ts, vs, ld = p.sem("ts"), p.sem("vs"), p.sem("ld")
    for oc in (oc_list if oc_list is not None else range(KC)):
        for tt in (tt_list if tt_list is not None else range(NTT)):
            t0 = tt * TT
            evac_prev = vs.n

            @block.tensor
            def _(t):
                t.wait_ge(vs.h, max(evac_prev, pre_wait_v))
                if pre_wait_ld:
                    t.wait_ge((ld_sem or ld).h, pre_wait_ld)
                for kc in range(n_kc):
                    inst = t.matmul(ps[:, :], w_sb[:, kc, oc * 128:(oc + 1) * 128],
                                    src_sb[:, kc, t0:t0 + TT],
                                    start=(kc == 0), stop=(kc == n_kc - 1))
                ts.inc(inst)

            epilogue(oc, tt, ts.inc.__self__.n if False else ts.n)



def emit_attn64(p, qf, kf, v65, av, v_done, chunk_plan, scale):
    block = p.block
    ts, vs, ss = p.sem("ts"), p.sem("vs"), p.sem("ss")
    ps_s, ps_av, ps_r = p.bufs["ps_s"], p.bufs["ps_av"], p.bufs["ps_r"]
    pt = [p.bufs["pt0"], p.bufs["pt1"]]
    shift_sb = p.bufs["shift_sb"]
    onesrow_f32 = p.bufs["onesrow_f32"]
    row_scr = p.bufs["row_scr"]
    scr_f32 = p.bufs["scr_f32"]
    avs = scr_f32[0:64, 0:TT]
    for h in range(16):
        kc_h = h // 2
        prow = (h % 2) * 64
        for qt in range(NTT):
            q0 = qt * TT
            kts = chunk_plan(h, qt)
            av_evac_prev = vs.n
            for i, (kt, mask) in enumerate(kts):
                k0 = kt * 128
                slot = pt[i % 2]
                exp_prev = ss.n

                @block.tensor
                def _(t):
                    t.wait_ge(ss.h, exp_prev)
                    if i == 0:
                        t.wait_ge(vs.h, max(v_done, av_evac_prev))
                    inst = t.matmul(ps_s[:, :],
                                    kf[prow:prow + 64, kc_h, k0:k0 + 128],
                                    qf[prow:prow + 64, kc_h, q0:q0 + TT],
                                    start=True, stop=True)
                    ts.inc(inst)

                sc_done = ts.n
                mask_prev = vs.n

                @block.scalar
                def _(s):
                    s.wait_ge(ts.h, sc_done)
                    if mask is not None:
                        s.wait_ge(vs.h, mask_prev)
                    inst = s.activation(slot[:, :], ps_s[:, :], AF.Exp,
                                        bias=shift_sb[:, 0:1], scale=scale)
                    ss.inc(inst)

                e_done = ss.n
                if mask is not None:
                    @block.vector
                    def _(v):
                        v.wait_ge(ss.h, e_done)
                        if mask[0] == "strip":
                            dd = mask[1]
                            strip = p.bufs["strip"]
                            inst = v.tensor_tensor(slot[:, :], slot[:, :],
                                                   strip[:, 384 + dd:384 + dd + TT],
                                                   ALU.mult)
                        elif mask[0] == "m5":
                            l2m = p.bufs["l2m"]
                            inst = v.tensor_tensor(slot[:, :], slot[:, :],
                                                   l2m[:, mask[1], :], ALU.mult)
                        elif mask[0] == "p16":
                            mask16 = p.bufs["mask16"]
                            inst = v.tensor_scalar(slot[:, :], slot[:, :],
                                                   mask16[:, 0:1], None, ALU.mult)
                        vs.inc(inst)
                    m_done = vs.n
                else:
                    m_done = None

                @block.tensor
                def _(t):
                    if m_done is None:
                        t.wait_ge(ss.h, e_done)
                    else:
                        t.wait_ge(vs.h, m_done)
                    inst = t.matmul(ps_av[:, :], v65[:, kt, h * 65:h * 65 + 65],
                                    slot[:, :], start=(i == 0), stop=(i == len(kts) - 1))
                    ts.inc(inst)

            av_done = ts.n
            r_row = row_scr[0:1, 0:TT]

            @block.vector
            def _(v):
                v.wait_ge(ts.h, av_done)
                inst = v.reciprocal(r_row, ps_av[64:65, :])
                vs.inc(inst)
                v.wait_ge(vs.h, vs.n)

            r_done = vs.n

            @block.tensor
            def _(t):
                t.wait_ge(vs.h, r_done)
                inst = t.matmul(ps_r[:, :], onesrow_f32[0:1, 0:64], r_row,
                                start=True, stop=True)
                ts.inc(inst)

            rb_done = ts.n

            @block.vector
            def _(v):
                v.wait_ge(ts.h, rb_done)
                v.tensor_copy(avs, ps_av[0:64, :])
                inst = v.tensor_tensor(av[prow:prow + 64, kc_h, q0:q0 + TT],
                                       avs, ps_r[:, :], ALU.mult)
                vs.inc(inst)
    return vs.n


def build_l1_attn(shared=None):
    p = Prog(shared)
    x_hbm = p.dram_in("x", [128, KC, T])
    w_hbm = {n: p.dram_in(n, [128, KC, D]) for n in ["wqT", "wkT", "wvT", "woT"]}
    b_hbm = {n: p.dram_in(n, [128, KC]) for n in ["bq", "bk", "bv", "bo"]}
    bvrow_hbm = p.dram_in("bv_row", [1, D])
    ropeC_h = p.dram_in("rope_C", [128, T])
    ropeS2_h = p.dram_in("rope_S2", [128, T])
    pswap_h = p.dram_in("p_swap", [128, 128])
    strip_h = p.dram_in("causal_strip", [128, 896])
    y_hbm = p.dram_out("y", [128, KC, T])

    xn = p.sbuf("xn", [128, KC, T], BF16)
    qf = p.sbuf("qf", [128, KC, T], BF16)
    kf = p.sbuf("kf", [128, KC, T], BF16)
    v65 = p.sbuf("v65", [128, T // 128, 16 * 65], BF16)
    wa = p.sbuf("wa", [128, KC, D], BF16)
    wb = p.sbuf("wb", [128, KC, D], BF16)
    x_t = p.sbuf("x_t", [128, KC, TT], F32)
    b_sb = {n: p.sbuf(n + "_sb", [128, KC], F32) for n in ["bq", "bk", "bv", "bo"]}
    ropeC = p.sbuf("ropeC", [128, T], BF16)
    ropeS2 = p.sbuf("ropeS2", [128, T], BF16)
    pswap = p.sbuf("pswap", [128, 128], BF16)
    strip = p.sbuf("strip", [128, 896], BF16)
    ones_bf = p.sbuf("ones_bf", [128, 1], BF16)
    shift_sb = p.sbuf("shift_sb", [128, 1], F32)
    onesrow_bf = p.sbuf("onesrow_bf", [1, 128], BF16)
    onesrow_f32 = p.sbuf("onesrow_f32", [1, 128], F32)
    bvb = p.sbuf("bvb", [128, D], F32)
    p.bufs["xbf_scr"] = qf[:, :, 0:TT]
    p.bufs["sq_scr"] = kf[:, :, 0:TT]
    scr_f32 = p.sbuf("scr_f32", [128, 2 * TT], F32)
    row_scr = p.sbuf("row_scr", [1, 4 * TT], F32)
    rope_s2 = p.sbuf("rope_s2", [128, TT], BF16)
    pt = [p.sbuf("pt0", [128, TT], BF16), p.sbuf("pt1", [128, TT], BF16)]
    r_bf = p.sbuf("r_bf", [1, TT], BF16)
    xres = x_t[:, 0:2, :]

    ps_s = p.psum("ps_s", [128, TT], F32)
    ps_av = p.psum("ps_av", [65, TT], F32)
    ps_a = p.psum("ps_a", [128, TT], F32)
    ps_b = p.psum("ps_b", [128, TT], F32)
    ps_c = p.psum("ps_c", [33, TT], F32)
    ps_v0 = p.psum("ps_v0", [128, TT], F32)
    ps_v1 = p.psum("ps_v1", [128, TT], F32)
    ps_r = p.psum("ps_r", [64, TT], F32)

    p.start_block()
    block = p.block
    ld, ts, vs, ss, os_ = p.sem("ld"), p.sem("ts"), p.sem("vs"), p.sem("ss"), p.sem("os")
    bvrow = row_scr[0:1, 0:D]

    @block.sync
    def _(s):
        for n in ["bq", "bk", "bv", "bo"]:
            ld.dma(s.dma_start(out=b_sb[n][:, :], in_=b_hbm[n]))
        ld.dma(s.dma_start(out=bvrow, in_=bvrow_hbm))

    lg = p.sem("lg")

    @block.gpsimd
    def _(g):
        lg.dma(g.dma_start(out=wa[:, :, :], in_=w_hbm["wqT"]))
        lg.dma(g.dma_start(out=wb[:, :, :], in_=w_hbm["wkT"]))
        lg.dma(g.dma_start(out=ropeC[:, :], in_=ropeC_h))
        lg.dma(g.dma_start(out=ropeS2[:, :], in_=ropeS2_h))
        lg.dma(g.dma_start(out=pswap[:, :], in_=pswap_h))
        lg.dma(g.dma_start(out=strip[:, :], in_=strip_h))

    all_loaded = ld.n
    all_loaded_g = lg.n

    @block.vector
    def _(v):
        v.wait_ge(ld.h, all_loaded)
        v.wait_ge(lg.h, all_loaded_g)
        v.memset(ones_bf[:, :], 1.0)
        v.memset(shift_sb[:, :], -8.0)
        v.memset(onesrow_bf[:, :], 1.0)
        v.memset(onesrow_f32[:, :], 1.0)
        inst = v.memset(v65[:, :, :], 1.0)
        vs.inc(inst)

    init_done = vs.n

    @block.tensor
    def _(t):
        t.wait_ge(ld.h, all_loaded)
        t.wait_ge(vs.h, init_done)
        t.matmul(ps_v0[:, :], onesrow_f32[0:1, :], bvrow[0:1, 0:512], start=True, stop=True)
        inst = t.matmul(ps_v1[:, :], onesrow_f32[0:1, :], bvrow[0:1, 512:1024],
                        start=True, stop=True)
        ts.inc(inst)

    bvb_mm = ts.n

    @block.vector
    def _(v):
        v.wait_ge(ts.h, bvb_mm)
        v.tensor_copy(bvb[:, 0:512], ps_v0[:, :])
        inst = v.tensor_copy(bvb[:, 512:1024], ps_v1[:, :])
        vs.inc(inst)

    emit_ln_stream(p, x_hbm, xn)
    ln_done = vs.n

    def rope_epilogue(bias_sb, out_sb):
        def epi(oc, tt, mm_done):
            t0 = tt * TT
            rope_s1 = scr_f32[:, 0:TT]
            rope_s3 = scr_f32[:, TT:2 * TT]

            @block.vector
            def _(v):
                v.wait_ge(ts.h, mm_done)
                v.tensor_scalar(rope_s1, ps_s[:, :], bias_sb[:, oc:oc + 1],
                                None, ALU.add)
                inst = v.tensor_tensor(rope_s2[:, :], rope_s1,
                                       ropeS2[:, t0:t0 + TT], ALU.mult)
                vs.inc(inst)

            u_done = vs.n

            @block.tensor
            def _(t):
                t.wait_ge(vs.h, u_done)
                inst = t.matmul(ps_a[:, :], pswap[:, :], rope_s2[:, :],
                                start=True, stop=True)
                ts.inc(inst)

            sw_done = ts.n

            @block.vector
            def _(v):
                v.tensor_tensor(rope_s3, rope_s1, ropeC[:, t0:t0 + TT], ALU.mult)
                v.wait_ge(ts.h, sw_done)
                inst = v.tensor_tensor(out_sb[:, oc, t0:t0 + TT], rope_s3,
                                       ps_a[:, :], ALU.add)
                vs.inc(inst)
        return epi

    emit_proj_fm(p, xn, wa, ps_s, rope_epilogue(b_sb["bq"], qf), pre_wait_v=ln_done)
    q_mm_done = ts.n

    @block.gpsimd
    def _(g):
        g.wait_ge(ts.h, q_mm_done)
        lg.dma(g.dma_start(out=wa[:, :, :], in_=w_hbm["wvT"]))

    wv_loaded = lg.n

    emit_proj_fm(p, xn, wb, ps_s, rope_epilogue(b_sb["bk"], kf), pre_wait_v=ln_done)
    k_mm_done = ts.n

    @block.gpsimd
    def _(g):
        g.wait_ge(ts.h, k_mm_done)
        lg.dma(g.dma_start(out=wb[:, :, :], in_=w_hbm["woT"]))

    wo_loaded = lg.n

    for t128 in range(T // 128):
        evac_prev = vs.n

        @block.tensor
        def _(t):
            t.wait_ge(vs.h, max(evac_prev, ln_done))
            t.wait_ge(lg.h, wv_loaded)
            for kc in range(KC):
                t.matmul(ps_v0[:, :], xn[:, kc, t128 * 128:(t128 + 1) * 128],
                         wa[:, kc, 0:512], start=(kc == 0), stop=(kc == KC - 1))
            for kc in range(KC):
                inst = t.matmul(ps_v1[:, :], xn[:, kc, t128 * 128:(t128 + 1) * 128],
                                wa[:, kc, 512:1024], start=(kc == 0), stop=(kc == KC - 1))
            ts.inc(inst)

        vmm = ts.n

        @block.vector
        def _(v):
            v.wait_ge(ts.h, vmm)
            view = v65[:, t128, :].rearrange("p (h c) -> p h c", c=65)
            vs0 = ps_v0[:, :].rearrange("p (h c) -> p h c", c=64)
            vs1 = ps_v1[:, :].rearrange("p (h c) -> p h c", c=64)
            b0 = bvb[:, 0:512].rearrange("p (h c) -> p h c", c=64)
            b1 = bvb[:, 512:1024].rearrange("p (h c) -> p h c", c=64)
            v.tensor_tensor(view[:, 0:8, 0:64], vs0, b0, ALU.add)
            inst = v.tensor_tensor(view[:, 8:16, 0:64], vs1, b1, ALU.add)
            vs.inc(inst)

    v_done = vs.n
    av = xn

    def l1_chunks(h, qt):
        out = []
        for kt in range(0, min(16, qt * 4 + 4)):
            dd = qt * TT - kt * 128
            out.append((kt, ("strip", dd) if dd < 127 else None))
        return out

    emit_attn64(p, qf, kf, v65, av, v_done, l1_chunks, 0.125)
    attn_done = vs.n

    def out_epilogue(oc, tt, mm_done):
        t0 = tt * TT
        slot = (oc * NTT + tt) % 2
        y_scr = scr_f32[:, slot * TT:(slot + 1) * TT]
        stt_prev2 = max(0, vs.n - 1)

        @block.sync
        def _(s):
            s.wait_ge(vs.h, stt_prev2)
            ld.dma(s.dma_start(out=xres[:, slot, :], in_=x_hbm[:, oc, t0:t0 + TT]))

        x_ld = ld.n
        os_prev2 = max(0, os_.n - 16)

        @block.vector
        def _(v):
            v.wait_ge(ts.h, mm_done)
            v.wait_ge(ld.h, x_ld)
            v.wait_ge(os_.h, os_prev2)
            inst = v.scalar_tensor_tensor(y_scr, ps_s[:, :], b_sb["bo"][:, oc:oc + 1],
                                          xres[:, slot, :], ALU.add, ALU.add)
            vs.inc(inst)

        y_done = vs.n

        @block.sync
        def _(s):
            s.wait_ge(vs.h, y_done)
            os_.dma(s.dma_start(out=y_hbm[:, oc, t0:t0 + TT], in_=y_scr))

    emit_proj_fm(p, av, wb, ps_s, out_epilogue, pre_wait_v=attn_done,
                 pre_wait_ld=wo_loaded, ld_sem=lg)

    @block.sync
    def _(s):
        s.wait_ge(os_.h, os_.n)

    p.close()
    return p


def host_inputs_l1_attn(x_row, w):
    cc = host_consts_common()
    rc = host_consts_rope()
    return {
        "x": fm_x(x_row),
        "wqT": fm_w(np.ascontiguousarray(w["Wq"].T)),
        "wkT": fm_w(np.ascontiguousarray(w["Wk"].T)),
        "wvT": fm_w(np.ascontiguousarray(w["Wv"].T)),
        "woT": fm_w(np.ascontiguousarray(w["Wo"].T)),
        "bq": fm_vec(w["bq"]), "bk": fm_vec(w["bk"]),
        "bv": fm_vec(w["bv"]), "bo": fm_vec(w["bo"]),
        "bv_row": np.ascontiguousarray(w["bv"][None, :]),
        "rope_C": rc["rope_C"], "rope_S2": rc["rope_S2"],
        "p_swap": cc["p_swap"], "causal_strip": cc["causal_strip"],
    }




def build_l2_attn(shared=None):
    p = Prog(shared)
    x_hbm = p.dram_in("x", [128, KC, T])
    w_hbm = {n: p.dram_in(n, [128, KC, D]) for n in ["wqT", "wkT", "wvT", "woT"]}
    l2m_h = p.dram_in("l2_masks", [128, 5, TT])
    mask16_h = p.dram_in("mask16_in", [128, 1])
    y_hbm = p.dram_out("y", [128, KC, T])

    xn = p.sbuf("xn", [128, KC, T], BF16)
    qf = p.sbuf("qf", [128, KC, T], BF16)
    kf = p.sbuf("kf", [128, KC, T], BF16)
    v65 = p.sbuf("v65", [128, T // 128, 16 * 65], BF16)
    wa = p.sbuf("wa", [128, KC, D], BF16)
    wb = p.sbuf("wb", [128, KC, D], BF16)
    x_t = p.sbuf("x_t", [128, KC, TT], F32)
    l2m = p.sbuf("l2m", [128, 5, TT], BF16)
    mask16 = p.sbuf("mask16", [128, 1], F32)
    ones_bf = p.sbuf("ones_bf", [128, 1], BF16)
    shift_sb = p.sbuf("shift_sb", [128, 1], F32)
    onesrow_f32 = p.sbuf("onesrow_f32", [1, 128], F32)
    p.bufs["xbf_scr"] = qf[:, :, 0:TT]
    p.bufs["sq_scr"] = kf[:, :, 0:TT]
    scr_f32 = p.sbuf("scr_f32", [128, 2 * TT], F32)
    row_scr = p.sbuf("row_scr", [1, 4 * TT], F32)
    pt = [p.sbuf("pt0", [128, TT], BF16), p.sbuf("pt1", [128, TT], BF16)]
    xres = x_t[:, 0:2, :]

    ps_s = p.psum("ps_s", [128, TT], F32)
    ps_av = p.psum("ps_av", [65, TT], F32)
    ps_a = p.psum("ps_a", [128, TT], F32)
    ps_b = p.psum("ps_b", [128, TT], F32)
    ps_c = p.psum("ps_c", [33, TT], F32)
    ps_v0 = p.psum("ps_v0", [128, TT], F32)
    ps_v1 = p.psum("ps_v1", [128, TT], F32)
    ps_r = p.psum("ps_r", [64, TT], F32)

    p.start_block()
    block = p.block
    ld, ts, vs, ss, os_ = p.sem("ld"), p.sem("ts"), p.sem("vs"), p.sem("ss"), p.sem("os")
    lg = p.sem("lg")

    @block.sync
    def _(s):
        ld.dma(s.dma_start(out=mask16[:, :], in_=mask16_h))

    @block.gpsimd
    def _(g):
        lg.dma(g.dma_start(out=wa[:, :, :], in_=w_hbm["wqT"]))
        lg.dma(g.dma_start(out=wb[:, :, :], in_=w_hbm["wkT"]))
        lg.dma(g.dma_start(out=l2m[:, :, :], in_=l2m_h))

    all_loaded = ld.n
    all_loaded_g = lg.n

    @block.vector
    def _(v):
        v.wait_ge(ld.h, all_loaded)
        v.wait_ge(lg.h, all_loaded_g)
        v.memset(ones_bf[:, :], 1.0)
        v.memset(shift_sb[:, :], -8.0)
        v.memset(onesrow_f32[:, :], 1.0)
        inst = v.memset(v65[:, :, :], 1.0)
        vs.inc(inst)

    @block.scalar
    def _(s):
        s.wait_ge(ld.h, all_loaded)

    emit_ln_stream(p, x_hbm, xn)
    ln_done = vs.n

    def cast_epilogue(out_sb):
        def epi(oc, tt, mm_done):
            t0 = tt * TT

            @block.vector
            def _(v):
                v.wait_ge(ts.h, mm_done)
                inst = v.tensor_copy(out_sb[:, oc, t0:t0 + TT], ps_s[:, :])
                vs.inc(inst)
        return epi

    emit_proj_fm(p, xn, wa, ps_s, cast_epilogue(qf), pre_wait_v=ln_done)
    q_mm_done = ts.n

    @block.gpsimd
    def _(g):
        g.wait_ge(ts.h, q_mm_done)
        lg.dma(g.dma_start(out=wa[:, :, :], in_=w_hbm["wvT"]))

    wv_loaded = lg.n

    emit_proj_fm(p, xn, wb, ps_s, cast_epilogue(kf), pre_wait_v=ln_done)
    k_mm_done = ts.n

    @block.gpsimd
    def _(g):
        g.wait_ge(ts.h, k_mm_done)
        lg.dma(g.dma_start(out=wb[:, :, :], in_=w_hbm["woT"]))

    wo_loaded = lg.n

    for t128 in range(T // 128):
        evac_prev = vs.n

        @block.tensor
        def _(t):
            t.wait_ge(vs.h, max(evac_prev, ln_done))
            t.wait_ge(lg.h, wv_loaded)
            for kc in range(KC):
                t.matmul(ps_v0[:, :], xn[:, kc, t128 * 128:(t128 + 1) * 128],
                         wa[:, kc, 0:512], start=(kc == 0), stop=(kc == KC - 1))
            for kc in range(KC):
                inst = t.matmul(ps_v1[:, :], xn[:, kc, t128 * 128:(t128 + 1) * 128],
                                wa[:, kc, 512:1024], start=(kc == 0), stop=(kc == KC - 1))
            ts.inc(inst)

        vmm = ts.n

        @block.vector
        def _(v):
            v.wait_ge(ts.h, vmm)
            view = v65[:, t128, :].rearrange("p (h c) -> p h c", c=65)
            vs0 = ps_v0[:, :].rearrange("p (h c) -> p h c", c=64)
            vs1 = ps_v1[:, :].rearrange("p (h c) -> p h c", c=64)
            v.tensor_copy(view[:, 0:8, 0:64], vs0)
            inst = v.tensor_copy(view[:, 8:16, 0:64], vs1)
            vs.inc(inst)

    v_done = vs.n
    av = xn

    def l2_chunks(h, qt):
        if qt == 0:
            return [(0, ("m5", 4)), (1, ("m5", 1)), (2, ("m5", 2)), (3, ("m5", 3))]
        out = [(0, ("p16",))]
        for j in range(4):
            out.append((qt * 4 + j, ("m5", j)))
        return out

    emit_attn64(p, qf, kf, v65, av, v_done, l2_chunks, 0.125)
    attn_done = vs.n

    def out_epilogue(oc, tt, mm_done):
        t0 = tt * TT
        slot = (oc * NTT + tt) % 2
        y_scr = scr_f32[:, slot * TT:(slot + 1) * TT]
        stt_prev2 = max(0, vs.n - 1)

        @block.sync
        def _(s):
            s.wait_ge(vs.h, stt_prev2)
            ld.dma(s.dma_start(out=xres[:, slot, :], in_=x_hbm[:, oc, t0:t0 + TT]))

        x_ld = ld.n
        os_prev2 = max(0, os_.n - 16)

        @block.vector
        def _(v):
            v.wait_ge(ts.h, mm_done)
            v.wait_ge(ld.h, x_ld)
            v.wait_ge(os_.h, os_prev2)
            inst = v.tensor_add(y_scr, ps_s[:, :], xres[:, slot, :])
            vs.inc(inst)

        y_done = vs.n

        @block.sync
        def _(s):
            s.wait_ge(vs.h, y_done)
            os_.dma(s.dma_start(out=y_hbm[:, oc, t0:t0 + TT], in_=y_scr))

    emit_proj_fm(p, av, wb, ps_s, out_epilogue, pre_wait_v=attn_done,
                 pre_wait_ld=wo_loaded, ld_sem=lg)

    @block.sync
    def _(s):
        s.wait_ge(os_.h, os_.n)

    p.close()
    return p


def host_consts_l2():
    kr = np.arange(128)[:, None]
    qc = np.arange(TT)[None, :]
    masks = np.zeros((128, 5, TT), np.float32)
    for j in range(4):
        sb = ((2 * j + kr // 64) == (qc // 64)) & (128 * j + kr <= qc)
        masks[:, j, :] = sb.astype(np.float32)
    glob = (kr < 16) & (kr <= qc)
    masks[:, 4, :] = np.maximum(masks[:, 0, :], glob.astype(np.float32))
    mask16 = (np.arange(128) < 16).astype(np.float32)[:, None]
    return {"l2_masks": masks, "mask16": np.ascontiguousarray(mask16)}


def host_inputs_l2_attn(x_row, w):
    c = host_consts_l2()
    return {
        "x": fm_x(x_row),
        "wqT": fm_w(np.ascontiguousarray(w["Wq"].T)),
        "wkT": fm_w(np.ascontiguousarray(w["Wk"].T)),
        "wvT": fm_w(np.ascontiguousarray(w["Wv"].T)),
        "woT": fm_w(np.ascontiguousarray(w["Wo"].T)),
        "l2_masks": c["l2_masks"], "mask16_in": c["mask16"],
    }




def build_l3_attn(shared=None):
    p = Prog(shared)
    x_hbm = p.dram_in("x", [128, KC, T])
    w_hbm = {n: p.dram_in(n, [128, KC, D]) for n in ["wqT", "wkT", "wvT", "woT"]}
    bo_h = p.dram_in("bo", [128, KC])
    memx_h = p.dram_in("memx_hb", [128, 4, 2, 384])
    betas_h = p.dram_in("betas", [1, 4])
    strip_h = p.dram_in("causal_strip", [128, 896])
    y_hbm = p.dram_out("y", [128, KC, T])

    xn = p.sbuf("xn", [128, KC, T], BF16)
    qf = p.sbuf("qf", [128, KC, T], BF16)
    kf = p.sbuf("kf", [128, KC, T], BF16)
    v_tm = p.sbuf("v_tm", [128, T // 128, D], BF16)
    wa = p.sbuf("wa", [128, KC, D], BF16)
    wb = p.sbuf("wb", [128, KC, D], BF16)
    x_t = p.sbuf("x_t", [128, KC, TT], F32)
    bo_sb = p.sbuf("bo_sb", [128, KC], F32)
    memx = p.sbuf("memx", [128, 4, 2, 384], BF16)
    strip = p.sbuf("strip", [128, 896], BF16)
    g_bc = p.sbuf("g_bc", [128, 4], F32)
    ones_bf = p.sbuf("ones_bf", [128, 1], BF16)
    shift_sb = p.sbuf("shift_sb", [128, 1], F32)
    onesrow_f32 = p.sbuf("onesrow_f32", [1, 128], F32)
    p.bufs["xbf_scr"] = qf[:, :, 0:TT]
    p.bufs["sq_scr"] = kf[:, :, 0:TT]
    scr_f32 = p.sbuf("scr_f32", [128, 2 * TT], F32)
    scr2_f32 = p.sbuf("scr2_f32", [128, 2 * TT], F32)
    row_scr = p.sbuf("row_scr", [1, 4 * TT], F32)
    sq_t = p.sbuf("sq_t", [128, 2, TT], BF16)
    pt = [p.sbuf("pt0", [128, TT], BF16), p.sbuf("pt1", [128, TT], BF16)]
    betarow = p.sbuf("betarow", [1, 4], F32)
    xres = x_t[:, 0:2, :]

    ps_s = p.psum("ps_s", [128, TT], F32)
    ps_av0 = p.psum("ps_av0", [128, TT], F32)
    ps_av1 = p.psum("ps_av1", [128, TT], F32)
    ps_d = p.psum("ps_d", [1, TT], F32)
    ps_a = p.psum("ps_a", [128, TT], F32)
    ps_b = p.psum("ps_b", [128, TT], F32)
    ps_c = p.psum("ps_c", [33, TT], F32)
    ps_r = p.psum("ps_r", [128, TT], F32)

    p.start_block()
    block = p.block
    ld, ts, vs, ss, os_ = p.sem("ld"), p.sem("ts"), p.sem("vs"), p.sem("ss"), p.sem("os")
    lg = p.sem("lg")

    @block.sync
    def _(s):
        ld.dma(s.dma_start(out=bo_sb[:, :], in_=bo_h))
        ld.dma(s.dma_start(out=betarow[:, :], in_=betas_h))

    @block.gpsimd
    def _(g):
        lg.dma(g.dma_start(out=wa[:, :, :], in_=w_hbm["wqT"]))
        lg.dma(g.dma_start(out=wb[:, :, :], in_=w_hbm["wkT"]))
        lg.dma(g.dma_start(out=memx[:, :, :, :], in_=memx_h))
        lg.dma(g.dma_start(out=strip[:, :], in_=strip_h))

    all_loaded = ld.n
    all_loaded_g = lg.n

    @block.vector
    def _(v):
        v.wait_ge(ld.h, all_loaded)
        v.wait_ge(lg.h, all_loaded_g)
        v.memset(ones_bf[:, :], 1.0)
        v.memset(shift_sb[:, :], -20.0)
        inst = v.memset(onesrow_f32[:, :], 1.0)
        vs.inc(inst)

    init_done = vs.n

    @block.scalar
    def _(s):
        s.wait_ge(ld.h, all_loaded)
        s.wait_ge(vs.h, init_done)
        inst = s.activation(betarow[:, :], betarow[:, :], AF.Sigmoid)
        ss.inc(inst)

    sig_done = ss.n

    @block.tensor
    def _(t):
        t.wait_ge(ss.h, sig_done)
        inst = t.matmul(ps_r[:, 0:4], onesrow_f32[0:1, :], betarow[0:1, :],
                        start=True, stop=True)
        ts.inc(inst)

    gb_mm = ts.n

    @block.vector
    def _(v):
        v.wait_ge(ts.h, gb_mm)
        inst = v.tensor_copy(g_bc[:, :], ps_r[:, 0:4])
        vs.inc(inst)

    @block.scalar
    def _(s):
        s.wait_ge(ld.h, all_loaded)

    emit_ln_stream(p, x_hbm, xn)
    ln_done = vs.n

    def cast_epilogue(out_sb):
        def epi(oc, tt, mm_done):
            t0 = tt * TT

            @block.vector
            def _(v):
                v.wait_ge(ts.h, mm_done)
                inst = v.tensor_copy(out_sb[:, oc, t0:t0 + TT], ps_s[:, :])
                vs.inc(inst)
        return epi

    emit_proj_fm(p, xn, wa, ps_s, cast_epilogue(qf), pre_wait_v=ln_done)
    q_mm_done = ts.n

    @block.gpsimd
    def _(g):
        g.wait_ge(ts.h, q_mm_done)
        lg.dma(g.dma_start(out=wa[:, :, :], in_=w_hbm["wvT"]))

    wv_loaded = lg.n

    emit_proj_fm(p, xn, wb, ps_s, cast_epilogue(kf), pre_wait_v=ln_done)
    k_mm_done = ts.n

    @block.gpsimd
    def _(g):
        g.wait_ge(ts.h, k_mm_done)
        lg.dma(g.dma_start(out=wb[:, :, :], in_=w_hbm["woT"]))

    wo_loaded = lg.n

    for t128 in range(T // 128):
        evac_prev = vs.n

        @block.tensor
        def _(t):
            t.wait_ge(vs.h, max(evac_prev, ln_done))
            t.wait_ge(lg.h, wv_loaded)
            for kc in range(KC):
                t.matmul(ps_av0[:, :], xn[:, kc, t128 * 128:(t128 + 1) * 128],
                         wa[:, kc, 0:512], start=(kc == 0), stop=(kc == KC - 1))
            for kc in range(KC):
                inst = t.matmul(ps_av1[:, :], xn[:, kc, t128 * 128:(t128 + 1) * 128],
                                wa[:, kc, 512:1024], start=(kc == 0), stop=(kc == KC - 1))
            ts.inc(inst)

        vmm = ts.n

        @block.vector
        def _(v):
            v.wait_ge(ts.h, vmm)
            v.tensor_copy(v_tm[:, t128, 0:512], ps_av0[:, :])
            inst = v.tensor_copy(v_tm[:, t128, 512:1024], ps_av1[:, :])
            vs.inc(inst)

    v_done = vs.n
    av = xn

    for h in range(4):
        for qt in range(NTT):
            q0 = qt * TT
            kts = list(range(0, min(16, qt * 4 + 4)))
            av_evac_prev = vs.n
            for i, kt in enumerate(kts):
                k0 = kt * 128
                dd = q0 - k0
                partial = dd < 127
                slot = pt[i % 2]
                exp_prev = ss.n

                @block.tensor
                def _(t):
                    t.wait_ge(ss.h, exp_prev)
                    if i == 0:
                        t.wait_ge(vs.h, max(v_done, av_evac_prev))
                    for j in range(2):
                        inst = t.matmul(ps_s[:, :],
                                        kf[:, 2 * h + j, k0:k0 + 128],
                                        qf[:, 2 * h + j, q0:q0 + TT],
                                        start=(j == 0), stop=(j == 1))
                    ts.inc(inst)

                sc_done = ts.n
                mask_prev = vs.n

                @block.scalar
                def _(s):
                    s.wait_ge(ts.h, sc_done)
                    if partial:
                        s.wait_ge(vs.h, mask_prev)
                    inst = s.activation(slot[:, :], ps_s[:, :], AF.Exp,
                                        bias=shift_sb[:, 0:1], scale=1.0)
                    ss.inc(inst)

                e_done = ss.n
                if partial:
                    @block.vector
                    def _(v):
                        v.wait_ge(ss.h, e_done)
                        inst = v.tensor_tensor(slot[:, :], slot[:, :],
                                               strip[:, 384 + dd:384 + dd + TT], ALU.mult)
                        vs.inc(inst)
                    m_done = vs.n
                else:
                    m_done = None

                @block.tensor
                def _(t):
                    if m_done is None:
                        t.wait_ge(ss.h, e_done)
                    else:
                        t.wait_ge(vs.h, m_done)
                    t.matmul(ps_av0[:, :], v_tm[:, kt, h * 256:h * 256 + 128],
                             slot[:, :], start=(i == 0), stop=(i == len(kts) - 1))
                    t.matmul(ps_av1[:, :], v_tm[:, kt, h * 256 + 128:h * 256 + 256],
                             slot[:, :], start=(i == 0), stop=(i == len(kts) - 1))
                    inst = t.matmul(ps_d[:, :], ones_bf[:, 0:1],
                                    slot[:, :], start=(i == 0), stop=(i == len(kts) - 1))
                    ts.inc(inst)

            av_done = ts.n
            r_row = row_scr[0:1, 0:TT]

            @block.vector
            def _(v):
                v.wait_ge(ts.h, av_done)
                inst = v.reciprocal(r_row, ps_d[0:1, :])
                vs.inc(inst)
                v.wait_ge(vs.h, vs.n)

            r_done = vs.n

            @block.tensor
            def _(t):
                t.wait_ge(vs.h, r_done)
                inst = t.matmul(ps_r[:, :], onesrow_f32[0:1, :], r_row,
                                start=True, stop=True)
                ts.inc(inst)

            rb_done = ts.n

            @block.vector
            def _(v):
                v.wait_ge(ts.h, rb_done)
                avs = scr_f32[:, 0:TT]
                v.tensor_copy(avs, ps_av0[:, :])
                v.tensor_tensor(av[:, 2 * h, q0:q0 + TT], avs, ps_r[:, :], ALU.mult)
                v.tensor_copy(avs, ps_av1[:, :])
                inst = v.tensor_tensor(av[:, 2 * h + 1, q0:q0 + TT], avs, ps_r[:, :],
                                       ALU.mult)
                vs.inc(inst)

    attn_done = vs.n

    for h in range(4):
        for tt in range(NTT):
            t0 = tt * TT
            sq_prev = ts.n

            @block.vector
            def _(v):
                v.wait_ge(ts.h, sq_prev)
                v.wait_ge(vs.h, attn_done)
                v.tensor_scalar_min(scr_f32[:, 0:TT],
                                    qf[:, 2 * h, t0:t0 + TT], 0.0)
                inst = v.tensor_scalar_min(scr_f32[:, TT:2 * TT],
                                           qf[:, 2 * h + 1, t0:t0 + TT], 0.0)
                vs.inc(inst)

            min_done = vs.n

            @block.scalar
            def _(s):
                s.wait_ge(vs.h, min_done)
                s.activation(scr2_f32[:, 0:TT], scr_f32[:, 0:TT], AF.Exp)
                inst = s.activation(scr2_f32[:, TT:2 * TT], scr_f32[:, TT:2 * TT], AF.Exp)
                ss.inc(inst)

            exp_done = ss.n

            @block.vector
            def _(v):
                v.wait_ge(ss.h, exp_done)
                for j in range(2):
                    inst = v.scalar_tensor_tensor(sq_t[:, j, :],
                                                  qf[:, 2 * h + j, t0:t0 + TT], 0.0,
                                                  scr2_f32[:, j * TT:(j + 1) * TT],
                                                  ALU.max, ALU.add)
                vs.inc(inst)

            sq_done = vs.n

            @block.tensor
            def _(t):
                t.wait_ge(vs.h, sq_done)
                for ec, psm in [(0, ps_av0), (1, ps_av1), (2, ps_s)]:
                    for j in range(2):
                        inst = t.matmul(psm[:, :], memx[:, h, j, ec * 128:(ec + 1) * 128],
                                        sq_t[:, j, :], start=(j == 0), stop=(j == 1))
                ts.inc(inst)

            am_done = ts.n
            rz_row = row_scr[0:1, TT:2 * TT]

            @block.vector
            def _(v):
                v.wait_ge(ts.h, am_done)
                v.tensor_scalar_add(rz_row, ps_s[0:1, :], 1e-6)
                inst = v.reciprocal(rz_row, rz_row)
                vs.inc(inst)
                v.wait_ge(vs.h, vs.n)

            rz_done = vs.n

            @block.tensor
            def _(t):
                t.wait_ge(vs.h, rz_done)
                inst = t.matmul(ps_r[:, :], onesrow_f32[0:1, :], rz_row,
                                start=True, stop=True)
                ts.inc(inst)

            rzb_done = ts.n

            @block.vector
            def _(v):
                v.wait_ge(ts.h, rzb_done)
                for j, psm in [(0, ps_av0), (1, ps_av1)]:
                    amem = scr_f32[:, 0:TT]
                    dsc = scr2_f32[:, 0:TT]
                    v.tensor_copy(amem, psm[:, :])
                    v.tensor_tensor(amem, amem, ps_r[:, :], ALU.mult)
                    v.tensor_sub(dsc, amem, av[:, 2 * h + j, t0:t0 + TT])
                    inst = v.scalar_tensor_tensor(av[:, 2 * h + j, t0:t0 + TT],
                                                  dsc, g_bc[:, h:h + 1],
                                                  av[:, 2 * h + j, t0:t0 + TT],
                                                  ALU.mult, ALU.add)
                vs.inc(inst)

    comb_done = vs.n

    def out_epilogue(oc, tt, mm_done):
        t0 = tt * TT
        slot = (oc * NTT + tt) % 2
        y_scr = scr_f32[:, slot * TT:(slot + 1) * TT]
        stt_prev2 = max(0, vs.n - 1)

        @block.sync
        def _(s):
            s.wait_ge(vs.h, stt_prev2)
            ld.dma(s.dma_start(out=xres[:, slot, :], in_=x_hbm[:, oc, t0:t0 + TT]))

        x_ld = ld.n
        os_prev2 = max(0, os_.n - 16)

        @block.vector
        def _(v):
            v.wait_ge(ts.h, mm_done)
            v.wait_ge(ld.h, x_ld)
            v.wait_ge(os_.h, os_prev2)
            inst = v.scalar_tensor_tensor(y_scr, ps_s[:, :], bo_sb[:, oc:oc + 1],
                                          xres[:, slot, :], ALU.add, ALU.add)
            vs.inc(inst)

        y_done = vs.n

        @block.sync
        def _(s):
            s.wait_ge(vs.h, y_done)
            os_.dma(s.dma_start(out=y_hbm[:, oc, t0:t0 + TT], in_=y_scr))

    emit_proj_fm(p, av, wb, ps_s, out_epilogue, pre_wait_v=comb_done,
                 pre_wait_ld=wo_loaded, ld_sem=lg)

    @block.sync
    def _(s):
        s.wait_ge(os_.h, os_.n)

    p.close()
    return p


def host_inputs_l3_attn(x_row, w):
    cc = host_consts_common()
    mem = np.asarray(w["mem"])
    z = np.asarray(w["z"])
    memx = np.zeros((4, 256, 384), np.float32)
    memx[:, :, 0:256] = mem
    memx[:, :, 256:257] = z
    memx_l = memx.reshape(4, 2, 128, 384).transpose(2, 0, 1, 3)
    return {
        "x": fm_x(x_row),
        "wqT": fm_w(np.ascontiguousarray(w["Wq"].T)),
        "wkT": fm_w(np.ascontiguousarray(w["Wk"].T)),
        "wvT": fm_w(np.ascontiguousarray(w["Wv"].T)),
        "woT": fm_w(np.ascontiguousarray(w["Wo"].T)),
        "bo": fm_vec(w["bo"]),
        "memx_hb": np.ascontiguousarray(memx_l),
        "betas": np.ascontiguousarray(np.asarray(w["betas"]).reshape(1, 4)),
        "causal_strip": cc["causal_strip"],
    }




def emit_ln_sbuf(p, src_of, out_fn, sq_scr, ps_a, ps_b, ps_c, tts=None, pre_wait_v=0):
    block = p.block
    ts, vs, ss = p.sem("ts"), p.sem("vs"), p.sem("ss")
    ones_bf = p.bufs["ones_bf"]
    onesrow_f32 = p.bufs["onesrow_f32"]
    row = p.bufs["row_scr"]
    for tt in (tts if tts is not None else range(NTT)):
        stats_done_prev = ts.n

        @block.scalar
        def _(s):
            s.wait_ge(ts.h, stats_done_prev)
            s.wait_ge(vs.h, pre_wait_v)
            for kc in range(KC):
                inst = s.activation(sq_scr[:, kc, :], src_of(kc, tt), AF.Square)
            ss.inc(inst)

        s_ready = ss.n
        src_ready = vs.n

        @block.tensor
        def _(t):
            t.wait_ge(ss.h, s_ready)
            t.wait_ge(vs.h, max(src_ready, pre_wait_v))
            for kc in range(KC):
                t.matmul(ps_c[0:1, :], ones_bf[:, 0:1], src_of(kc, tt),
                         start=(kc == 0), stop=(kc == KC - 1))
            for kc in range(KC):
                inst = t.matmul(ps_c[32:33, :], ones_bf[:, 0:1], sq_scr[:, kc, :],
                                start=(kc == 0), stop=(kc == KC - 1))
            ts.inc(inst)

        t_ready = ts.n
        mu = row[0:1, 0:TT]
        var = row[0:1, TT:2 * TT]
        rr = row[0:1, 2 * TT:3 * TT]
        mur = row[0:1, 3 * TT:4 * TT]

        @block.vector
        def _(v):
            v.wait_ge(ts.h, t_ready)
            v.tensor_scalar_mul(mu, ps_c[0:1, :], 1.0 / D)
            v.tensor_scalar_mul(var, ps_c[32:33, :], 1.0 / D)
            v.tensor_tensor(rr, mu, mu, ALU.mult)
            v.tensor_tensor(var, var, rr, ALU.subtract)
            inst = v.tensor_scalar_add(var, var, LN_EPS)
            vs.inc(inst)

        vv = vs.n

        @block.scalar
        def _(s):
            s.wait_ge(vs.h, vv)
            inst = s.activation(rr, var, AF.Sqrt)
            ss.inc(inst)

        sv = ss.n

        @block.vector
        def _(v):
            v.wait_ge(ss.h, sv)
            inst = v.reciprocal(rr, rr)
            vs.inc(inst)
            v.wait_ge(vs.h, vs.n)
            inst = v.tensor_tensor(mur, mu, rr, ALU.mult)
            vs.inc(inst)

        vv2 = vs.n

        @block.tensor
        def _(t):
            t.wait_ge(vs.h, vv2)
            t.matmul(ps_a[:, :], onesrow_f32[0:1, :], rr, start=True, stop=True)
            inst = t.matmul(ps_b[:, :], onesrow_f32[0:1, :], mur, start=True, stop=True)
            ts.inc(inst)

        tv2 = ts.n

        @block.vector
        def _(v):
            v.wait_ge(ts.h, tv2)
            for kc in range(KC):
                out_fn(v, kc, tt, ps_a, ps_b)



def _l0_bail(p, y_hbm, buf):
    block = p.block
    os_ = p.sem("os")
    vs = p.sem("vs")

    @block.sync
    def _(s):
        s.wait_ge(vs.h, vs.n)
        for tt in range(NTT):
            os_.dma(s.dma_start(out=y_hbm[:, :, tt * TT:(tt + 1) * TT], in_=buf))
        s.wait_ge(os_.h, os_.n)


def build_l0_attn(shared=None):
    import os as _os
    STOP = _os.environ.get("L0_STOP", "")
    NORM = 64.0 ** (-0.25)
    EPS_K = 1e-4
    p = Prog(shared)
    x_hbm = p.dram_in("x", [128, KC, T])
    w_hbm = {n: p.dram_in(n, [128, KC, D]) for n in ["wqT", "wkT", "wvT", "woT"]}
    b_hbm = {n: p.dram_in(n, [128, KC]) for n in ["bq", "bk", "bv", "bo",
                                                  "g2", "be2", "g3", "be3", "b2v"]}
    b1_hbm = p.dram_in("b1v", [128, 32])
    bvrow_hbm = p.dram_in("bv_row", [1, D])
    projt2_h = p.dram_in("projT2", [128, 256])
    ident_h = p.dram_in("ident", [128, 128])
    w1T_hbm = p.dram_in("w1T", [128, KC, 4096])
    w2T_hbm = p.dram_in("w2T", [128, 32, D])
    y_hbm = p.dram_out("y", [128, KC, T])

    xn = p.sbuf("xn", [128, KC, T], BF16)
    qf = p.sbuf("qf", [128, KC, T], BF16)
    kf = p.sbuf("kf", [128, KC, T], BF16)
    v_tm = p.sbuf("v_tm", [128, T // 128, D], BF16)
    wa = p.sbuf("wa", [128, KC, D], BF16)
    wb = p.sbuf("wb", [128, KC, D], BF16)
    x_t = p.sbuf("x_t", [128, KC, TT], F32)
    b_sb = {n: p.sbuf(n + "_sb", [128, KC], F32) for n in
            ["bq", "bk", "bv", "bo", "g2", "be2", "g3", "be3", "b2v"]}
    b1_sb = p.sbuf("b1_sb", [128, 32], F32)
    projt2 = p.sbuf("projt2", [128, 256], BF16)
    ident_bf = p.sbuf("ident_bf", [128, 128], BF16)
    ones_bf = p.sbuf("ones_bf", [128, 1], BF16)
    onesrow_f32 = p.sbuf("onesrow_f32", [1, 128], F32)
    bvb = p.sbuf("bvb", [128, D], F32)
    scr_f32 = p.sbuf("scr_f32", [128, 2 * TT], F32)
    row_scr = p.sbuf("row_scr", [1, 4 * TT], F32)
    q2_scr = p.sbuf("q2_scr", [128, T], BF16)
    diag_sc = p.sbuf("diag_sc", [128, 20], F32)
    ksum_bf = p.sbuf("ksum_bf", [128, 2], BF16)
    ctx_sb = p.sbuf("ctx_sb", [128, 2, 64], BF16)
    veps_sb = p.sbuf("veps_sb", [128, 64], F32)
    kmax_sc = p.sbuf("kmax_sc", [128, 18], F32)
    ff_acc = xn.ap().bitcast(F32)[:, :, 0:TT]
    p.bufs["xbf_scr"] = qf[:, :, 0:TT]
    p.bufs["sq_scr"] = kf[:, :, 0:TT]
    xres2 = x_t[:, 0:2, :]

    ps0 = p.psum("ps0", [128, TT], F32)
    ps1 = p.psum("ps1", [128, TT], F32)
    ps_c = p.psum("ps_c", [33, TT], F32)
    ps3 = p.psum("ps3", [1, TT], F32)
    ps4 = p.psum("ps4", [128, 1024], BF16)
    ps5 = p.psum("ps5", [128, TT], F32)
    ps6 = p.psum("ps6", [128, TT], F32)
    ps7 = p.psum("ps7", [128, TT], F32)
    p.bufs["ps_a"] = ps0
    p.bufs["ps_b"] = ps1

    p.start_block()
    block = p.block
    ld, ts, vs, ss, os_ = p.sem("ld"), p.sem("ts"), p.sem("vs"), p.sem("ss"), p.sem("os")
    lg = p.sem("lg")
    bvrow = row_scr[0:1, 0:D]

    @block.sync
    def _(s):
        for n in b_sb:
            ld.dma(s.dma_start(out=b_sb[n][:, :], in_=b_hbm[n]))
        ld.dma(s.dma_start(out=b1_sb[:, :], in_=b1_hbm))
        ld.dma(s.dma_start(out=bvrow, in_=bvrow_hbm))

    @block.gpsimd
    def _(g):
        lg.dma(g.dma_start(out=wa[:, :, :], in_=w_hbm["wqT"]))
        lg.dma(g.dma_start(out=wb[:, :, :], in_=w_hbm["wkT"]))
        lg.dma(g.dma_start(out=projt2[:, :], in_=projt2_h))
        lg.dma(g.dma_start(out=ident_bf[:, :], in_=ident_h))

    all_loaded = ld.n
    all_loaded_g = lg.n

    @block.vector
    def _(v):
        v.wait_ge(ld.h, all_loaded)
        v.wait_ge(lg.h, all_loaded_g)
        v.memset(ones_bf[:, :], 1.0)
        inst = v.memset(onesrow_f32[:, :], 1.0)
        vs.inc(inst)

    init_done = vs.n

    @block.tensor
    def _(t):
        t.wait_ge(ld.h, all_loaded)
        t.wait_ge(vs.h, init_done)
        t.matmul(ps5[:, :], onesrow_f32[0:1, :], bvrow[0:1, 0:512], start=True, stop=True)
        inst = t.matmul(ps6[:, :], onesrow_f32[0:1, :], bvrow[0:1, 512:1024],
                        start=True, stop=True)
        ts.inc(inst)

    bvb_mm = ts.n

    @block.vector
    def _(v):
        v.wait_ge(ts.h, bvb_mm)
        v.tensor_copy(bvb[:, 0:512], ps5[:, :])
        inst = v.tensor_copy(bvb[:, 512:1024], ps6[:, :])
        vs.inc(inst)

    @block.scalar
    def _(s):
        s.wait_ge(ld.h, all_loaded)

    emit_ln_stream(p, x_hbm, xn)
    ln_done = vs.n
    if STOP == "ln":
        _l0_bail(p, y_hbm, ff_acc)
        p.close()
        return p

    def bias_epilogue(bias_sb, out_sb):
        def epi(oc, tt, mm_done):
            t0 = tt * TT

            @block.vector
            def _(v):
                v.wait_ge(ts.h, mm_done)
                inst = v.tensor_scalar(out_sb[:, oc, t0:t0 + TT], ps0[:, :],
                                       bias_sb[:, oc:oc + 1], None, ALU.add)
                vs.inc(inst)
        return epi

    emit_proj_fm(p, xn, wa, ps0, bias_epilogue(b_sb["bq"], qf), pre_wait_v=ln_done)
    q_mm_done = ts.n
    if STOP == "q":
        _l0_bail(p, y_hbm, ff_acc)
        p.close()
        return p

    @block.gpsimd
    def _(g):
        g.wait_ge(ts.h, q_mm_done)
        lg.dma(g.dma_start(out=wa[:, :, :], in_=w_hbm["wvT"]))

    wv_loaded = lg.n

    emit_proj_fm(p, xn, wb, ps0, bias_epilogue(b_sb["bk"], kf), pre_wait_v=ln_done)
    k_mm_done = ts.n

    @block.gpsimd
    def _(g):
        g.wait_ge(ts.h, k_mm_done)
        lg.dma(g.dma_start(out=wb[:, :, :], in_=w_hbm["woT"]))

    wo_loaded = lg.n

    for t128 in range(T // 128):
        evac_prev = vs.n

        @block.tensor
        def _(t):
            t.wait_ge(vs.h, max(evac_prev, ln_done))
            t.wait_ge(lg.h, wv_loaded)
            for kc in range(KC):
                t.matmul(ps5[:, :], xn[:, kc, t128 * 128:(t128 + 1) * 128],
                         wa[:, kc, 0:512], start=(kc == 0), stop=(kc == KC - 1))
            for kc in range(KC):
                inst = t.matmul(ps6[:, :], xn[:, kc, t128 * 128:(t128 + 1) * 128],
                                wa[:, kc, 512:1024], start=(kc == 0), stop=(kc == KC - 1))
            ts.inc(inst)

        vmm = ts.n

        @block.vector
        def _(v):
            v.wait_ge(ts.h, vmm)
            v.tensor_add(v_tm[:, t128, 0:512], ps5[:, :], bvb[:, 0:512])
            inst = v.tensor_add(v_tm[:, t128, 512:1024], ps6[:, :], bvb[:, 512:1024])
            vs.inc(inst)

    v_done = vs.n
    if STOP == "v":
        _l0_bail(p, y_hbm, ff_acc)
        p.close()
        return p

    wa_flat = wa.ap().rearrange("p c d -> p (c d)")
    kp_tm = wa_flat[:, 0:4096].rearrange("p (n m) -> p n m", m=256)
    qpT_buf = wa_flat[:, 4096:8192].rearrange("p (m t) -> p m t", m=2)
    xpk_store = x_t.ap().rearrange("p c t -> p (c t)").rearrange(
        "p (n m) -> p n m", m=256)

    class _StopEmit(Exception):
        pass

    NHEADS = 1 if STOP.startswith("h1") else 16
    try:
        for h in range(NHEADS):
            kc_h = h // 2
            prow = (h % 2) * 64

            for nt in range(16):
                n0 = nt * 128
                prev_use = vs.n

                @block.tensor
                def _(t):
                    t.wait_ge(vs.h, max(v_done, prev_use))
                    inst = t.matmul(ps0[:, 0:256], kf[prow:prow + 64, kc_h, n0:n0 + 128],
                                    projt2[prow:prow + 64, :], start=True, stop=True)
                    ts.inc(inst)

                mmk = ts.n

                @block.vector
                def _(v):
                    v.wait_ge(ts.h, mmk)
                    v.tensor_copy(xpk_store[:, nt, :], ps0[:, 0:256])
                    inst = v.reduce_max(kmax_sc[:, nt:nt + 1], ps0[:, 0:256], axis=AX.X)
                    vs.inc(inst)
                    v.wait_ge(vs.h, vs.n)

            if STOP == "h1kx":
                raise _StopEmit
            @block.vector
            def _(v):
                inst = v.reduce_max(kmax_sc[:, 16:17], kmax_sc[:, 0:16], axis=AX.X)
                vs.inc(inst)
                v.wait_ge(vs.h, vs.n)
                inst = v.tensor_copy(q2_scr[:, 0:1], kmax_sc[:, 16:17])
                vs.inc(inst)

            gk1 = vs.n
            tp_prev = vs.n

            @block.tensor
            def _(t):
                t.wait_ge(vs.h, gk1)
                inst = t.transpose(ps4[0:1, 0:128], q2_scr[:, 0:1], ident_bf[:, :])
                ts.inc(inst)

            gk2 = ts.n

            @block.vector
            def _(v):
                v.wait_ge(ts.h, gk2)
                inst = v.reduce_max(row_scr[0:1, 0:1], ps4[0:1, 0:128], axis=AX.X)
                vs.inc(inst)
                v.wait_ge(vs.h, vs.n)

            gk3 = vs.n

            @block.tensor
            def _(t):
                t.wait_ge(vs.h, gk3)
                inst = t.matmul(ps7[:, 0:1], onesrow_f32[0:1, :], row_scr[0:1, 0:1],
                                start=True, stop=True)
                ts.inc(inst)

            gk4 = ts.n

            @block.vector
            def _(v):
                v.wait_ge(ts.h, gk4)
                inst = v.tensor_scalar_mul(diag_sc[:, 16:17], ps7[:, 0:1], -NORM)
                vs.inc(inst)

            sgneg_done = vs.n
            if STOP == "h1gmax":
                raise _StopEmit

            @block.scalar
            def _(s):
                s.wait_ge(vs.h, sgneg_done)
                inst = s.activation(q2_scr[prow:prow + 64, :], kf[prow:prow + 64, kc_h, :],
                                    AF.Square)
                ss.inc(inst)

            k2_done = ss.n
            for ti in range(NTT):
                ev_prev = vs.n

                @block.tensor
                def _(t):
                    if ti == 0:
                        t.wait_ge(ss.h, k2_done)
                    t.wait_ge(vs.h, ev_prev)
                    inst = t.matmul(ps3[0:1, :], ones_bf[prow:prow + 64, 0:1],
                                    q2_scr[prow:prow + 64, ti * TT:(ti + 1) * TT],
                                    start=True, stop=True)
                    ts.inc(inst)

                tdone = ts.n

                @block.vector
                def _(v):
                    v.wait_ge(ts.h, tdone)
                    inst = v.tensor_scalar_mul(row_scr[0:1, ti * TT:(ti + 1) * TT],
                                               ps3[0:1, :], 0.5 * NORM * NORM)
                    vs.inc(inst)

            diag_rows = vs.n

            @block.tensor
            def _(t):
                t.wait_ge(vs.h, diag_rows)
                for nt in range(16):
                    inst = t.matmul(ps7[:, nt:nt + 1],
                                    row_scr[0:1, nt * 128:(nt + 1) * 128],
                                    onesrow_f32[0:1, 0:1], start=True, stop=True)
                ts.inc(inst)

            diag_cols = ts.n

            @block.vector
            def _(v):
                v.wait_ge(ts.h, diag_cols)
                for nt in range(16):
                    inst = v.scalar_tensor_tensor(diag_sc[:, nt:nt + 1], ps7[:, nt:nt + 1],
                                                  -1.0, diag_sc[:, 16:17],
                                                  ALU.mult, ALU.add)
                vs.inc(inst)

            biask_done = vs.n
            if STOP == "h1diagk":
                raise _StopEmit

            for nt in range(16):
                @block.scalar
                def _(s):
                    if nt == 0:
                        s.wait_ge(vs.h, biask_done)
                    inst = s.activation(kp_tm[:, nt, :], xpk_store[:, nt, :], AF.Exp,
                                        bias=diag_sc[:, nt:nt + 1], scale=NORM)
                    ss.inc(inst)

            kp_done = ss.n
            if STOP == "h1kp":
                raise _StopEmit

            @block.tensor
            def _(t):
                t.wait_ge(ss.h, kp_done)
                for mc in range(2):
                    for nt in range(16):
                        t.matmul(ps7[:, 4 + mc:5 + mc],
                                 kp_tm[:, nt, mc * 128:(mc + 1) * 128],
                                 ones_bf[:, 0:1], start=(nt == 0), stop=(nt == 15))
                for mc in range(2):
                    for nt in range(16):
                        t.matmul(ps1[:, mc * 64:(mc + 1) * 64],
                                 kp_tm[:, nt, mc * 128:(mc + 1) * 128],
                                 v_tm[:, nt, h * 64:h * 64 + 64],
                                 start=(nt == 0), stop=(nt == 15))
                for nt in range(16):
                    inst = t.matmul(ps3[0:1, 0:64], ones_bf[:, 0:1],
                                    v_tm[:, nt, h * 64:h * 64 + 64],
                                    start=(nt == 0), stop=(nt == 15))
                ts.inc(inst)

            kc_mm = ts.n

            @block.vector
            def _(v):
                v.wait_ge(ts.h, kc_mm)
                inst = v.tensor_scalar_add(ksum_bf[:, 0:2], ps7[:, 4:6], T * EPS_K)
                vs.inc(inst)
                inst = v.tensor_scalar_mul(row_scr[0:1, 0:64], ps3[0:1, 0:64], EPS_K)
                vs.inc(inst)

            veps1 = vs.n

            @block.tensor
            def _(t):
                t.wait_ge(vs.h, veps1)
                inst = t.matmul(ps6[:, 0:64], onesrow_f32[0:1, :], row_scr[0:1, 0:64],
                                start=True, stop=True)
                ts.inc(inst)

            veps2 = ts.n

            @block.vector
            def _(v):
                v.wait_ge(ts.h, veps2)
                v.tensor_copy(veps_sb[:, :], ps6[:, 0:64])
                v.tensor_add(ctx_sb[:, 0, :], ps1[:, 0:64], veps_sb[:, :])
                inst = v.tensor_add(ctx_sb[:, 1, :], ps1[:, 64:128], veps_sb[:, :])
                vs.inc(inst)

            ctx_done = vs.n
            if STOP == "h1ctx":
                raise _StopEmit

            @block.scalar
            def _(s):
                s.wait_ge(vs.h, ctx_done)
                inst = s.activation(q2_scr[prow:prow + 64, :], qf[prow:prow + 64, kc_h, :],
                                    AF.Square)
                ss.inc(inst)

            q2_done = ss.n
            for ti in range(NTT):
                ev_prev = vs.n

                @block.tensor
                def _(t):
                    if ti == 0:
                        t.wait_ge(ss.h, q2_done)
                    t.wait_ge(vs.h, ev_prev)
                    inst = t.matmul(ps3[0:1, :], ones_bf[prow:prow + 64, 0:1],
                                    q2_scr[prow:prow + 64, ti * TT:(ti + 1) * TT],
                                    start=True, stop=True)
                    ts.inc(inst)

                tdone = ts.n

                @block.vector
                def _(v):
                    v.wait_ge(ts.h, tdone)
                    inst = v.tensor_scalar_mul(row_scr[0:1, ti * TT:(ti + 1) * TT],
                                               ps3[0:1, :], 0.5 * NORM * NORM)
                    vs.inc(inst)

            dq_rows = vs.n

            @block.tensor
            def _(t):
                t.wait_ge(vs.h, dq_rows)
                for nt in range(16):
                    inst = t.matmul(ps7[:, nt:nt + 1],
                                    row_scr[0:1, nt * 128:(nt + 1) * 128],
                                    onesrow_f32[0:1, 0:1], start=True, stop=True)
                ts.inc(inst)

            dq_cols = ts.n

            @block.vector
            def _(v):
                v.wait_ge(ts.h, dq_cols)
                for nt in range(16):
                    inst = v.tensor_copy(diag_sc[:, nt:nt + 1], ps7[:, nt:nt + 1])
                vs.inc(inst)

            dq_done = vs.n
            if STOP == "h1dq":
                raise _StopEmit

            for nt in range(16):
                n0 = nt * 128

                @block.tensor
                def _(t):
                    if nt == 0:
                        t.wait_ge(vs.h, dq_done)
                    t.wait_ge(ss.h, ss.n)
                    inst = t.matmul(ps0[:, 0:256], qf[prow:prow + 64, kc_h, n0:n0 + 128],
                                    projt2[prow:prow + 64, :], start=True, stop=True)
                    ts.inc(inst)

                mmq = ts.n

                @block.vector
                def _(v):
                    v.wait_ge(ts.h, mmq)
                    inst = v.reduce_max(kmax_sc[:, 17:18], ps0[:, 0:256], axis=AX.X)
                    vs.inc(inst)
                    v.wait_ge(vs.h, vs.n)
                    inst = v.scalar_tensor_tensor(kmax_sc[:, 17:18], kmax_sc[:, 17:18],
                                                  -NORM, diag_sc[:, nt:nt + 1],
                                                  ALU.mult, ALU.subtract)
                    vs.inc(inst)

                bq_done = vs.n

                @block.scalar
                def _(s):
                    s.wait_ge(vs.h, bq_done)
                    inst = s.activation(q2_scr[:, 128:384], ps0[:, 0:256], AF.Exp,
                                        bias=kmax_sc[:, 17:18], scale=NORM)
                    ss.inc(inst)

                qp_exp = ss.n
                qpt_evac_prev = vs.n

                @block.tensor
                def _(t):
                    t.wait_ge(ss.h, qp_exp)
                    t.wait_ge(vs.h, qpt_evac_prev)
                    t.transpose(ps4[:, 256:384], q2_scr[:, 128:256], ident_bf[:, :])
                    inst = t.transpose(ps4[:, 384:512], q2_scr[:, 256:384], ident_bf[:, :])
                    ts.inc(inst)

                qpt_mm = ts.n

                @block.vector
                def _(v):
                    v.wait_ge(ts.h, qpt_mm)
                    v.tensor_scalar_add(qpT_buf[:, 0, n0:n0 + 128], ps4[:, 256:384], EPS_K)
                    inst = v.tensor_scalar_add(qpT_buf[:, 1, n0:n0 + 128], ps4[:, 384:512],
                                               EPS_K)
                    vs.inc(inst)

            qpt_done = vs.n
            if STOP == "h1qpt":
                raise _StopEmit

            for ti in range(NTT):
                t0 = ti * TT
                av_prev = vs.n

                @block.tensor
                def _(t):
                    t.wait_ge(vs.h, max(qpt_done, av_prev))
                    for mc in range(2):
                        t.matmul(ps5[0:64, :], ctx_sb[:, mc, :], qpT_buf[:, mc, t0:t0 + TT],
                                 start=(mc == 0), stop=(mc == 1))
                    for mc in range(2):
                        inst = t.matmul(ps3[0:1, :], ksum_bf[:, mc:mc + 1],
                                        qpT_buf[:, mc, t0:t0 + TT],
                                        start=(mc == 0), stop=(mc == 1))
                    ts.inc(inst)

                av_mm = ts.n
                r_row = row_scr[0:1, 0:TT]

                @block.vector
                def _(v):
                    v.wait_ge(ts.h, av_mm)
                    inst = v.reciprocal(r_row, ps3[0:1, :])
                    vs.inc(inst)
                    v.wait_ge(vs.h, vs.n)

                rcp_done = vs.n

                @block.tensor
                def _(t):
                    t.wait_ge(vs.h, rcp_done)
                    inst = t.matmul(ps6[:, :], onesrow_f32[0:1, :], r_row,
                                    start=True, stop=True)
                    ts.inc(inst)

                rb_done = ts.n

                @block.vector
                def _(v):
                    v.wait_ge(ts.h, rb_done)
                    avs = scr_f32[0:64, 0:TT]
                    v.tensor_copy(avs, ps5[0:64, :])
                    inst = v.tensor_tensor(qf[prow:prow + 64, kc_h, t0:t0 + TT],
                                           avs, ps6[0:64, :], ALU.mult)
                    vs.inc(inst)


    except _StopEmit:
        pass
    attn_done = vs.n
    if STOP.startswith("h1") or STOP == "attn":
        _l0_bail(p, y_hbm, ff_acc)
        p.close()
        return p
    av = qf

    u_buf = kf

    def u_epilogue(oc, tt, mm_done):
        t0 = tt * TT

        @block.vector
        def _(v):
            v.wait_ge(ts.h, mm_done)
            inst = v.scalar_tensor_tensor(u_buf[:, oc, t0:t0 + TT], ps0[:, :],
                                          b_sb["bo"][:, oc:oc + 1],
                                          xn[:, oc, t0:t0 + TT], ALU.add, ALU.add)
            vs.inc(inst)

    emit_proj_fm(p, av, wb, ps0, u_epilogue, pre_wait_v=attn_done,
                 pre_wait_ld=wo_loaded, ld_sem=lg)
    u_done = vs.n
    if STOP == "u":
        _l0_bail(p, y_hbm, ff_acc)
        p.close()
        return p

    x2_buf = qf
    sqs = v_tm.ap().rearrange("p n d -> p (n d)")[:, 0:KC * TT].rearrange(
        "p (c t) -> p c t", c=KC)

    def ln2_out(v, kc, tt, ps_a, ps_b):
        t0 = tt * TT
        v.tensor_tensor(scr_f32[:, 0:TT], u_buf[:, kc, t0:t0 + TT], ps_a[:, :], ALU.mult)
        v.tensor_tensor(scr_f32[:, 0:TT], scr_f32[:, 0:TT], ps_b[:, :], ALU.subtract)
        inst = v.tensor_scalar(x2_buf[:, kc, t0:t0 + TT], scr_f32[:, 0:TT],
                               b_sb["g2"][:, kc:kc + 1], b_sb["be2"][:, kc:kc + 1],
                               ALU.mult, ALU.add)
        vs.inc(inst)

    emit_ln_sbuf(p, lambda kc, tt: u_buf[:, kc, tt * TT:(tt + 1) * TT], ln2_out,
                 sqs, ps0, ps1, ps_c, pre_wait_v=u_done)
    x2_done = vs.n
    if STOP == "x2":
        _l0_bail(p, y_hbm, ff_acc)
        p.close()
        return p

    h_tt = v_tm.ap().rearrange("p n d -> p (n d)").rearrange(
        "p (c t) -> p c t", c=32)
    wbufs = [wa, wb]
    y2_buf = u_buf
    yf = ff_acc

    for tt in range(NTT):
        t0 = tt * TT
        prev_tile_done = vs.n
        for ds in range(4):
            wbuf = wbufs[ds % 2]
            w_prev_mm = ts.n

            @block.gpsimd
            def _(g):
                g.wait_ge(ts.h, w_prev_mm)
                if tt == 0 and ds == 0:
                    g.wait_ge(vs.h, x2_done)
                lg.dma(g.dma_start(out=wbuf[:, :, :],
                                   in_=w1T_hbm[:, :, ds * 1024:(ds + 1) * 1024]))

            w_ld = lg.n
            for ocr in range(8):
                dffc = ds * 8 + ocr
                relu_prev = ss.n

                @block.tensor
                def _(t):
                    t.wait_ge(lg.h, w_ld)
                    t.wait_ge(ss.h, relu_prev)
                    if ds == 0 and ocr == 0:
                        t.wait_ge(vs.h, max(x2_done, prev_tile_done))
                    for kc in range(KC):
                        inst = t.matmul(ps0[:, :], wbuf[:, kc, ocr * 128:(ocr + 1) * 128],
                                        x2_buf[:, kc, t0:t0 + TT],
                                        start=(kc == 0), stop=(kc == KC - 1))
                    ts.inc(inst)

                h_mm = ts.n

                @block.scalar
                def _(s):
                    s.wait_ge(ts.h, h_mm)
                    inst = s.activation(h_tt[:, dffc, :], ps0[:, :], AF.Relu,
                                        bias=b1_sb[:, dffc:dffc + 1], scale=1.0)
                    ss.inc(inst)

        h_done = ss.n

        for ds in range(4):
            wbuf = wbufs[ds % 2]
            w_prev_mm = ts.n

            @block.gpsimd
            def _(g):
                g.wait_ge(ts.h, w_prev_mm)
                lg.dma(g.dma_start(out=wbuf[:, :, :],
                                   in_=w2T_hbm[:, ds * 8:(ds + 1) * 8, :]))

            w_ld = lg.n
            for oc in range(KC):
                acc_prev = vs.n

                @block.tensor
                def _(t):
                    t.wait_ge(lg.h, w_ld)
                    t.wait_ge(vs.h, acc_prev)
                    if ds == 0 and oc == 0:
                        t.wait_ge(ss.h, h_done)
                    for j in range(8):
                        inst = t.matmul(ps1[:, :], wbuf[:, j, oc * 128:(oc + 1) * 128],
                                        h_tt[:, ds * 8 + j, :],
                                        start=(j == 0), stop=(j == 7))
                    ts.inc(inst)

                ff_mm = ts.n

                @block.vector
                def _(v):
                    v.wait_ge(ts.h, ff_mm)
                    if ds == 0:
                        inst = v.tensor_copy(ff_acc[:, oc, :], ps1[:, :])
                    else:
                        inst = v.tensor_add(ff_acc[:, oc, :], ff_acc[:, oc, :], ps1[:, :])
                    vs.inc(inst)

        @block.vector
        def _(v):
            for oc in range(KC):
                inst = v.scalar_tensor_tensor(y2_buf[:, oc, t0:t0 + TT],
                                              ff_acc[:, oc, :],
                                              b_sb["b2v"][:, oc:oc + 1],
                                              x2_buf[:, oc, t0:t0 + TT],
                                              ALU.add, ALU.add)
            vs.inc(inst)

        y2_done = vs.n

        @block.sync
        def _(s):
            s.wait_ge(vs.h, y2_done)
            ld.dma(s.dma_start(out=x_t[:, :, :], in_=x_hbm[:, :, t0:t0 + TT]))

        x_ld = ld.n

        def ln3_out(v, kc, tt2, ps_a, ps_b):
            v.wait_ge(ld.h, x_ld)
            v.tensor_tensor(scr_f32[:, 0:TT], y2_buf[:, kc, t0:t0 + TT],
                            ps_a[:, :], ALU.mult)
            v.tensor_tensor(scr_f32[:, 0:TT], scr_f32[:, 0:TT], ps_b[:, :], ALU.subtract)
            v.tensor_scalar(scr_f32[:, 0:TT], scr_f32[:, 0:TT],
                            b_sb["g3"][:, kc:kc + 1], b_sb["be3"][:, kc:kc + 1],
                            ALU.mult, ALU.add)
            inst = v.tensor_add(yf[:, kc, :], scr_f32[:, 0:TT], x_t[:, kc, :])
            vs.inc(inst)

        emit_ln_sbuf(p, lambda kc, tt2: y2_buf[:, kc, t0:t0 + TT], ln3_out,
                     sqs, ps0, ps1, ps_c, tts=[tt], pre_wait_v=y2_done)
        y_done = vs.n

        @block.sync
        def _(s):
            s.wait_ge(vs.h, y_done)
            os_.dma(s.dma_start(out=y_hbm[:, :, t0:t0 + TT], in_=yf[:, :, :]))

        y_stored = os_.n

        @block.vector
        def _(v):
            v.wait_ge(os_.h, y_stored)

    @block.sync
    def _(s):
        s.wait_ge(os_.h, os_.n)

    p.close()
    return p


def host_inputs_l0_attn(x_row, w):
    proj = np.asarray(w["proj"])
    projT2 = np.zeros((128, 256), np.float32)
    projT2[0:64, :] = proj.T
    projT2[64:128, :] = proj.T
    ident = np.eye(128, dtype=np.float32)
    return {
        "x": fm_x(x_row),
        "wqT": fm_w(np.ascontiguousarray(w["Wq"].T)),
        "wkT": fm_w(np.ascontiguousarray(w["Wk"].T)),
        "wvT": fm_w(np.ascontiguousarray(w["Wv"].T)),
        "woT": fm_w(np.ascontiguousarray(w["Wo"].T)),
        "bq": fm_vec(w["bq"]), "bk": fm_vec(w["bk"]),
        "bv": fm_vec(w["bv"]), "bo": fm_vec(w["bo"]),
        "g2": fm_vec(w["g2"]), "be2": fm_vec(w["be2"]),
        "g3": fm_vec(w["g3"]), "be3": fm_vec(w["be3"]),
        "b2v": fm_vec(w["b2"]),
        "b1v": fm_vec(w["b1"]),
        "bv_row": np.ascontiguousarray(w["bv"][None, :]),
        "projT2": projT2,
        "ident": ident,
        "w1T": fm_w(np.ascontiguousarray(w["W1"].T)),
        "w2T": fm_w(np.ascontiguousarray(w["W2"].T)),
    }




def build_l1_ffn(shared=None):
    E = 16
    IN = 512
    NH = IN // 128
    p = Prog(shared)
    x_hbm = p.dram_in("x", [128, KC, T])
    gate_h = p.dram_in("gateT", [128, KC, E])
    wsin_h = p.dram_in("wsinT", [128, KC, D])
    vsin_h = p.dram_in("vsinT", [128, KC, D])
    wsout_h = p.dram_in("wsoutT", [128, KC, D])
    ewin_h = p.dram_in("ewinT", [128, E, KC, IN])
    evin_h = p.dram_in("evinT", [128, E, KC, IN])
    ewout_h = p.dram_in("ewoutT", [128, E, NH, D])
    iota16_h = p.dram_in("iota16", [128, E])
    ident_h = p.dram_in("ident", [128, 128])
    y_hbm = p.dram_out("y", [128, KC, T])
    h2w_dram = p.nc.dram_tensor(p.prefix + "h2w_scr", [128, E * NH, T], BF16, kind="Internal").ap()

    xn = p.sbuf("xn", [128, KC, T], BF16)
    sh = p.sbuf("sh", [128, KC, T], BF16)
    shh = p.sbuf("shh", [128, KC, T], BF16)
    wa = p.sbuf("wa", [128, KC, D], BF16)
    wb = p.sbuf("wb", [128, KC, D], BF16)
    x_t = p.sbuf("x_t", [128, KC, TT], F32)
    gate_sb = p.sbuf("gate_sb", [128, KC, E], BF16)
    W16 = p.sbuf("W16", [128, T // 128, E], BF16)
    iota16 = p.sbuf("iota16_sb", [128, E], F32)
    ident_bf = p.sbuf("ident_bf", [128, 128], BF16)
    ones_bf = p.sbuf("ones_bf", [128, 1], BF16)
    onesrow_bf = p.sbuf("onesrow_bf", [1, 128], BF16)
    onesrow_f32 = p.sbuf("onesrow_f32", [1, 128], F32)
    scr_f32 = p.sbuf("scr_f32", [128, 2 * TT], F32)
    row_scr = p.sbuf("row_scr", [1, 4 * TT], F32)
    g_sc = p.sbuf("g_sc", [128, E], F32)
    m8_sc = p.sbuf("m8_sc", [128, 8], F32)
    i8_sc = p.sbuf("i8_sc", [128, 8], U32)
    idxf_sc = p.sbuf("idxf_sc", [128, 8], F32)
    col_sc = p.sbuf("col_sc", [128, 8], F32)
    sil_bf = p.sbuf("sil_bf", [128, TT], BF16)
    h2a_bf = p.sbuf("h2a_bf", [128, TT], BF16)
    sil_bf2 = p.sbuf("sil_bf2", [128, TT], BF16)
    h2_bf = p.sbuf("h2_bf", [128, NH, TT], BF16)
    wrow_bf = p.sbuf("wrow_bf", [1, TT], BF16)
    racc = p.sbuf("racc", [128, KC, TT], F32)
    p.bufs["xbf_scr"] = shh[:, :, 0:TT]
    p.bufs["sq_scr"] = sh[:, :, 0:TT]
    h2in = shh.ap().rearrange("p c t -> p (c t)")[:, 0:16 * TT].rearrange(
        "p (c t) -> p c t", c=16)
    xres = x_t[:, 0:2, :]

    ps0 = p.psum("ps0", [128, TT], F32)
    ps1 = p.psum("ps1", [128, TT], F32)
    ps_c = p.psum("ps_c", [33, TT], F32)
    ps3 = p.psum("ps3", [1, TT], F32)
    ps5 = p.psum("ps5", [128, TT], F32)
    ps6 = p.psum("ps6", [128, TT], F32)
    psA1 = p.psum("psA1", [128, TT], F32)
    p.bufs["ps_a"] = ps0
    p.bufs["ps_b"] = ps1

    p.start_block()
    block = p.block
    ld, ts, vs, ss, os_ = p.sem("ld"), p.sem("ts"), p.sem("vs"), p.sem("ss"), p.sem("os")
    lg = p.sem("lg")
    hs = p.sem("hs")

    @block.gpsimd
    def _(g):
        lg.dma(g.dma_start(out=gate_sb[:, :, :], in_=gate_h))
        lg.dma(g.dma_start(out=wa[:, :, :], in_=wsin_h))
        lg.dma(g.dma_start(out=wb[:, :, :], in_=vsin_h))
        lg.dma(g.dma_start(out=ident_bf[:, :], in_=ident_h))

    @block.sync
    def _(s):
        ld.dma(s.dma_start(out=iota16[:, :], in_=iota16_h))

    all_loaded = ld.n
    all_loaded_g = lg.n

    @block.vector
    def _(v):
        v.wait_ge(ld.h, all_loaded)
        v.wait_ge(lg.h, all_loaded_g)
        v.memset(ones_bf[:, :], 1.0)
        v.memset(onesrow_bf[:, :], 1.0)
        inst = v.memset(onesrow_f32[:, :], 1.0)
        vs.inc(inst)

    @block.scalar
    def _(s):
        s.wait_ge(ld.h, all_loaded)
        s.wait_ge(lg.h, all_loaded_g)

    emit_ln_stream(p, x_hbm, xn)
    ln_done = vs.n

    for t128 in range(T // 128):
        t0 = t128 * 128
        evac_prev = vs.n

        @block.tensor
        def _(t):
            t.wait_ge(vs.h, max(evac_prev, ln_done))
            for kc in range(KC):
                inst = t.matmul(ps5[:, 0:E], xn[:, kc, t0:t0 + 128],
                                gate_sb[:, kc, :], start=(kc == 0), stop=(kc == KC - 1))
            ts.inc(inst)

        g_mm = ts.n

        @block.scalar
        def _(s):
            s.wait_ge(ts.h, g_mm)
            inst = s.activation(g_sc[:, :], ps5[:, 0:E], AF.Sigmoid)
            ss.inc(inst)

        sig_done = ss.n

        @block.vector
        def _(v):
            v.wait_ge(ss.h, sig_done)
            def F(inst):
                vs.inc(inst)
                v.wait_ge(vs.h, vs.n)
            F(v.max(out=m8_sc[:, :], in_=g_sc[:, :]))
            F(v.max_index(out=i8_sc[:, :], in_max=m8_sc[:, :], in_values=g_sc[:, :]))
            F(v.tensor_copy(idxf_sc[:, :], i8_sc[:, :]))
            F(v.reduce_sum(col_sc[:, 0:1], m8_sc[:, 0:4], axis=AX.X))
            F(v.reciprocal(col_sc[:, 1:2], col_sc[:, 0:1]))
            F(v.tensor_scalar(col_sc[:, 4:8], m8_sc[:, 0:4], col_sc[:, 1:2], None,
                              ALU.mult))
            F(v.memset(W16[:, t128, :], 0.0))
            for j in range(4):
                F(v.tensor_scalar(scr_f32[:, 0:E], iota16[:, :], idxf_sc[:, j:j + 1],
                                  None, ALU.is_equal))
                F(v.scalar_tensor_tensor(W16[:, t128, :], scr_f32[:, 0:E],
                                         col_sc[:, 4 + j:5 + j], W16[:, t128, :],
                                         ALU.mult, ALU.add))

    w16_done = vs.n

    def silu_epilogue(oc, tt, mm_done):
        t0 = tt * TT

        @block.scalar
        def _(s):
            s.wait_ge(ts.h, mm_done)
            inst = s.activation(sh[:, oc, t0:t0 + TT], ps0[:, :], AF.Silu)
            ss.inc(inst)

        sdone = ss.n

        @block.vector
        def _(v):
            v.wait_ge(ss.h, sdone)
            inst = v.nop()
            vs.inc(inst)

    emit_proj_fm(p, xn, wa, ps0, silu_epilogue, pre_wait_v=ln_done)

    def mult_epilogue(oc, tt, mm_done):
        t0 = tt * TT

        @block.vector
        def _(v):
            v.wait_ge(ts.h, mm_done)
            inst = v.tensor_tensor(shh[:, oc, t0:t0 + TT], sh[:, oc, t0:t0 + TT],
                                   ps0[:, :], ALU.mult)
            vs.inc(inst)

    emit_proj_fm(p, xn, wb, ps0, mult_epilogue, pre_wait_v=ln_done)
    shh_done = vs.n
    vsin_mm_done = ts.n

    @block.gpsimd
    def _(g):
        g.wait_ge(ts.h, vsin_mm_done)
        lg.dma(g.dma_start(out=wa[:, :, :], in_=wsout_h))

    wsout_ld = lg.n

    def shout_epilogue(oc, tt, mm_done):
        t0 = tt * TT

        @block.vector
        def _(v):
            v.wait_ge(ts.h, mm_done)
            inst = v.tensor_copy(sh[:, oc, t0:t0 + TT], ps0[:, :])
            vs.inc(inst)

    emit_proj_fm(p, shh, wa, ps0, shout_epilogue, pre_wait_v=shh_done,
                 pre_wait_ld=wsout_ld, ld_sem=lg)
    sh_done = vs.n

    wbufs = [wa, wb]
    for e in range(E):
        wbuf = wbufs[e % 2]
        w_prev_mm = ts.n

        @block.gpsimd
        def _(g):
            g.wait_ge(ts.h, w_prev_mm)
            lg.dma(g.dma_start(out=wbuf[:, :, 0:IN], in_=ewin_h[:, e, :, :]))
            lg.dma(g.dma_start(out=wbuf[:, :, IN:2 * IN], in_=evin_h[:, e, :, :]))

        w_ld = lg.n
        for tt in range(NTT):
            t0 = tt * TT
            wrow_prev = vs.n

            @block.tensor
            def _(t):
                t.wait_ge(vs.h, max(w16_done, wrow_prev))
                for j in range(4):
                    inst = t.matmul(ps3[0:1, j * 128:(j + 1) * 128],
                                    W16[:, tt * 4 + j, e:e + 1], ident_bf[:, :],
                                    start=True, stop=True)
                ts.inc(inst)

            wrow_mm = ts.n

            @block.vector
            def _(v):
                v.wait_ge(ts.h, wrow_mm)
                inst = v.tensor_copy(wrow_bf[0:1, :], ps3[0:1, :])
                vs.inc(inst)

            wrow_ev = vs.n
            bcast_prev = vs.n

            @block.tensor
            def _(t):
                t.wait_ge(vs.h, wrow_ev)
                inst = t.matmul(ps6[:, :], onesrow_bf[0:1, :], wrow_bf[0:1, :],
                                start=True, stop=True)
                ts.inc(inst)

            wb_mm = ts.n
            h2ev = [vs.n, vs.n]
            for hd in range(NH):
                h2_store_prev = hs.n
                psA = (ps0, ps5)[hd % 2]
                psB = (ps1, psA1)[hd % 2]
                silc = (sil_bf, sil_bf2)[hd % 2]
                h2ev_par = h2ev[hd % 2]

                @block.tensor
                def _(t):
                    t.wait_ge(lg.h, w_ld)
                    t.wait_ge(vs.h, h2ev_par)
                    for kc in range(KC):
                        t.matmul(psA[:, :], wbuf[:, kc, hd * 128:(hd + 1) * 128],
                                 xn[:, kc, t0:t0 + TT],
                                 start=(kc == 0), stop=(kc == KC - 1))
                    for kc in range(KC):
                        inst = t.matmul(psB[:, :],
                                        wbuf[:, kc, IN + hd * 128:IN + (hd + 1) * 128],
                                        xn[:, kc, t0:t0 + TT],
                                        start=(kc == 0), stop=(kc == KC - 1))
                    ts.inc(inst)

                hv_mm = ts.n

                @block.scalar
                def _(s):
                    s.wait_ge(ts.h, hv_mm)
                    inst = s.activation(silc[:, :], psA[:, :], AF.Silu)
                    ss.inc(inst)

                sil_done = ss.n

                @block.vector
                def _(v):
                    v.wait_ge(ss.h, sil_done)
                    v.wait_ge(ts.h, wb_mm)
                    v.wait_ge(hs.h, h2_store_prev)
                    v.tensor_tensor(h2a_bf[:, :], silc[:, :], psB[:, :], ALU.mult)
                    inst = v.tensor_tensor(h2_bf[:, hd, :], h2a_bf[:, :], ps6[:, :],
                                           ALU.mult)
                    vs.inc(inst)
                h2ev[hd % 2] = vs.n

            h2_done = vs.n

            @block.sync
            def _(s):
                s.wait_ge(vs.h, h2_done)
                hs.dma(s.dma_start(out=h2w_dram[:, e * NH:(e + 1) * NH, t0:t0 + TT],
                                   in_=h2_bf[:, :, :]))

    ph_a_done = hs.n

    for tt in range(NTT):
        t0 = tt * TT
        for eg in range(8):
            h2_prev_mm = ts.n

            @block.sync
            def _(s):
                s.wait_ge(hs.h, ph_a_done)
                s.wait_ge(ts.h, h2_prev_mm)
                ld.dma(s.dma_start(out=h2in[:, 0:8, :],
                                   in_=h2w_dram[:, eg * 8:(eg + 1) * 8, t0:t0 + TT]))

            h2_ld = ld.n
            wq_prev_mm = ts.n
            wbuf = wbufs[eg % 2]

            @block.gpsimd
            def _(g):
                g.wait_ge(ts.h, wq_prev_mm)
                lg.dma(g.dma_start(out=wbuf[:, :, :],
                                   in_=ewout_h[:, eg * 2:(eg + 1) * 2, :, :]))

            wq_ld = lg.n
            rev = [vs.n, vs.n]
            for oc in range(KC):
                racc_prev = rev[oc % 2]
                psP = (ps5, psA1)[oc % 2]

                @block.tensor
                def _(t):
                    t.wait_ge(ld.h, h2_ld)
                    t.wait_ge(lg.h, wq_ld)
                    t.wait_ge(vs.h, racc_prev)
                    wv = wbuf.ap().rearrange("p c d -> p (c d)").rearrange(
                        "p (a b) -> p a b", b=D)
                    for j in range(8):
                        inst = t.matmul(psP[:, :], wv[:, j, oc * 128:(oc + 1) * 128],
                                        h2in[:, j, :], start=(j == 0), stop=(j == 7))
                    ts.inc(inst)

                r_mm = ts.n

                @block.vector
                def _(v):
                    v.wait_ge(ts.h, r_mm)
                    if eg == 0:
                        inst = v.tensor_copy(racc[:, oc, :], psP[:, :])
                    else:
                        inst = v.tensor_add(racc[:, oc, :], racc[:, oc, :], psP[:, :])
                    vs.inc(inst)
                rev[oc % 2] = vs.n

        @block.sync
        def _(s):
            s.wait_ge(vs.h, vs.n)
            ld.dma(s.dma_start(out=x_t[:, :, :], in_=x_hbm[:, :, t0:t0 + TT]))

        x_ld = ld.n
        for oc in range(KC):
            y_prev = os_.n

            @block.vector
            def _(v):
                v.wait_ge(ld.h, x_ld)
                if oc == 0:
                    v.wait_ge(os_.h, max(0, os_.n - 16 * 6))
                v.tensor_add(scr_f32[:, 0:TT], racc[:, oc, :], sh[:, oc, t0:t0 + TT])
                inst = v.tensor_add(racc[:, oc, :], scr_f32[:, 0:TT], x_t[:, oc, :])
                vs.inc(inst)

            y_done = vs.n

            @block.sync
            def _(s):
                s.wait_ge(vs.h, y_done)
                os_.dma(s.dma_start(out=y_hbm[:, oc, t0:t0 + TT], in_=racc[:, oc, :]))

    @block.sync
    def _(s):
        s.wait_ge(os_.h, os_.n)

    p.close()
    return p


def host_inputs_l1_ffn(x_row, w):
    E, IN, NH = 16, 512, 4
    ewin = np.asarray(w["e_win"])
    evin = np.asarray(w["e_vin"])
    ewout = np.asarray(w["e_wout"])
    return {
        "x": fm_x(x_row),
        "gateT": fm_w(np.ascontiguousarray(w["gate_w"].T)),
        "wsinT": fm_w(w["ws_in"]),
        "vsinT": fm_w(w["vs_in"]),
        "wsoutT": fm_w(w["ws_out"]),
        "ewinT": np.ascontiguousarray(
            ewin.reshape(E, KC, 128, IN).transpose(2, 0, 1, 3)),
        "evinT": np.ascontiguousarray(
            evin.reshape(E, KC, 128, IN).transpose(2, 0, 1, 3)),
        "ewoutT": np.ascontiguousarray(
            ewout.reshape(E, NH, 128, D).transpose(2, 0, 1, 3)),
        "iota16": np.broadcast_to(np.arange(E, dtype=np.float32)[None, :],
                                  (128, E)).copy(),
        "ident": np.eye(128, dtype=np.float32),
    }




def build_l3_ffn(shared=None):
    E = 32
    IN = 512
    NH = IN // 128
    p = Prog(shared)
    x_hbm = p.dram_in("x", [128, KC, T])
    gw_h = p.dram_in("gwT", [128, KC, E])
    gb_h = p.dram_in("gb_row", [1, E])
    sw1_h = p.dram_in("sw1T", [128, KC, D])
    sw3_h = p.dram_in("sw3T", [128, KC, D])
    sw2_h = p.dram_in("sw2T", [128, KC, D])
    sb1_h = p.dram_in("sb1v", [128, KC])
    sb3_h = p.dram_in("sb3v", [128, KC])
    sb2_h = p.dram_in("sb2v", [128, KC])
    w1_h = p.dram_in("w1T", [128, E, KC, IN])
    w3_h = p.dram_in("w3T", [128, E, KC, IN])
    w2_h = p.dram_in("w2T", [128, E, NH, D])
    b1_h = p.dram_in("b1m", [128, E, NH])
    b3_h = p.dram_in("b3m", [128, E, NH])
    b2_h = p.dram_in("b2m", [32, D])
    iota32_h = p.dram_in("iota32", [128, E])
    iota8_h = p.dram_in("iota8", [128, 8])
    ident_h = p.dram_in("ident", [128, 128])
    y_hbm = p.dram_out("y", [128, KC, T])
    h2w_dram = p.nc.dram_tensor(p.prefix + "h2w_scr", [128, E * NH, T], BF16, kind="Internal").ap()

    xn = p.sbuf("xn", [128, KC, T], BF16)
    sh = p.sbuf("sh", [128, KC, T], BF16)
    shh = p.sbuf("shh", [128, KC, T], BF16)
    wa = p.sbuf("wa", [128, KC, D], BF16)
    wb = p.sbuf("wb", [128, KC, D], BF16)
    x_t = p.sbuf("x_t", [128, KC, TT], F32)
    gw_sb = p.sbuf("gw_sb", [128, KC, E], BF16)
    W32 = p.sbuf("W32", [128, T // 128, E], BF16)
    W32T = p.sbuf("W32T", [32, T], BF16)
    b2_sb = p.sbuf("b2_sb", [32, D], BF16)
    b1_sb = p.sbuf("b1_sb", [128, E, NH], F32)
    b3_sb = p.sbuf("b3_sb", [128, E, NH], F32)
    sb1_sb = p.sbuf("sb1_sb", [128, KC], F32)
    sb3_sb = p.sbuf("sb3_sb", [128, KC], F32)
    sb2_sb = p.sbuf("sb2_sb", [128, KC], F32)
    gb_bc = p.sbuf("gb_bc", [128, E], F32)
    iota32 = p.sbuf("iota32_sb", [128, E], F32)
    iota8 = p.sbuf("iota8_sb", [128, 8], F32)
    ident_bf = p.sbuf("ident_bf", [128, 128], BF16)
    ones_bf = p.sbuf("ones_bf", [128, 1], BF16)
    onesrow_bf = p.sbuf("onesrow_bf", [1, 128], BF16)
    onesrow_f32 = p.sbuf("onesrow_f32", [1, 128], F32)
    scr_f32 = p.sbuf("scr_f32", [128, 2 * TT], F32)
    row_scr = p.sbuf("row_scr", [1, 4 * TT], F32)
    e32_sc = p.sbuf("e32_sc", [128, E], F32)
    sg_sc = p.sbuf("sg_sc", [128, 8], F32)
    m8_sc = p.sbuf("m8_sc", [128, 8], F32)
    i8_sc = p.sbuf("i8_sc", [128, 8], U32)
    idxf_sc = p.sbuf("idxf_sc", [128, 8], F32)
    col_sc = p.sbuf("col_sc", [128, 8], F32)
    km_sc = p.sbuf("km_sc", [128, E], F32)
    sil_bf = p.sbuf("sil_bf", [128, TT], BF16)
    h2a_bf = p.sbuf("h2a_bf", [128, TT], BF16)
    sil_bf2 = p.sbuf("sil_bf2", [128, TT], BF16)
    h2_bf = p.sbuf("h2_bf", [128, NH, TT], BF16)
    wrow_bf = p.sbuf("wrow_bf", [1, TT], BF16)
    racc = p.sbuf("racc", [128, KC, TT], F32)
    p.bufs["xbf_scr"] = shh[:, :, 0:TT]
    p.bufs["sq_scr"] = sh[:, :, 0:TT]
    h2in = shh.ap().rearrange("p c t -> p (c t)")[:, 0:8 * TT].rearrange(
        "p (c t) -> p c t", c=8)
    xres = x_t[:, 0:2, :]

    ps0 = p.psum("ps0", [128, TT], F32)
    ps1 = p.psum("ps1", [128, TT], F32)
    ps_c = p.psum("ps_c", [33, TT], F32)
    ps3 = p.psum("ps3", [1, TT], F32)
    ps4 = p.psum("ps4", [128, 1024], BF16)
    ps5 = p.psum("ps5", [128, TT], F32)
    ps6 = p.psum("ps6", [128, TT], F32)
    psA1 = p.psum("psA1", [128, TT], F32)
    p.bufs["ps_a"] = ps0
    p.bufs["ps_b"] = ps1

    p.start_block()
    block = p.block
    ld, ts, vs, ss, os_ = p.sem("ld"), p.sem("ts"), p.sem("vs"), p.sem("ss"), p.sem("os")
    lg = p.sem("lg")
    hs = p.sem("hs")

    @block.gpsimd
    def _(g):
        lg.dma(g.dma_start(out=gw_sb[:, :, :], in_=gw_h))
        lg.dma(g.dma_start(out=wa[:, :, :], in_=sw1_h))
        lg.dma(g.dma_start(out=wb[:, :, :], in_=sw3_h))
        lg.dma(g.dma_start(out=ident_bf[:, :], in_=ident_h))
        lg.dma(g.dma_start(out=b2_sb[:, :], in_=b2_h))

    @block.sync
    def _(s):
        ld.dma(s.dma_start(out=iota32[:, :], in_=iota32_h))
        ld.dma(s.dma_start(out=iota8[:, :], in_=iota8_h))
        ld.dma(s.dma_start(out=b1_sb[:, :, :], in_=b1_h))
        ld.dma(s.dma_start(out=b3_sb[:, :, :], in_=b3_h))
        ld.dma(s.dma_start(out=sb1_sb[:, :], in_=sb1_h))
        ld.dma(s.dma_start(out=sb3_sb[:, :], in_=sb3_h))
        ld.dma(s.dma_start(out=sb2_sb[:, :], in_=sb2_h))
        ld.dma(s.dma_start(out=row_scr[0:1, 0:E], in_=gb_h))

    all_loaded = ld.n
    all_loaded_g = lg.n

    @block.vector
    def _(v):
        v.wait_ge(ld.h, all_loaded)
        v.wait_ge(lg.h, all_loaded_g)
        v.memset(ones_bf[:, :], 1.0)
        v.memset(onesrow_bf[:, :], 1.0)
        inst = v.memset(onesrow_f32[:, :], 1.0)
        vs.inc(inst)

    init_done = vs.n

    @block.tensor
    def _(t):
        t.wait_ge(vs.h, init_done)
        inst = t.matmul(ps5[:, 0:E], onesrow_f32[0:1, :], row_scr[0:1, 0:E],
                        start=True, stop=True)
        ts.inc(inst)

    gbm = ts.n

    @block.vector
    def _(v):
        v.wait_ge(ts.h, gbm)
        inst = v.tensor_copy(gb_bc[:, :], ps5[:, 0:E])
        vs.inc(inst)

    @block.scalar
    def _(s):
        s.wait_ge(ld.h, all_loaded)
        s.wait_ge(lg.h, all_loaded_g)

    emit_ln_stream(p, x_hbm, xn)
    ln_done = vs.n

    for t128 in range(T // 128):
        t0 = t128 * 128
        evac_prev = vs.n

        @block.tensor
        def _(t):
            t.wait_ge(vs.h, max(evac_prev, ln_done))
            for kc in range(KC):
                inst = t.matmul(ps5[:, 0:E], xn[:, kc, t0:t0 + 128],
                                gw_sb[:, kc, :], start=(kc == 0), stop=(kc == KC - 1))
            ts.inc(inst)

        g_mm = ts.n

        @block.vector
        def _(v):
            v.wait_ge(ts.h, g_mm)
            inst = v.tensor_tensor(e32_sc[:, :], ps5[:, 0:E], gb_bc[:, :], ALU.add)
            vs.inc(inst)
            v.wait_ge(vs.h, vs.n)
            inst = v.reduce_max(col_sc[:, 0:1], e32_sc[:, :], axis=AX.X)
            vs.inc(inst)
            v.wait_ge(vs.h, vs.n)
            inst = v.tensor_scalar_mul(col_sc[:, 1:2], col_sc[:, 0:1], -1.0)
            vs.inc(inst)

        sm1 = vs.n

        @block.scalar
        def _(s):
            s.wait_ge(vs.h, sm1)
            inst = s.activation(e32_sc[:, :], e32_sc[:, :], AF.Exp,
                                bias=col_sc[:, 1:2], scale=1.0)
            ss.inc(inst)

        sm2 = ss.n

        @block.vector
        def _(v):
            v.wait_ge(ss.h, sm2)
            def F(inst):
                vs.inc(inst)
                v.wait_ge(vs.h, vs.n)
            F(v.reduce_sum(col_sc[:, 2:3], e32_sc[:, :], axis=AX.X))
            F(v.reciprocal(col_sc[:, 3:4], col_sc[:, 2:3]))
            F(v.tensor_scalar(e32_sc[:, :], e32_sc[:, :], col_sc[:, 3:4], None,
                              ALU.mult))
            sg = e32_sc[:, :].rearrange("p (g f) -> p g f", f=4)
            first = True
            for i in range(4):
                for j in range(i + 1, 4):
                    F(v.tensor_tensor(scr_f32[:, 0:8], sg[:, :, i], sg[:, :, j],
                                      ALU.add))
                    if first:
                        F(v.tensor_copy(sg_sc[:, :], scr_f32[:, 0:8]))
                        first = False
                    else:
                        F(v.tensor_tensor(sg_sc[:, :], sg_sc[:, :], scr_f32[:, 0:8],
                                          ALU.max))
            F(v.max(out=m8_sc[:, :], in_=sg_sc[:, :]))
            F(v.max_index(out=i8_sc[:, :], in_max=m8_sc[:, :], in_values=sg_sc[:, :]))
            F(v.tensor_copy(idxf_sc[:, :], i8_sc[:, :]))
            F(v.memset(km_sc[:, 0:8], 0.0))
            for j in range(4):
                F(v.tensor_scalar(scr_f32[:, 0:8], iota8[:, :], idxf_sc[:, j:j + 1],
                                  None, ALU.is_equal))
                F(v.tensor_tensor(km_sc[:, 0:8], km_sc[:, 0:8], scr_f32[:, 0:8],
                                  ALU.add))
            kmb = km_sc[:, 0:8].unsqueeze(2).to_broadcast([128, 8, 4])
            F(v.tensor_copy(scr_f32[:, 8:40].rearrange("p (g f) -> p g f", f=4), kmb))
            F(v.tensor_scalar_add(scr_f32[:, 40:72], e32_sc[:, :], 10.0))
            F(v.tensor_tensor(scr_f32[:, 40:72], scr_f32[:, 40:72], scr_f32[:, 8:40],
                              ALU.mult))
            F(v.tensor_scalar_add(scr_f32[:, 40:72], scr_f32[:, 40:72], -10.0))
            F(v.max(out=m8_sc[:, :], in_=scr_f32[:, 40:72]))
            F(v.max_index(out=i8_sc[:, :], in_max=m8_sc[:, :],
                          in_values=scr_f32[:, 40:72]))
            F(v.tensor_copy(idxf_sc[:, :], i8_sc[:, :]))
            F(v.memset(W32[:, t128, :], 0.0))
            for j in range(4):
                F(v.tensor_scalar(scr_f32[:, 0:E], iota32[:, :], idxf_sc[:, j:j + 1],
                                  None, ALU.is_equal))
                F(v.scalar_tensor_tensor(W32[:, t128, :], scr_f32[:, 0:E],
                                         m8_sc[:, j:j + 1], W32[:, t128, :],
                                         ALU.mult, ALU.add))

        w_done = vs.n

        @block.tensor
        def _(t):
            t.wait_ge(vs.h, w_done)
            inst = t.transpose(ps4[0:32, 0:128], W32[:, t128, :], ident_bf[:, :])
            ts.inc(inst)

        wt_mm = ts.n

        @block.vector
        def _(v):
            v.wait_ge(ts.h, wt_mm)
            inst = v.tensor_copy(W32T[:, t0:t0 + 128], ps4[0:32, 0:128])
            vs.inc(inst)

    w32_done = vs.n

    def silu_epilogue(oc, tt, mm_done):
        t0 = tt * TT

        @block.scalar
        def _(s):
            s.wait_ge(ts.h, mm_done)
            inst = s.activation(sh[:, oc, t0:t0 + TT], ps0[:, :], AF.Silu,
                                bias=sb1_sb[:, oc:oc + 1], scale=1.0)
            ss.inc(inst)

        sdone = ss.n

        @block.vector
        def _(v):
            v.wait_ge(ss.h, sdone)
            inst = v.nop()
            vs.inc(inst)

    emit_proj_fm(p, xn, wa, ps0, silu_epilogue, pre_wait_v=ln_done)

    def mult_epilogue(oc, tt, mm_done):
        t0 = tt * TT

        @block.vector
        def _(v):
            v.wait_ge(ts.h, mm_done)
            inst = v.scalar_tensor_tensor(shh[:, oc, t0:t0 + TT], ps0[:, :],
                                          sb3_sb[:, oc:oc + 1],
                                          sh[:, oc, t0:t0 + TT], ALU.add, ALU.mult)
            vs.inc(inst)

    emit_proj_fm(p, xn, wb, ps0, mult_epilogue, pre_wait_v=ln_done)
    shh_done = vs.n
    sw3_mm_done = ts.n

    @block.gpsimd
    def _(g):
        g.wait_ge(ts.h, sw3_mm_done)
        lg.dma(g.dma_start(out=wa[:, :, :], in_=sw2_h))

    sw2_ld = lg.n

    def shout_epilogue(oc, tt, mm_done):
        t0 = tt * TT

        @block.vector
        def _(v):
            v.wait_ge(ts.h, mm_done)
            inst = v.tensor_scalar(sh[:, oc, t0:t0 + TT], ps0[:, :],
                                   sb2_sb[:, oc:oc + 1], None, ALU.add)
            vs.inc(inst)

    emit_proj_fm(p, shh, wa, ps0, shout_epilogue, pre_wait_v=shh_done,
                 pre_wait_ld=sw2_ld, ld_sem=lg)
    sh_done = vs.n

    wbufs = [wa, wb]
    for e in range(E):
        wbuf = wbufs[e % 2]
        w_prev_mm = ts.n

        @block.gpsimd
        def _(g):
            g.wait_ge(ts.h, w_prev_mm)
            lg.dma(g.dma_start(out=wbuf[:, :, 0:IN], in_=w1_h[:, e, :, :]))
            lg.dma(g.dma_start(out=wbuf[:, :, IN:2 * IN], in_=w3_h[:, e, :, :]))

        w_ld = lg.n
        for tt in range(NTT):
            t0 = tt * TT
            wrow_prev = vs.n

            @block.tensor
            def _(t):
                t.wait_ge(vs.h, max(w32_done, wrow_prev))
                for j in range(4):
                    inst = t.matmul(ps3[0:1, j * 128:(j + 1) * 128],
                                    W32[:, tt * 4 + j, e:e + 1], ident_bf[:, :],
                                    start=True, stop=True)
                ts.inc(inst)

            wrow_mm = ts.n

            @block.vector
            def _(v):
                v.wait_ge(ts.h, wrow_mm)
                inst = v.tensor_copy(wrow_bf[0:1, :], ps3[0:1, :])
                vs.inc(inst)

            wrow_ev = vs.n

            @block.tensor
            def _(t):
                t.wait_ge(vs.h, wrow_ev)
                inst = t.matmul(ps6[:, :], onesrow_bf[0:1, :], wrow_bf[0:1, :],
                                start=True, stop=True)
                ts.inc(inst)

            wb_mm = ts.n
            h2ev = [vs.n, vs.n]
            for hd in range(NH):
                h2_store_prev = hs.n
                psA = (ps0, ps5)[hd % 2]
                psB = (ps1, psA1)[hd % 2]
                silc = (sil_bf, sil_bf2)[hd % 2]
                h2ev_par = h2ev[hd % 2]

                @block.tensor
                def _(t):
                    t.wait_ge(lg.h, w_ld)
                    t.wait_ge(vs.h, h2ev_par)
                    for kc in range(KC):
                        t.matmul(psA[:, :], wbuf[:, kc, hd * 128:(hd + 1) * 128],
                                 xn[:, kc, t0:t0 + TT],
                                 start=(kc == 0), stop=(kc == KC - 1))
                    for kc in range(KC):
                        inst = t.matmul(psB[:, :],
                                        wbuf[:, kc, IN + hd * 128:IN + (hd + 1) * 128],
                                        xn[:, kc, t0:t0 + TT],
                                        start=(kc == 0), stop=(kc == KC - 1))
                    ts.inc(inst)

                hv_mm = ts.n

                @block.scalar
                def _(s):
                    s.wait_ge(ts.h, hv_mm)
                    inst = s.activation(silc[:, :], psA[:, :], AF.Silu,
                                        bias=b1_sb[:, e, hd:hd + 1], scale=1.0)
                    ss.inc(inst)

                sil_done = ss.n

                @block.vector
                def _(v):
                    v.wait_ge(ss.h, sil_done)
                    v.wait_ge(ts.h, wb_mm)
                    v.wait_ge(hs.h, h2_store_prev)
                    v.scalar_tensor_tensor(h2a_bf[:, :], psB[:, :],
                                           b3_sb[:, e, hd:hd + 1], silc[:, :],
                                           ALU.add, ALU.mult)
                    inst = v.tensor_tensor(h2_bf[:, hd, :], h2a_bf[:, :], ps6[:, :],
                                           ALU.mult)
                    vs.inc(inst)
                h2ev[hd % 2] = vs.n

            h2_done = vs.n

            @block.sync
            def _(s):
                s.wait_ge(vs.h, h2_done)
                hs.dma(s.dma_start(out=h2w_dram[:, e * NH:(e + 1) * NH, t0:t0 + TT],
                                   in_=h2_bf[:, :, :]))

    ph_a_done = hs.n

    for tt in range(NTT):
        t0 = tt * TT
        for eg in range(16):
            h2_prev_mm = ts.n

            @block.sync
            def _(s):
                s.wait_ge(hs.h, ph_a_done)
                s.wait_ge(ts.h, h2_prev_mm)
                ld.dma(s.dma_start(out=h2in[:, 0:8, :],
                                   in_=h2w_dram[:, eg * 8:(eg + 1) * 8, t0:t0 + TT]))

            h2_ld = ld.n
            wq_prev_mm = ts.n
            wbuf = wbufs[eg % 2]

            @block.gpsimd
            def _(g):
                g.wait_ge(ts.h, wq_prev_mm)
                lg.dma(g.dma_start(out=wbuf[:, :, :],
                                   in_=w2_h[:, eg * 2:(eg + 1) * 2, :, :]))

            wq_ld = lg.n
            rev = [vs.n, vs.n]
            for oc in range(KC):
                racc_prev = rev[oc % 2]
                psP = (ps5, psA1)[oc % 2]

                @block.tensor
                def _(t):
                    t.wait_ge(ld.h, h2_ld)
                    t.wait_ge(lg.h, wq_ld)
                    t.wait_ge(vs.h, racc_prev)
                    wv = wbuf.ap().rearrange("p c d -> p (c d)").rearrange(
                        "p (a b) -> p a b", b=D)
                    if eg == 0:
                        t.matmul(psP[:, :], b2_sb[:, oc * 128:(oc + 1) * 128],
                                 W32T[:, t0:t0 + TT], start=True, stop=False)
                    for j in range(8):
                        inst = t.matmul(psP[:, :], wv[:, j, oc * 128:(oc + 1) * 128],
                                        h2in[:, j, :],
                                        start=(j == 0 and eg != 0), stop=(j == 7))
                    ts.inc(inst)

                r_mm = ts.n

                @block.vector
                def _(v):
                    v.wait_ge(ts.h, r_mm)
                    if eg == 0:
                        inst = v.tensor_copy(racc[:, oc, :], psP[:, :])
                    else:
                        inst = v.tensor_add(racc[:, oc, :], racc[:, oc, :], psP[:, :])
                    vs.inc(inst)
                rev[oc % 2] = vs.n

        @block.sync
        def _(s):
            s.wait_ge(vs.h, vs.n)
            ld.dma(s.dma_start(out=x_t[:, :, :], in_=x_hbm[:, :, t0:t0 + TT]))

        x_ld = ld.n
        for oc in range(KC):
            @block.vector
            def _(v):
                v.wait_ge(ld.h, x_ld)
                if oc == 0:
                    v.wait_ge(os_.h, max(0, os_.n - 16 * 6))
                v.tensor_add(scr_f32[:, 0:TT], racc[:, oc, :], sh[:, oc, t0:t0 + TT])
                inst = v.tensor_add(racc[:, oc, :], scr_f32[:, 0:TT], x_t[:, oc, :])
                vs.inc(inst)

            y_done = vs.n

            @block.sync
            def _(s):
                s.wait_ge(vs.h, y_done)
                os_.dma(s.dma_start(out=y_hbm[:, oc, t0:t0 + TT], in_=racc[:, oc, :]))

    @block.sync
    def _(s):
        s.wait_ge(os_.h, os_.n)

    p.close()
    return p


def host_inputs_l3_ffn(x_row, w):
    E, IN, NH = 32, 512, 4
    w1 = np.asarray(w["w1"])
    w3 = np.asarray(w["w3"])
    w2 = np.asarray(w["w2"])
    b1 = np.asarray(w["b1"])
    b3 = np.asarray(w["b3"])
    return {
        "x": fm_x(x_row),
        "gwT": fm_w(w["gw"]),
        "gb_row": np.ascontiguousarray(np.asarray(w["gb"])[None, :]),
        "sw1T": fm_w(w["sw1"]),
        "sw3T": fm_w(w["sw3"]),
        "sw2T": fm_w(w["sw2"]),
        "sb1v": fm_vec(w["sb1"]), "sb3v": fm_vec(w["sb3"]), "sb2v": fm_vec(w["sb2"]),
        "w1T": np.ascontiguousarray(w1.reshape(E, KC, 128, IN).transpose(2, 0, 1, 3)),
        "w3T": np.ascontiguousarray(w3.reshape(E, KC, 128, IN).transpose(2, 0, 1, 3)),
        "w2T": np.ascontiguousarray(w2.reshape(E, NH, 128, D).transpose(2, 0, 1, 3)),
        "b1m": np.ascontiguousarray(b1.reshape(E, NH, 128).transpose(2, 0, 1)),
        "b3m": np.ascontiguousarray(b3.reshape(E, NH, 128).transpose(2, 0, 1)),
        "b2m": np.ascontiguousarray(np.asarray(w["b2"])),
        "iota32": np.broadcast_to(np.arange(E, dtype=np.float32)[None, :],
                                  (128, E)).copy(),
        "iota8": np.broadcast_to(np.arange(8, dtype=np.float32)[None, :],
                                 (128, 8)).copy(),
        "ident": np.eye(128, dtype=np.float32),
    }




def build_l0_ffn(shared=None):
    import os as _os
    E = 8
    IN = 1024
    NH = IN // 128
    CAP = T
    p = Prog(shared)
    x_hbm = p.dram_in("x", [128, KC, T])
    wr_h = p.dram_in("wrT_f32", [128, KC, E])
    br_h = p.dram_in("br_row", [1, E])
    w1_h = p.dram_in("w1T", [128, E, KC, IN])
    w2_h = p.dram_in("w2T", [128, E, NH, D])
    b1_h = p.dram_in("b1m", [128, E, NH])
    b2_h = p.dram_in("b2m", [E, D])
    iota8_h = p.dram_in("iota8", [128, 8])
    ident_h = p.dram_in("ident", [128, 128])
    identf_h = p.dram_in("identf", [128, 128])
    sel_h = p.dram_in("sel_prior", [64, E])
    y_hbm = p.dram_out("y", [128, KC, T])
    h2w_dram = p.nc.dram_tensor(p.prefix + "h2w_scr", [128, E * NH, T], BF16, kind="Internal").ap()
    cc_in = p.nc.dram_tensor(p.prefix + "cc_in", [E, 1], F32).ap()
    cc_out = p.nc.dram_tensor(p.prefix + "cc_out", [8 * E, 1], F32).ap()

    xn = p.sbuf("xn", [128, KC, T], BF16)
    wa = p.sbuf("wa", [128, KC, D], BF16)
    wb = p.sbuf("wb", [128, KC, D], BF16)
    x_t = p.sbuf("x_t", [128, KC, TT], F32)
    wr_sb = p.sbuf("wr_sb", [128, KC, E], F32)
    br_bc = p.sbuf("br_bc", [128, E], F32)
    W8 = p.sbuf("W8", [128, T // 128, E], BF16)
    W8T = p.sbuf("W8T", [E, T], BF16)
    b2_sb = p.sbuf("b2_sb", [E, D], BF16)
    b1_sb = p.sbuf("b1_sb", [128, E, NH], F32)
    sel_sb = p.sbuf("sel_sb", [64, E], F32)
    oh_tm = p.sbuf("oh_tm", [128, T // 128, E], BF16)
    oh_em = p.sbuf("oh_em", [E, T], F32)
    ones8t = p.sbuf("ones8t", [E, T], F32)
    pos_em = p.sbuf("pos_em", [E, T], F32)
    cts_sb = p.sbuf("cts_sb", [64, 1], F32)
    cnt_sb = p.sbuf("cnt_sb", [E, 1], F32)
    prior_sb = p.sbuf("prior_sb", [E, 1], F32)
    iota8 = p.sbuf("iota8_sb", [128, 8], F32)
    ident_bf = p.sbuf("ident_bf", [128, 128], BF16)
    ident_f = p.sbuf("ident_f", [128, 128], F32)
    ones_bf = p.sbuf("ones_bf", [128, 1], BF16)
    onesrow_bf = p.sbuf("onesrow_bf", [1, 128], BF16)
    onesrow_f32 = p.sbuf("onesrow_f32", [1, 128], F32)
    scr_f32 = p.sbuf("scr_f32", [128, 2 * TT], F32)
    row_scr = p.sbuf("row_scr", [1, 4 * TT], F32)
    ln_f32 = p.sbuf("ln_f32", [128, TT], F32)
    p.bufs["ln_f32"] = ln_f32
    gate_acc = p.sbuf("gate_acc", [128, 128], F32)
    e8_sc = p.sbuf("e8_sc", [128, 8], F32)
    m8_sc = p.sbuf("m8_sc", [128, 8], F32)
    i8_sc = p.sbuf("i8_sc", [128, 8], U32)
    idxf_sc = p.sbuf("idxf_sc", [128, 8], F32)
    col_sc = p.sbuf("col_sc", [128, 8], F32)
    sil_bf = p.sbuf("sil_bf", [128, TT], BF16)
    h2_bf = p.sbuf("h2_bf", [128, NH, TT], BF16)
    h2in = p.sbuf("h2in", [128, NH, TT], BF16)
    wrow_bf = p.sbuf("wrow_bf", [1, TT], BF16)
    wbc_sb = p.sbuf("wbc_sb", [128, TT], F32)
    racc = p.sbuf("racc", [128, KC, TT], F32)
    p.bufs["xbf_scr"] = xn[:, :, 0:TT]
    xbf_scr = p.sbuf("xbf_scr2", [128, KC, TT], BF16)
    sq_scr = p.sbuf("sq_scr2", [128, KC, TT], BF16)
    p.bufs["xbf_scr"] = xbf_scr
    p.bufs["sq_scr"] = sq_scr
    xres = x_t[:, 0:2, :]

    ps0 = p.psum("ps0", [128, TT], F32)
    ps1 = p.psum("ps1", [128, TT], F32)
    ps_c = p.psum("ps_c", [33, TT], F32)
    ps3 = p.psum("ps3", [1, TT], F32)
    ps4 = p.psum("ps4", [128, 1024], BF16)
    ps5 = p.psum("ps5", [128, TT], F32)
    ps6 = p.psum("ps6", [128, TT], F32)
    ps_g = p.psum("ps_g", [128, 128], F32)
    p.bufs["ps_a"] = ps0
    p.bufs["ps_b"] = ps1

    p.start_block()
    block = p.block
    ld, ts, vs, ss, os_ = p.sem("ld"), p.sem("ts"), p.sem("vs"), p.sem("ss"), p.sem("os")
    lg = p.sem("lg")
    hs = p.sem("hs")
    cs = p.sem("cs")

    @block.gpsimd
    def _(g):
        lg.dma(g.dma_start(out=ident_bf[:, :], in_=ident_h))
        lg.dma(g.dma_start(out=b2_sb[:, :], in_=b2_h))

    @block.sync
    def _(s):
        ld.dma(s.dma_start(out=wr_sb[:, :, :], in_=wr_h))
        ld.dma(s.dma_start(out=iota8[:, :], in_=iota8_h))
        ld.dma(s.dma_start(out=ident_f[:, :], in_=identf_h))
        ld.dma(s.dma_start(out=sel_sb[:, :], in_=sel_h))
        ld.dma(s.dma_start(out=b1_sb[:, :, :], in_=b1_h))
        ld.dma(s.dma_start(out=row_scr[0:1, 0:E], in_=br_h))

    all_loaded = ld.n
    all_loaded_g = lg.n

    @block.vector
    def _(v):
        v.wait_ge(ld.h, all_loaded)
        v.wait_ge(lg.h, all_loaded_g)
        v.memset(ones_bf[:, :], 1.0)
        v.memset(onesrow_bf[:, :], 1.0)
        v.memset(onesrow_f32[:, :], 1.0)
        inst = v.memset(ones8t[:, :], 1.0)
        vs.inc(inst)

    init_done = vs.n

    @block.tensor
    def _(t):
        t.wait_ge(vs.h, init_done)
        inst = t.matmul(ps5[:, 0:E], onesrow_f32[0:1, :], row_scr[0:1, 0:E],
                        start=True, stop=True)
        ts.inc(inst)

    brm = ts.n

    @block.vector
    def _(v):
        v.wait_ge(ts.h, brm)
        inst = v.tensor_copy(br_bc[:, :], ps5[:, 0:E])
        vs.inc(inst)

    @block.scalar
    def _(s):
        s.wait_ge(ld.h, all_loaded)
        s.wait_ge(lg.h, all_loaded_g)

    def gate_cb(kc, tt, f32_ready):
        @block.tensor
        def _(t):
            t.wait_ge(vs.h, f32_ready)
            for q in range(4):
                inst = t.matmul(ps_g[:, q * 8:(q + 1) * 8],
                                ln_f32[:, q * 128:(q + 1) * 128],
                                wr_sb[:, kc, :], start=True, stop=True)
            ts.inc(inst)

        gm = ts.n

        @block.vector
        def _(v):
            v.wait_ge(ts.h, gm)
            if kc == 0:
                inst = v.tensor_copy(gate_acc[:, tt * 32:(tt + 1) * 32], ps_g[:, 0:32])
            else:
                inst = v.tensor_add(gate_acc[:, tt * 32:(tt + 1) * 32],
                                    gate_acc[:, tt * 32:(tt + 1) * 32], ps_g[:, 0:32])
            vs.inc(inst)

    emit_ln_stream(p, x_hbm, xn, f32_cb=gate_cb)
    ln_done = vs.n
    gates_mm = ts.n

    for t128 in range(T // 128):
        @block.vector
        def _(v):
            v.wait_ge(ts.h, gates_mm)
            v.tensor_tensor(e8_sc[:, :], gate_acc[:, t128 * 8:(t128 + 1) * 8],
                            br_bc[:, :], ALU.add)
            inst = v.reduce_max(col_sc[:, 0:1], e8_sc[:, :], axis=AX.X)
            vs.inc(inst)
            v.wait_ge(vs.h, vs.n)
            inst = v.tensor_scalar_mul(col_sc[:, 1:2], col_sc[:, 0:1], -1.0)
            vs.inc(inst)

        sm1 = vs.n

        @block.scalar
        def _(s):
            s.wait_ge(vs.h, sm1)
            inst = s.activation(e8_sc[:, :], e8_sc[:, :], AF.Exp,
                                bias=col_sc[:, 1:2], scale=1.0)
            ss.inc(inst)

        sm2 = ss.n

        @block.vector
        def _(v):
            v.wait_ge(ss.h, sm2)
            inst = v.reduce_sum(col_sc[:, 2:3], e8_sc[:, :], axis=AX.X)
            vs.inc(inst)
            v.wait_ge(vs.h, vs.n)
            inst = v.reciprocal(col_sc[:, 3:4], col_sc[:, 2:3])
            vs.inc(inst)
            v.wait_ge(vs.h, vs.n)
            inst = v.tensor_scalar(e8_sc[:, :], e8_sc[:, :], col_sc[:, 3:4],
                                   None, ALU.mult)
            vs.inc(inst)
            v.wait_ge(vs.h, vs.n)
            inst = v.max(out=m8_sc[:, :], in_=e8_sc[:, :])
            vs.inc(inst)
            v.wait_ge(vs.h, vs.n)
            inst = v.max_index(out=i8_sc[:, :], in_max=m8_sc[:, :], in_values=e8_sc[:, :])
            vs.inc(inst)
            v.wait_ge(vs.h, vs.n)
            inst = v.tensor_copy(idxf_sc[:, 0:1], i8_sc[:, 0:1])
            vs.inc(inst)
            v.tensor_copy(col_sc[:, 4:5], m8_sc[:, 0:1])
            inst = v.tensor_scalar(oh_tm[:, t128, :], iota8[:, :], idxf_sc[:, 0:1],
                                   None, ALU.is_equal)
            vs.inc(inst)

        oh_done = vs.n

        @block.tensor
        def _(t):
            t.wait_ge(vs.h, oh_done)
            inst = t.transpose(ps4[0:E, 0:128], oh_tm[:, t128, :], ident_bf[:, :])
            ts.inc(inst)

        oht_mm = ts.n

        @block.vector
        def _(v):
            v.wait_ge(ts.h, oht_mm)
            inst = v.tensor_copy(oh_em[:, t128 * 128:(t128 + 1) * 128], ps4[0:E, 0:128])
            vs.inc(inst)

        @block.vector
        def _(v):
            inst = v.tensor_copy(W8[:, t128, :], oh_tm[:, t128, :])
            vs.inc(inst)
            inst = v.tensor_copy(scr_f32[:, t128:t128 + 1], col_sc[:, 4:5])
            vs.inc(inst)

    oh_all = vs.n

    @block.vector
    def _(v):
        v.wait_ge(vs.h, oh_all)
        inst = v.reduce_sum(cnt_sb[:, 0:1], oh_em[:, :], axis=AX.X)
        vs.inc(inst)
        v.wait_ge(vs.h, vs.n)

    cnt_done = vs.n

    @block.gpsimd
    def _(g):
        g.wait_ge(vs.h, cnt_done)
        lg.dma(g.dma_start(out=cc_in, in_=cnt_sb[:, :]))
        g.wait_ge(lg.h, lg.n)
        g.collective_compute(
            "AllGather", ALU.bypass, replica_groups=[list(range(8))],
            ins=[cc_in], outs=[cc_out]).then_inc(cs.h, 1)
        cs.n += 1
        g.wait_ge(cs.h, cs.n)
        lg.dma(g.dma_start(out=cts_sb[:, :], in_=cc_out))

    cts_ld = lg.n

    @block.tensor
    def _(t):
        t.wait_ge(lg.h, cts_ld)
        inst = t.matmul(ps3[0:E, 0:1] if False else ps5[0:E, 0:1],
                        sel_sb[:, :], cts_sb[:, :], start=True, stop=True)
        ts.inc(inst)

    prior_mm = ts.n

    @block.vector
    def _(v):
        v.wait_ge(ts.h, prior_mm)
        inst = v.tensor_copy(prior_sb[:, :], ps5[0:E, 0:1])
        vs.inc(inst)
        v.wait_ge(vs.h, vs.n)
        inst = v.tensor_tensor_scan(pos_em[:, :], ones8t[:, :], oh_em[:, :],
                                    prior_sb[:, :], ALU.mult, ALU.add)
        vs.inc(inst)
        v.wait_ge(vs.h, vs.n)

    pos_done = vs.n

    for t128 in range(T // 128):
        t0 = t128 * 128
        keep_prev = vs.n

        @block.tensor
        def _(t):
            t.wait_ge(vs.h, max(pos_done, keep_prev))
            inst = t.transpose(ps6[0:128, 0:E], pos_em[:, t0:t0 + 128], ident_f[0:E, 0:E])
            ts.inc(inst)

        pt_mm = ts.n

        @block.vector
        def _(v):
            v.wait_ge(ts.h, pt_mm)
            def F(inst):
                vs.inc(inst)
                v.wait_ge(vs.h, vs.n)
            F(v.tensor_tensor(e8_sc[:, :], W8[:, t128, :], ps6[0:128, 0:E], ALU.mult))
            F(v.reduce_sum(col_sc[:, 5:6], e8_sc[:, :], axis=AX.X))
            F(v.tensor_scalar(col_sc[:, 6:7], col_sc[:, 5:6], float(CAP - 1), None,
                              ALU.is_le))
            F(v.tensor_tensor(col_sc[:, 7:8], scr_f32[:, t128:t128 + 1],
                              col_sc[:, 6:7], ALU.mult))
            F(v.tensor_scalar(W8[:, t128, :], W8[:, t128, :], col_sc[:, 7:8],
                              None, ALU.mult))

        w8_done = vs.n

        @block.tensor
        def _(t):
            t.wait_ge(vs.h, w8_done)
            inst = t.transpose(ps4[0:E, 128:256], W8[:, t128, :], ident_bf[:, :])
            ts.inc(inst)

        wt_mm = ts.n

        @block.vector
        def _(v):
            v.wait_ge(ts.h, wt_mm)
            inst = v.tensor_copy(W8T[:, t0:t0 + 128], ps4[0:E, 128:256])
            vs.inc(inst)

    w8_all = vs.n

    wbufs = [wa, wb]
    for e in range(E):
        wbuf = wbufs[e % 2]
        w_prev_mm = ts.n

        @block.gpsimd
        def _(g):
            g.wait_ge(ts.h, w_prev_mm)
            lg.dma(g.dma_start(out=wbuf[:, :, :], in_=w1_h[:, e, :, :]))

        w_ld = lg.n
        for tt in range(NTT):
            t0 = tt * TT
            wrow_prev = vs.n

            @block.tensor
            def _(t):
                t.wait_ge(vs.h, max(w8_all, wrow_prev))
                for j in range(4):
                    inst = t.matmul(ps3[0:1, j * 128:(j + 1) * 128],
                                    W8[:, tt * 4 + j, e:e + 1], ident_bf[:, :],
                                    start=True, stop=True)
                ts.inc(inst)

            wrow_mm = ts.n

            @block.vector
            def _(v):
                v.wait_ge(ts.h, wrow_mm)
                inst = v.tensor_copy(wrow_bf[0:1, :], ps3[0:1, :])
                vs.inc(inst)

            wrow_ev = vs.n

            @block.tensor
            def _(t):
                t.wait_ge(vs.h, wrow_ev)
                inst = t.matmul(ps6[:, :], onesrow_bf[0:1, :], wrow_bf[0:1, :],
                                start=True, stop=True)
                ts.inc(inst)

            wb_mm = ts.n

            @block.vector
            def _(v):
                v.wait_ge(ts.h, wb_mm)
                inst = v.tensor_copy(wbc_sb[:, :], ps6[:, :])
                vs.inc(inst)

            wbc_done = vs.n
            for hd in range(NH):
                h2_store_prev = hs.n

                @block.tensor
                def _(t):
                    t.wait_ge(lg.h, w_ld)
                    t.wait_ge(vs.h, vs.n)
                    for kc in range(KC):
                        inst = t.matmul(ps0[:, :], wbuf[:, kc, hd * 128:(hd + 1) * 128],
                                        xn[:, kc, t0:t0 + TT],
                                        start=(kc == 0), stop=(kc == KC - 1))
                    ts.inc(inst)

                hv_mm = ts.n

                @block.vector
                def _(v):
                    v.wait_ge(ts.h, hv_mm)
                    v.wait_ge(vs.h, wbc_done)
                    inst = v.tensor_tensor(scr_f32[:, 0:TT], ps0[:, :], wbc_sb[:, :],
                                           ALU.mult)
                    vs.inc(inst)

                u_done = vs.n

                @block.scalar
                def _(s):
                    s.wait_ge(vs.h, u_done)
                    inst = s.activation(sil_bf[:, :], scr_f32[:, 0:TT], AF.Relu,
                                        bias=b1_sb[:, e, hd:hd + 1], scale=1.0)
                    ss.inc(inst)

                sil_done = ss.n

                @block.vector
                def _(v):
                    v.wait_ge(ss.h, sil_done)
                    v.wait_ge(hs.h, h2_store_prev)
                    inst = v.tensor_tensor(h2_bf[:, hd, :], sil_bf[:, :], wbc_sb[:, :],
                                           ALU.mult)
                    vs.inc(inst)

            h2_done = vs.n

            @block.sync
            def _(s):
                s.wait_ge(vs.h, h2_done)
                hs.dma(s.dma_start(out=h2w_dram[:, e * NH:(e + 1) * NH, t0:t0 + TT],
                                   in_=h2_bf[:, :, :]))

    ph_a_done = hs.n

    if _os.environ.get("L0F_DBG"):
        dbg_w8 = p.dram_out("dbg_w8", [128, T // 128, E])
        dbg_h2 = p.dram_out("dbg_h2", [128, NH, T])
        dbg_ga = p.dram_out("dbg_ga", [128, 128])
        dbg_pos = p.dram_out("dbg_pos", [8, T])
        dbg_oh = p.dram_out("dbg_oh", [8, T])
        dbg_pri = p.dram_out("dbg_pri", [8, 1])
        dbg_ohtm = p.dram_out("dbg_ohtm", [128, T // 128, 8])
        dbg_scr = p.dram_out("dbg_scr", [128, 8])
        dbg_m8 = p.dram_out("dbg_m8", [128, 8])
        dbg_i8 = p.dram_out("dbg_i8", [128, 8], U32)
        dbg_idxf = p.dram_out("dbg_idxf", [128, 8])
        dbg_col = p.dram_out("dbg_col", [128, 8])

        @block.gpsimd
        def _(g):
            g.wait_ge(hs.h, ph_a_done)
            lg.dma(g.dma_start(out=dbg_h2, in_=h2w_dram[:, 0:NH, :]))
            lg.dma(g.dma_start(out=dbg_w8, in_=W8[:, :, :]))
            lg.dma(g.dma_start(out=dbg_ga, in_=gate_acc[:, :]))
            lg.dma(g.dma_start(out=dbg_pos, in_=pos_em[:, :]))
            lg.dma(g.dma_start(out=dbg_oh, in_=oh_em[:, :]))
            lg.dma(g.dma_start(out=dbg_pri, in_=prior_sb[:, :]))
            lg.dma(g.dma_start(out=dbg_ohtm, in_=oh_tm[:, :, :]))
            lg.dma(g.dma_start(out=dbg_scr, in_=e8_sc[:, :]))
            lg.dma(g.dma_start(out=dbg_m8, in_=m8_sc[:, :]))
            lg.dma(g.dma_start(out=dbg_i8, in_=i8_sc[:, :]))
            lg.dma(g.dma_start(out=dbg_idxf, in_=idxf_sc[:, :]))
            lg.dma(g.dma_start(out=dbg_col, in_=col_sc[:, :]))

    for tt in range(NTT):
        t0 = tt * TT
        for e in range(E):
            h2_prev_mm = ts.n

            @block.sync
            def _(s):
                s.wait_ge(hs.h, ph_a_done)
                s.wait_ge(ts.h, h2_prev_mm)
                ld.dma(s.dma_start(out=h2in[:, :, :],
                                   in_=h2w_dram[:, e * NH:(e + 1) * NH, t0:t0 + TT]))

            h2_ld = ld.n
            wq_prev_mm = ts.n
            wbuf = wbufs[e % 2]

            @block.gpsimd
            def _(g):
                g.wait_ge(ts.h, wq_prev_mm)
                lg.dma(g.dma_start(out=wbuf[:, :, :], in_=w2_h[:, e, :, :]))

            wq_ld = lg.n
            for oc in range(KC):
                racc_prev = vs.n

                @block.tensor
                def _(t):
                    t.wait_ge(ld.h, h2_ld)
                    t.wait_ge(lg.h, wq_ld)
                    t.wait_ge(vs.h, racc_prev)
                    if e == 0:
                        t.matmul(ps5[:, :], b2_sb[:, oc * 128:(oc + 1) * 128],
                                 W8T[:, t0:t0 + TT], start=True, stop=False)
                    for j in range(NH):
                        inst = t.matmul(ps5[:, :], wbuf[:, j, oc * 128:(oc + 1) * 128],
                                        h2in[:, j, :],
                                        start=(j == 0 and e != 0), stop=(j == NH - 1))
                    ts.inc(inst)

                r_mm = ts.n

                @block.vector
                def _(v):
                    v.wait_ge(ts.h, r_mm)
                    if e == 0:
                        inst = v.tensor_copy(racc[:, oc, :], ps5[:, :])
                    else:
                        inst = v.tensor_add(racc[:, oc, :], racc[:, oc, :], ps5[:, :])
                    vs.inc(inst)

        @block.sync
        def _(s):
            s.wait_ge(vs.h, vs.n)
            ld.dma(s.dma_start(out=x_t[:, :, :], in_=x_hbm[:, :, t0:t0 + TT]))

        x_ld = ld.n
        for oc in range(KC):
            @block.vector
            def _(v):
                v.wait_ge(ld.h, x_ld)
                if oc == 0:
                    v.wait_ge(os_.h, max(0, os_.n - 16 * 6))
                inst = v.tensor_add(racc[:, oc, :], racc[:, oc, :], x_t[:, oc, :])
                vs.inc(inst)

            y_done = vs.n

            @block.sync
            def _(s):
                s.wait_ge(vs.h, y_done)
                os_.dma(s.dma_start(out=y_hbm[:, oc, t0:t0 + TT], in_=racc[:, oc, :]))

    @block.sync
    def _(s):
        s.wait_ge(os_.h, os_.n)

    p.close()
    return p


def host_inputs_l0_ffn(x_row, w, core_id):
    E, IN, NH = 8, 1024, 8
    w1 = np.asarray(w["W1"])
    w2 = np.asarray(w["W2"])
    b1 = np.asarray(w["b1"])
    sel = np.zeros((64, E), np.float32)
    for r in range(core_id):
        for e in range(E):
            sel[r * E + e, e] = 1.0
    return {
        "x": fm_x(x_row),
        "wrT_f32": fm_w(w["Wr"]),
        "br_row": np.ascontiguousarray(np.asarray(w["br"])[None, :]),
        "w1T": np.ascontiguousarray(w1.reshape(E, KC, 128, IN).transpose(2, 0, 1, 3)),
        "w2T": np.ascontiguousarray(w2.reshape(E, NH, 128, D).transpose(2, 0, 1, 3)),
        "b1m": np.ascontiguousarray(b1.reshape(E, NH, 128).transpose(2, 0, 1)),
        "b2m": np.ascontiguousarray(np.asarray(w["b2"])),
        "iota8": np.broadcast_to(np.arange(8, dtype=np.float32)[None, :],
                                 (128, 8)).copy(),
        "ident": np.eye(128, dtype=np.float32),
        "identf": np.eye(128, dtype=np.float32),
        "sel_prior": sel,
    }




def build_l2_ffn(shared=None):
    import os as _os2
    H8 = 8
    NOC = 16
    EPS_BN = 1e-5
    p = Prog(shared)
    x_hbm = p.dram_in("x", [128, KC, T])
    wq_h = p.dram_in("wqT", [128, KC, 2048])
    bq_h = p.dram_in("bqm", [128, NOC])
    gam_h = p.dram_in("bn_g", [128, 2])
    bet_h = p.dram_in("bn_b", [128, 2])
    k0_h = p.dram_in("k0", [32, 128])
    k1_h = p.dram_in("k1", [32, 128])
    ed_h = p.dram_in("edT", [128, KC, D])
    eu_h = p.dram_in("euT", [128, KC, D])
    iota32t_h = p.dram_in("iota32t", [128, 16, 32])
    ident_h = p.dram_in("ident", [128, 128])
    y_hbm = p.dram_out("y", [128, KC, T])
    q_dram = p.nc.dram_tensor(p.prefix + "q_scr", [128, NOC, T], BF16, kind="Internal").ap()
    cc_in = p.nc.dram_tensor(p.prefix + "cc_in", [128, 4], F32).ap()
    cc_out = p.nc.dram_tensor(p.prefix + "cc_out", [128, 4], F32).ap()

    xn = p.sbuf("xn", [128, KC, T], BF16)
    wa = p.sbuf("wa", [128, KC, D], BF16)
    wb = p.sbuf("wb", [128, KC, D], BF16)
    x_t = p.sbuf("x_t", [128, KC, TT], F32)
    xbf_scr = p.sbuf("xbf_scr2", [128, KC, TT], BF16)
    sq_scr = p.sbuf("sq_scr2", [128, KC, TT], BF16)
    p.bufs["xbf_scr"] = xbf_scr
    p.bufs["sq_scr"] = sq_scr
    bq_sb = p.sbuf("bq_sb", [128, NOC], F32)
    gam_sb = p.sbuf("gam_sb", [128, 2], F32)
    bet_sb = p.sbuf("bet_sb", [128, 2], F32)
    q_t = p.sbuf("q_t", [128, 2, TT], BF16)
    rs_cols = p.sbuf("rs_cols", [128, NOC, NTT], F32)
    rq_cols = p.sbuf("rq_cols", [128, NOC, NTT], F32)
    stats = p.sbuf("stats", [128, 4], F32)
    bnp = p.sbuf("bnp", [128, 8], F32)
    k01 = p.sbuf("k01", [32, 128], F32)
    k0nT = p.sbuf("k0nT", [128, 32], BF16)
    k1nT = p.sbuf("k1nT", [128, 32], BF16)
    kcol = p.sbuf("kcol", [32, 2], F32)
    qbn = p.sbuf("qbn", [128, T], BF16)
    q2s = p.sbuf("q2s", [128, T], BF16)
    rcol = p.sbuf("rcol", [128, 16], F32)
    m_all = p.sbuf("m_all", [128, 16, H8, 2, 32], BF16)
    sums = p.sbuf("sums", [128, 16, H8, 2], F32)
    m8a = p.sbuf("m8a", [128, 16, 8], F32)
    idxa = p.sbuf("idxa", [128, 16, 8], F32)
    e4a = p.sbuf("e4a", [128, 16, 4], F32)
    rz_all = p.sbuf("rz_all", [128, 16, H8], F32)
    S_tm = p.sbuf("S_tm", [128, 1024], F32)
    S_t2 = p.sbuf("S_t2", [128, 1024], F32)
    G_t = p.sbuf("G_t", [128, 1024], BF16)
    SG_t = p.sbuf("SG_t", [128, 1024], BF16)
    SGT = p.sbuf("SGT", [128, 8, 128], BF16)
    iota32t = p.sbuf("iota32t_sb", [128, 16, 32], F32)
    ident_bf = p.sbuf("ident_bf", [128, 128], BF16)
    ones_bf = p.sbuf("ones_bf", [128, 1], BF16)
    onesrow_f32 = p.sbuf("onesrow_f32", [1, 128], F32)
    scr_f32 = p.sbuf("scr_f32", [128, 2 * TT], F32)
    row_scr = p.sbuf("row_scr", [1, 4 * TT], F32)
    m8_sc = p.sbuf("m8_sc", [128, 8], F32)
    i8_sc = p.sbuf("i8_sc", [128, 8], U32)
    xres8 = p.sbuf("xres8", [128, KC, 128], F32)
    y8 = p.sbuf("y8", [128, KC, 128], F32)

    ps0 = p.psum("ps0", [128, TT], F32)
    ps1 = p.psum("ps1", [128, TT], F32)
    ps_c = p.psum("ps_c", [33, TT], F32)
    ps3 = p.psum("ps3", [1, TT], F32)
    ps4 = p.psum("ps4", [128, 1024], BF16)
    ps5 = p.psum("ps5", [128, TT], F32)
    ps6 = p.psum("ps6", [128, TT], F32)
    p.bufs["ps_a"] = ps0
    p.bufs["ps_b"] = ps1

    p.start_block()
    block = p.block
    ld, ts, vs, ss, os_ = p.sem("ld"), p.sem("ts"), p.sem("vs"), p.sem("ss"), p.sem("os")
    lg = p.sem("lg")
    cs = p.sem("cs")
    qs = p.sem("qs")

    @block.gpsimd
    def _(g):
        lg.dma(g.dma_start(out=wa[:, :, :], in_=wq_h[:, :, 0:1024]))
        lg.dma(g.dma_start(out=wb[:, :, :], in_=wq_h[:, :, 1024:2048]))
        lg.dma(g.dma_start(out=ident_bf[:, :], in_=ident_h))

    @block.sync
    def _(s):
        ld.dma(s.dma_start(out=bq_sb[:, :], in_=bq_h))
        ld.dma(s.dma_start(out=gam_sb[:, :], in_=gam_h))
        ld.dma(s.dma_start(out=bet_sb[:, :], in_=bet_h))
        ld.dma(s.dma_start(out=iota32t[:, :, :], in_=iota32t_h))

    all_loaded = ld.n
    all_loaded_g = lg.n

    @block.vector
    def _(v):
        v.wait_ge(ld.h, all_loaded)
        v.wait_ge(lg.h, all_loaded_g)
        v.memset(ones_bf[:, :], 1.0)
        inst = v.memset(onesrow_f32[:, :], 1.0)
        vs.inc(inst)

    @block.scalar
    def _(s):
        s.wait_ge(ld.h, all_loaded)
        s.wait_ge(lg.h, all_loaded_g)

    emit_ln_stream(p, x_hbm, xn)
    ln_done = vs.n

    for oc in range(NOC):
        wsel = wa if oc < 8 else wb
        ocr = oc % 8
        for tt in range(NTT):
            t0 = tt * TT
            evac_prev = ss.n

            @block.tensor
            def _(t):
                t.wait_ge(ss.h, evac_prev)
                t.wait_ge(vs.h, ln_done)
                for kc in range(KC):
                    inst = t.matmul(ps0[:, :], wsel[:, kc, ocr * 128:(ocr + 1) * 128],
                                    xn[:, kc, t0:t0 + TT],
                                    start=(kc == 0), stop=(kc == KC - 1))
                ts.inc(inst)

            q_mm = ts.n
            slot = (oc * NTT + tt) % 2
            st_prev = qs.n

            @block.scalar
            def _(s):
                s.wait_ge(ts.h, q_mm)
                s.wait_ge(qs.h, max(0, st_prev - 16))
                s.activation(q_t[:, slot, :], ps0[:, :], AF.Identity,
                             bias=bq_sb[:, oc:oc + 1], scale=1.0,
                             accum_out=rs_cols[:, oc, tt:tt + 1])
                inst = s.activation(q2s[:, 0:TT], ps0[:, :], AF.Square,
                                    bias=bq_sb[:, oc:oc + 1], scale=1.0,
                                    accum_out=rq_cols[:, oc, tt:tt + 1])
                ss.inc(inst)

            q_ev = ss.n

            @block.sync
            def _(s):
                s.wait_ge(ss.h, q_ev)
                qs.dma(s.dma_start(out=q_dram[:, oc, t0:t0 + TT], in_=q_t[:, slot, :]))

    q_all = ss.n

    wq_mm_done = ts.n

    @block.gpsimd
    def _(g):
        g.wait_ge(ts.h, wq_mm_done)
        lg.dma(g.dma_start(out=wa[:, :, :], in_=ed_h))
        lg.dma(g.dma_start(out=wb[:, :, :], in_=eu_h))

    edu_ld = lg.n

    @block.vector
    def _(v):
        v.wait_ge(ss.h, q_all)
        def F(inst):
            vs.inc(inst)
            v.wait_ge(vs.h, vs.n)
        F(v.reduce_sum(stats[:, 0:1], rs_cols[:, 0:NOC:2, :], axis=AX.XY))
        F(v.reduce_sum(stats[:, 1:2], rs_cols[:, 1:NOC:2, :], axis=AX.XY))
        F(v.reduce_sum(stats[:, 2:3], rq_cols[:, 0:NOC:2, :], axis=AX.XY))
        F(v.reduce_sum(stats[:, 3:4], rq_cols[:, 1:NOC:2, :], axis=AX.XY))

    st_done = vs.n

    @block.gpsimd
    def _(g):
        g.wait_ge(vs.h, st_done)
        lg.dma(g.dma_start(out=cc_in, in_=stats[:, :]))
        g.wait_ge(lg.h, lg.n)
        g.collective_compute("AllReduce", ALU.add, replica_groups=[list(range(8))],
                             ins=[cc_in], outs=[cc_out]).then_inc(cs.h, 1)
        cs.n += 1
        g.wait_ge(cs.h, cs.n)
        lg.dma(g.dma_start(out=stats[:, :], in_=cc_out))

    st_ld = lg.n
    NBN = float(8 * T * H8)

    @block.vector
    def _(v):
        v.wait_ge(lg.h, st_ld)
        def F(inst):
            vs.inc(inst)
            v.wait_ge(vs.h, vs.n)
        F(v.tensor_scalar_mul(bnp[:, 0:2], stats[:, 0:2], 1.0 / NBN))
        F(v.tensor_scalar_mul(bnp[:, 2:4], stats[:, 2:4], 1.0 / NBN))
        F(v.tensor_tensor(bnp[:, 4:6], bnp[:, 0:2], bnp[:, 0:2], ALU.mult))
        F(v.tensor_tensor(bnp[:, 2:4], bnp[:, 2:4], bnp[:, 4:6], ALU.subtract))
        F(v.tensor_scalar_add(bnp[:, 2:4], bnp[:, 2:4], EPS_BN))

    bn1 = vs.n

    @block.scalar
    def _(s):
        s.wait_ge(vs.h, bn1)
        inst = s.activation(bnp[:, 2:4], bnp[:, 2:4], AF.Sqrt)
        ss.inc(inst)

    bn2 = ss.n

    @block.vector
    def _(v):
        v.wait_ge(ss.h, bn2)
        inst = v.reciprocal(bnp[:, 2:4], bnp[:, 2:4])
        vs.inc(inst)
        v.wait_ge(vs.h, vs.n)

    bn_done = vs.n

    if _os2.environ.get("L2F_DBG"):
        dbg_st = p.dram_out("dbg_st", [128, 4])
        dbg_bnp = p.dram_out("dbg_bnp", [128, 8])
        dbg_rs = p.dram_out("dbg_rs", [128, NOC, NTT])
        dbg_rq = p.dram_out("dbg_rq", [128, NOC, NTT])

        @block.gpsimd
        def _(g):
            g.wait_ge(vs.h, bn_done)
            lg.dma(g.dma_start(out=dbg_st, in_=stats[:, :]))
            lg.dma(g.dma_start(out=dbg_bnp, in_=bnp[:, :]))
            lg.dma(g.dma_start(out=dbg_rs, in_=rs_cols[:, :, :]))
            lg.dma(g.dma_start(out=dbg_rq, in_=rq_cols[:, :, :]))

    for which, kh, kT in [(0, k0_h, k0nT), (1, k1_h, k1nT)]:
        @block.sync
        def _(s):
            s.wait_ge(vs.h, bn_done if which == 0 else vs.n)
            ld.dma(s.dma_start(out=k01[:, :], in_=kh))

        k_ld = ld.n

        @block.scalar
        def _(s):
            s.wait_ge(ld.h, k_ld)
            inst = s.activation(scr_f32[0:32, 0:128], k01[:, :], AF.Square)
            ss.inc(inst)

        k_sq = ss.n

        @block.vector
        def _(v):
            v.wait_ge(ss.h, k_sq)
            inst = v.reduce_sum(kcol[:, 0:1], scr_f32[0:32, 0:128], axis=AX.X)
            vs.inc(inst)
            v.wait_ge(vs.h, vs.n)

        k_rs = vs.n

        @block.scalar
        def _(s):
            s.wait_ge(vs.h, k_rs)
            inst = s.activation(kcol[:, 0:1], kcol[:, 0:1], AF.Sqrt)
            ss.inc(inst)

        k_sq2 = ss.n

        @block.vector
        def _(v):
            v.wait_ge(ss.h, k_sq2)
            inst = v.tensor_scalar_max(kcol[:, 0:1], kcol[:, 0:1], 1e-12)
            vs.inc(inst)
            v.wait_ge(vs.h, vs.n)
            inst = v.reciprocal(kcol[:, 1:2], kcol[:, 0:1])
            vs.inc(inst)
            v.wait_ge(vs.h, vs.n)
            inst = v.tensor_scalar(q2s[0:32, 0:128], k01[:, :], kcol[:, 1:2],
                                   None, ALU.mult)
            vs.inc(inst)

        k_n = vs.n

        @block.tensor
        def _(t):
            t.wait_ge(vs.h, k_n)
            inst = t.transpose(ps4[:, 0:32], q2s[0:32, 0:128], ident_bf[0:32, 0:32])
            ts.inc(inst)

        k_t = ts.n

        @block.vector
        def _(v):
            v.wait_ge(ts.h, k_t)
            inst = v.tensor_copy(kT[:, :], ps4[:, 0:32])
            vs.inc(inst)

    keys_done = vs.n

    for h in range(H8):
        for half in range(2):
            oc = 2 * h + half
            chunk_prev = vs.n

            @block.sync
            def _(s):
                s.wait_ge(vs.h, max(keys_done, chunk_prev))
                s.wait_ge(qs.h, qs.n)
                ld.dma(s.dma_start(out=qbn[:, :], in_=q_dram[:, oc, :]))

            qc_ld = ld.n

            @block.vector
            def _(v):
                v.wait_ge(ld.h, qc_ld)
                v.tensor_scalar(qbn[:, :], qbn[:, :], bnp[:, half:half + 1],
                                bnp[:, 2 + half:3 + half], ALU.subtract, ALU.mult)
                inst = v.tensor_scalar(qbn[:, :], qbn[:, :], gam_sb[:, half:half + 1],
                                       bet_sb[:, half:half + 1], ALU.mult, ALU.add)
                vs.inc(inst)

            bn_ap = vs.n

            @block.scalar
            def _(s):
                s.wait_ge(vs.h, bn_ap)
                inst = s.activation(q2s[:, :], qbn[:, :], AF.Square)
                ss.inc(inst)

            sq_done = ss.n

            for ti in range(NTT):
                ev_prev = vs.n

                @block.tensor
                def _(t):
                    if ti == 0:
                        t.wait_ge(ss.h, sq_done)
                    t.wait_ge(vs.h, ev_prev)
                    inst = t.matmul(ps3[0:1, :], ones_bf[:, 0:1],
                                    q2s[:, ti * TT:(ti + 1) * TT],
                                    start=True, stop=True)
                    ts.inc(inst)

                n_mm = ts.n

                @block.vector
                def _(v):
                    v.wait_ge(ts.h, n_mm)
                    inst = v.tensor_copy(row_scr[0:1, ti * TT:(ti + 1) * TT], ps3[0:1, :])
                    vs.inc(inst)

            nr_done = vs.n

            @block.scalar
            def _(s):
                s.wait_ge(vs.h, nr_done)
                inst = s.activation(row_scr[0:1, 0:T], row_scr[0:1, 0:T], AF.Sqrt)
                ss.inc(inst)

            nsq = ss.n

            @block.vector
            def _(v):
                v.wait_ge(ss.h, nsq)
                inst = v.tensor_scalar_max(row_scr[0:1, 0:T], row_scr[0:1, 0:T],
                                           1e-12)
                vs.inc(inst)
                v.wait_ge(vs.h, vs.n)
                inst = v.reciprocal(row_scr[0:1, 0:T], row_scr[0:1, 0:T])
                vs.inc(inst)
                v.wait_ge(vs.h, vs.n)

            nrec = vs.n

            @block.tensor
            def _(t):
                t.wait_ge(vs.h, nrec)
                for q in range(16):
                    inst = t.matmul(ps6[:, q:q + 1], row_scr[0:1, q * 128:(q + 1) * 128],
                                    onesrow_f32[0:1, 0:1], start=True, stop=True)
                ts.inc(inst)

            rc_mm = ts.n

            @block.vector
            def _(v):
                v.wait_ge(ts.h, rc_mm)
                inst = v.tensor_copy(rcol[:, :], ps6[:, 0:16])
                vs.inc(inst)

            rc_done = vs.n
            kT = k0nT if half == 0 else k1nT

            for t128 in range(16):
                t0 = t128 * 128
                sc_prev = vs.n

                @block.tensor
                def _(t):
                    t.wait_ge(vs.h, max(rc_done, sc_prev))
                    inst = t.matmul(ps5[:, 0:32], qbn[:, t0:t0 + 128], kT[:, :],
                                    start=True, stop=True)
                    ts.inc(inst)

                s_mm = ts.n

                @block.vector
                def _(v):
                    v.wait_ge(ts.h, s_mm)
                    inst = v.tensor_scalar(scr_f32[:, 0:32], ps5[:, 0:32],
                                           rcol[:, t128:t128 + 1], None, ALU.mult)
                    vs.inc(inst)
                    v.wait_ge(vs.h, vs.n)
                    inst = v.max(out=m8_sc[:, :], in_=scr_f32[:, 0:32])
                    vs.inc(inst)
                    v.wait_ge(vs.h, vs.n)
                    inst = v.max_index(out=i8_sc[:, :], in_max=m8_sc[:, :],
                                       in_values=scr_f32[:, 0:32])
                    vs.inc(inst)
                    v.wait_ge(vs.h, vs.n)
                    inst = v.tensor_copy(m8a[:, t128, :], m8_sc[:, :])
                    vs.inc(inst)
                    v.wait_ge(vs.h, vs.n)
                    inst = v.tensor_copy(idxa[:, t128, :], i8_sc[:, :])
                    vs.inc(inst)
                    v.wait_ge(vs.h, vs.n)

            tops_done = vs.n

            @block.vector
            def _(v):
                v.wait_ge(vs.h, tops_done)
                mx = m8a[:, :, 0:1].to_broadcast([128, 16, 4])
                inst = v.tensor_tensor(e4a[:, :, :], m8a[:, :, 0:4], mx, ALU.subtract)
                vs.inc(inst)

            e4s = vs.n

            @block.scalar
            def _(s):
                s.wait_ge(vs.h, e4s)
                inst = s.activation(e4a[:, :, :], e4a[:, :, :], AF.Exp)
                ss.inc(inst)

            e4e = ss.n

            @block.vector
            def _(v):
                v.wait_ge(ss.h, e4e)
                inst = v.reduce_sum(sums[:, :, h, half:half + 1], e4a[:, :, :],
                                    axis=AX.X)
                vs.inc(inst)
                v.wait_ge(vs.h, vs.n)
                mv = m_all[:, :, h, half, :]
                v.memset(mv, 0.0)
                for j in range(4):
                    idb = idxa[:, :, j:j + 1].to_broadcast([128, 16, 32])
                    v.tensor_tensor(scr_f32[:, 0:TT].rearrange(
                        "p (a b) -> p a b", b=32), iota32t[:, :, :], idb, ALU.is_equal)
                    e4b = e4a[:, :, j:j + 1].to_broadcast([128, 16, 32])
                    v.tensor_tensor(scr_f32[:, TT:2 * TT].rearrange(
                        "p (a b) -> p a b", b=32),
                        scr_f32[:, 0:TT].rearrange("p (a b) -> p a b", b=32),
                        e4b, ALU.mult)
                    inst = v.tensor_tensor(mv, mv, scr_f32[:, TT:2 * TT].rearrange(
                        "p (a b) -> p a b", b=32), ALU.add)
                vs.inc(inst)

    mvec_done = vs.n

    @block.vector
    def _(v):
        v.wait_ge(vs.h, mvec_done)
        inst = v.tensor_tensor(rz_all[:, :, :], sums[:, :, :, 0], sums[:, :, :, 1],
                               ALU.mult)
        vs.inc(inst)
        v.wait_ge(vs.h, vs.n)
        inst = v.reciprocal(rz_all[:, :, :], rz_all[:, :, :])
        vs.inc(inst)
        v.wait_ge(vs.h, vs.n)

    rz_done = vs.n

    for t128 in range(16):
        t0 = t128 * 128

        @block.vector
        def _(v):
            if t128 == 0:
                v.wait_ge(vs.h, rz_done)
            for h in range(H8):
                m0b = m_all[:, t128, h, 0, :].unsqueeze(2).to_broadcast([128, 32, 32])
                m1b = m_all[:, t128, h, 1, :].unsqueeze(1).to_broadcast([128, 32, 32])
                Sv = S_tm[:, :].rearrange("p (a b) -> p a b", b=32)
                Tv = S_t2[:, :].rearrange("p (a b) -> p a b", b=32)
                v.tensor_tensor(Tv, m0b, m1b, ALU.mult)
                if h == 0:
                    inst = v.tensor_scalar(Sv, Tv, rz_all[:, t128, h:h + 1], None,
                                           ALU.mult)
                else:
                    inst = v.scalar_tensor_tensor(Sv, Tv, rz_all[:, t128, h:h + 1],
                                                  Sv, ALU.mult, ALU.add)
            vs.inc(inst)

        s_done = vs.n

        gel_prev = ss.n

        @block.tensor
        def _(t):
            t.wait_ge(lg.h, edu_ld)
            t.wait_ge(ss.h, gel_prev)
            for kc in range(KC):
                t.matmul(ps0[:, :], xn[:, kc, t0:t0 + 128], wa[:, kc, 0:512],
                         start=(kc == 0), stop=(kc == KC - 1))
            for kc in range(KC):
                inst = t.matmul(ps1[:, :], xn[:, kc, t0:t0 + 128], wa[:, kc, 512:1024],
                                start=(kc == 0), stop=(kc == KC - 1))
            ts.inc(inst)

        eh_mm = ts.n

        @block.scalar
        def _(s):
            s.wait_ge(ts.h, eh_mm)
            s.activation(G_t[:, 0:512], ps0[:, 0:128].rearrange("p q -> p q")
                         if False else ps0[:, :], AF.Gelu)
            inst = s.activation(G_t[:, 512:1024], ps1[:, :], AF.Gelu)
            ss.inc(inst)

        g_done = ss.n

        @block.vector
        def _(v):
            v.wait_ge(ss.h, g_done)
            v.wait_ge(vs.h, s_done)
            inst = v.tensor_tensor(SG_t[:, :], S_tm[:, :], G_t[:, :], ALU.mult)
            vs.inc(inst)

        sg_done = vs.n

        @block.tensor
        def _(t):
            t.wait_ge(vs.h, sg_done)
            for ec in range(8):
                inst = t.transpose(ps4[:, ec * 128:(ec + 1) * 128],
                                   SG_t[:, ec * 128:(ec + 1) * 128], ident_bf[:, :])
            ts.inc(inst)

        sgt_mm = ts.n

        @block.vector
        def _(v):
            v.wait_ge(ts.h, sgt_mm)
            inst = v.tensor_copy(SGT[:, :, :],
                                 ps4[:, :].rearrange("p (a b) -> p a b", b=128))
            vs.inc(inst)

        sgt_done = vs.n

        @block.sync
        def _(s):
            s.wait_ge(vs.h, sgt_done)
            ld.dma(s.dma_start(out=xres8[:, :, :], in_=x_hbm[:, :, t0:t0 + 128]))

        x_ld = ld.n
        y_prev = os_.n
        for oc in range(KC):
            y_ev_prev = vs.n

            @block.tensor
            def _(t):
                if oc == 0:
                    t.wait_ge(vs.h, sgt_done)
                t.wait_ge(vs.h, y_ev_prev)
                for ec in range(8):
                    inst = t.matmul(ps5[:, 0:128], wb[:, ec, oc * 128:(oc + 1) * 128],
                                    SGT[:, ec, :], start=(ec == 0), stop=(ec == 7))
                ts.inc(inst)

            u_mm = ts.n

            @block.vector
            def _(v):
                v.wait_ge(ts.h, u_mm)
                v.wait_ge(ld.h, x_ld)
                if oc == 0:
                    v.wait_ge(os_.h, y_prev)
                inst = v.tensor_add(y8[:, oc, :], ps5[:, 0:128], xres8[:, oc, :])
                vs.inc(inst)

        yt_done = vs.n

        @block.sync
        def _(s):
            s.wait_ge(vs.h, yt_done)
            os_.dma(s.dma_start(out=y_hbm[:, :, t0:t0 + 128], in_=y8[:, :, :]))

    @block.sync
    def _(s):
        s.wait_ge(os_.h, os_.n)

    p.close()
    return p


def host_inputs_l2_ffn(x_row, w):
    iota32t = np.broadcast_to(np.arange(32, dtype=np.float32)[None, None, :],
                              (128, 16, 32)).copy()
    return {
        "x": fm_x(x_row),
        "wqT": fm_w(w["Wq"]),
        "bqm": fm_vec(w["bq"]),
        "bn_g": np.ascontiguousarray(np.asarray(w["bn_gamma"]).reshape(2, 128).T),
        "bn_b": np.ascontiguousarray(np.asarray(w["bn_beta"]).reshape(2, 128).T),
        "k0": np.ascontiguousarray(w["sub_key0"]),
        "k1": np.ascontiguousarray(w["sub_key1"]),
        "edT": fm_w(np.ascontiguousarray(np.asarray(w["expert_down"]).T)),
        "euT": fm_w(np.asarray(w["expert_up"])),
        "iota32t": iota32t,
        "ident": np.eye(128, dtype=np.float32),
    }


_SLOTS = [
    ("l0a_", "build_l0_attn"),
    ("l0f_", "build_l0_ffn"),
    ("l1a_", "build_l1_attn"),
    ("l1f_", "build_l1_ffn"),
    ("l2a_", "build_l2_attn"),
    ("l2f_", "build_l2_ffn"),
    ("l3a_", "build_l3_attn"),
    ("l3f_", "build_l3_ffn"),
]


def build_fused():
    root = Prog()
    nc = root.nc
    x_in = nc.dram_tensor("x", [128, KC, T], F32, kind="ExternalInput").ap()
    y_out = nc.dram_tensor("y", [128, KC, T], F32, kind="ExternalOutput").ap()
    states = [x_in]
    for i in range(7):
        states.append(nc.dram_tensor(f"xs{i}", [128, KC, T], F32,
                                     kind="ExternalOutput").ap())
    states.append(y_out)
    g = globals()
    for i, (prefix, fn) in enumerate(_SLOTS):
        shared = {"nc": nc, "sems": root.sems, "prefix": prefix,
                  "x_ap": states[i], "y_ap": states[i + 1],
                  "root_ctx": root.root_ctx}
        g[fn](shared)
    return root


_PROG_CACHE = {}


def _prog(name, build_fn):
    if name not in _PROG_CACHE:
        _PROG_CACHE[name] = build_fn()
    return _PROG_CACHE[name]


def _slot_host_inputs(inp, core_id):
    g = lambda n: inp[n]
    w0a = {k: g("L0_attn_" + k) for k in
           ["Wq", "bq", "Wk", "bk", "Wv", "bv", "Wo", "bo", "proj",
            "W1", "b1", "W2", "b2", "g2", "be2", "g3", "be3"]}
    w0f = {"Wr": g("L0_ffn_Wr"), "br": g("L0_ffn_br"), "W1": g("L0_ffn_W1"),
           "b1": g("L0_ffn_b1"), "W2": g("L0_ffn_W2"), "b2": g("L0_ffn_b2")}
    w1a = {k: g("L1_attn_" + k) for k in
           ["Wq", "bq", "Wk", "bk", "Wv", "bv", "Wo", "bo"]}
    w1f = {k: g("L1_ffn_" + k) for k in
           ["gate_w", "ws_in", "vs_in", "ws_out", "e_win", "e_vin", "e_wout"]}
    w2a = {k: g("L2_attn_" + k) for k in ["Wq", "Wk", "Wv", "Wo"]}
    w2f = {k: g("L2_ffn_" + k) for k in
           ["Wq", "bq", "bn_gamma", "bn_beta", "sub_key0", "sub_key1",
            "expert_down", "expert_up"]}
    w3a = {k: g("L3_attn_" + k) for k in
           ["Wq", "Wk", "Wv", "Wo", "bo", "betas", "mem", "z"]}
    w3f = {k: g("L3_ffn_" + k) for k in
           ["gw", "gb", "w1", "b1", "w3", "b3", "w2", "b2",
            "sw1", "sb1", "sw3", "sb3", "sw2", "sb2"]}
    parts = {
        "l0a_": host_inputs_l0_attn(None, w0a),
        "l0f_": host_inputs_l0_ffn(None, w0f, core_id),
        "l1a_": host_inputs_l1_attn(None, w1a),
        "l1f_": host_inputs_l1_ffn(None, w1f),
        "l2a_": host_inputs_l2_attn(None, w2a),
        "l2f_": host_inputs_l2_ffn(None, w2f),
        "l3a_": host_inputs_l3_attn(None, w3a),
        "l3f_": host_inputs_l3_ffn(None, w3f),
    }
    out = {}
    for prefix, d in parts.items():
        for k, v in d.items():
            if v is not None:
                out[prefix + k] = v
    return out


def kernel(**inp):
    ncores = 8
    if "fused" not in _PROG_CACHE:
        _PROG_CACHE["fused"] = build_fused()
    root = _PROG_CACHE["fused"]
    base = _slot_host_inputs(inp, 0)
    in_maps = []
    for c in range(ncores):
        im = dict(base) if c == 0 else {**base, **{
            k: v for k, v in _slot_host_inputs_core(inp, c).items()}}
        im["x"] = fm_x(inp["L0_attn_video"][c])
        in_maps.append(im)
    res = run_bass_kernel_spmd(root.nc, in_maps, core_ids=list(range(ncores)))
    return np.stack([fm_x_inv(np.asarray(res.results[c]["y"]))
                     for c in range(ncores)]).astype(np.float32)


def _slot_host_inputs_core(inp, core_id):
    w0f = {"Wr": inp["L0_ffn_Wr"], "br": inp["L0_ffn_br"], "W1": inp["L0_ffn_W1"],
           "b1": inp["L0_ffn_b1"], "W2": inp["L0_ffn_W2"], "b2": inp["L0_ffn_b2"]}
    d = host_inputs_l0_ffn(None, w0f, core_id)
    return {"l0f_" + k: v for k, v in d.items() if v is not None}


# version 07
# speedup vs baseline: 1.1798x; 1.0274x over previous
import contextlib
import math
import numpy as np

import concourse.bass as bass
import concourse.mybir as mybir
from concourse.bass_utils import run_bass_kernel_spmd

F32 = mybir.dt.float32
BF16 = mybir.dt.bfloat16
U32 = mybir.dt.uint32
I32 = mybir.dt.int32
AF = mybir.ActivationFunctionType
ALU = mybir.AluOpType
AX = mybir.AxisListType

D = 1024
T = 2048
KC = 8
TT = 512
NTT = T // TT
LN_EPS = 1e-5


class Sem:
    def __init__(self, handle):
        self.h = handle
        self.n = 0

    def inc(self, inst, k=1):
        inst.then_inc(self.h, k)
        self.n += k
        return self.n

    def dma(self, inst):
        return self.inc(inst, 16)


class Prog:

    def __init__(self, shared=None):
        if shared is None:
            self.nc = bass.Bass("TRN2", target_bir_lowering=False,
                                detect_race_conditions=False)
            self.sems = {}
            self.prefix = ""
            self.x_ap = None
            self.y_ap = None
            self.root_ctx = contextlib.ExitStack()
        else:
            self.nc = shared["nc"]
            self.sems = shared["sems"]
            self.prefix = shared["prefix"]
            self.x_ap = shared["x_ap"]
            self.y_ap = shared["y_ap"]
            self.root_ctx = shared["root_ctx"]
        self.ctx = contextlib.ExitStack()
        self.block = None
        self.bufs = {}

    def dram_in(self, name, shape, dtype=F32):
        if name == "x" and self.x_ap is not None:
            return self.x_ap
        return self.nc.dram_tensor(self.prefix + name, list(shape), dtype,
                                   kind="ExternalInput").ap()

    def dram_out(self, name, shape, dtype=F32):
        if name == "y" and self.y_ap is not None:
            return self.y_ap
        return self.nc.dram_tensor(self.prefix + name, list(shape), dtype,
                                   kind="ExternalOutput").ap()

    def sbuf(self, name, shape, dtype):
        h = self.ctx.enter_context(
            self.nc.sbuf_tensor(self.prefix + name, list(shape), dtype))
        self.bufs[name] = h
        return h

    def psum(self, name, shape, dtype=F32):
        h = self.ctx.enter_context(
            self.nc.psum_tensor(self.prefix + name, list(shape), dtype))
        self.bufs[name] = h
        return h

    def sem(self, name):
        if name not in self.sems:
            self.sems[name] = Sem(self.root_ctx.enter_context(self.nc.semaphore(name)))
        return self.sems[name]

    def start_block(self):
        self.block = self.ctx.enter_context(self.nc.Block())

    def close(self):
        self.ctx.close()


def fm_w(w_T):
    w_T = np.ascontiguousarray(w_T)
    din, dout = w_T.shape
    return np.ascontiguousarray(w_T.reshape(din // 128, 128, dout).transpose(1, 0, 2))


def fm_vec(v):
    v = np.asarray(v)
    return np.ascontiguousarray(v.reshape(-1, 128).T)


def fm_x(x):
    if x is None:
        return None
    return np.ascontiguousarray(np.asarray(x).T.reshape(KC, 128, T).transpose(1, 0, 2))


def fm_x_inv(x_fm):
    return np.ascontiguousarray(x_fm.transpose(1, 0, 2).reshape(D, T).T)


def host_consts_common():
    c = {}
    psw = np.zeros((128, 128), np.float32)
    for k in range(128):
        psw[k, k ^ 1] = 1.0
    c["p_swap"] = psw
    j = np.arange(896)[None, :]
    kr = np.arange(128)[:, None]
    c["causal_strip"] = (j >= kr + 384).astype(np.float32)
    return c


def host_consts_rope():
    dh = 64
    inv_freq = 1.0 / (10000.0 ** (np.arange(0, dh, 2, dtype=np.float64) / dh))
    t = np.arange(T, dtype=np.float64)
    ang = np.outer(inv_freq, t)
    cos_i = np.cos(ang)
    sin_i = np.sin(ang)
    p = np.arange(128)
    i_of_p = (p % 64) // 2
    C = cos_i[i_of_p, :].astype(np.float32)
    sgn = np.where(p % 2 == 0, 1.0, -1.0)
    S2 = (sin_i[i_of_p, :] * sgn[:, None]).astype(np.float32)
    return {"rope_C": C, "rope_S2": S2}



def emit_ln_stream(p, x_hbm, xn_sb, ld_dma_engine="sync", f32_cb=None):
    block = p.block
    ts, vs, ss, ld = p.sem("ts"), p.sem("vs"), p.sem("ss"), p.sem("ld")
    x_t = p.bufs["x_t"]
    xbf = p.bufs["xbf_scr"]
    sq = p.bufs["sq_scr"]
    ones_bf = p.bufs["ones_bf"]
    onesrow_f32 = p.bufs["onesrow_f32"]
    scr = p.bufs["scr_f32"]
    row = p.bufs["row_scr"]
    ps_a, ps_b, ps_c = p.bufs["ps_a"], p.bufs["ps_b"], p.bufs["ps_c"]
    ln_t1 = scr[:, 0:TT]

    for tt in range(NTT):
        t0 = tt * TT
        apply_done_prev = vs.n

        @block.sync
        def _(s):
            s.wait_ge(vs.h, apply_done_prev)
            ld.dma(s.dma_start(out=x_t[:, :, :], in_=x_hbm[:, :, t0:t0 + TT]))

        x_ld = ld.n
        stats_done_prev = ts.n

        @block.vector
        def _(v):
            v.wait_ge(ld.h, x_ld)
            v.wait_ge(ts.h, stats_done_prev)
            for kc in range(KC):
                inst = v.tensor_copy(xbf[:, kc, :], x_t[:, kc, :])
            vs.inc(inst)

        @block.scalar
        def _(s):
            s.wait_ge(ld.h, x_ld)
            s.wait_ge(ts.h, stats_done_prev)
            for kc in range(KC):
                inst = s.activation(sq[:, kc, :], x_t[:, kc, :], AF.Square)
            ss.inc(inst)

        v_ready, s_ready = vs.n, ss.n

        @block.tensor
        def _(t):
            t.wait_ge(vs.h, v_ready)
            t.wait_ge(ss.h, s_ready)
            for kc in range(KC):
                t.matmul(ps_c[0:1, :], ones_bf[:, 0:1], xbf[:, kc, :],
                         start=(kc == 0), stop=(kc == KC - 1))
            for kc in range(KC):
                inst = t.matmul(ps_c[32:33, :], ones_bf[:, 0:1], sq[:, kc, :],
                                start=(kc == 0), stop=(kc == KC - 1))
            ts.inc(inst)

        t_ready = ts.n
        mu = row[0:1, 0:TT]
        var = row[0:1, TT:2 * TT]
        rr = row[0:1, 2 * TT:3 * TT]
        mur = row[0:1, 3 * TT:4 * TT]

        @block.vector
        def _(v):
            v.wait_ge(ts.h, t_ready)
            v.tensor_scalar_mul(mu, ps_c[0:1, :], 1.0 / D)
            v.tensor_scalar_mul(var, ps_c[32:33, :], 1.0 / D)
            v.tensor_tensor(rr, mu, mu, ALU.mult)
            v.tensor_tensor(var, var, rr, ALU.subtract)
            inst = v.tensor_scalar_add(var, var, LN_EPS)
            vs.inc(inst)

        vv = vs.n

        @block.scalar
        def _(s):
            s.wait_ge(vs.h, vv)
            inst = s.activation(rr, var, AF.Sqrt)
            ss.inc(inst)

        sv = ss.n

        @block.vector
        def _(v):
            v.wait_ge(ss.h, sv)
            inst = v.reciprocal(rr, rr)
            vs.inc(inst)
            v.wait_ge(vs.h, vs.n)
            inst = v.tensor_tensor(mur, mu, rr, ALU.mult)
            vs.inc(inst)

        vv2 = vs.n

        @block.tensor
        def _(t):
            t.wait_ge(vs.h, vv2)
            t.matmul(ps_a[:, :], onesrow_f32[0:1, :], rr, start=True, stop=True)
            inst = t.matmul(ps_b[:, :], onesrow_f32[0:1, :], mur, start=True, stop=True)
            ts.inc(inst)

        tv2 = ts.n

        if f32_cb is None:
            @block.vector
            def _(v):
                v.wait_ge(ts.h, tv2)
                for kc in range(KC):
                    v.tensor_tensor(ln_t1, x_t[:, kc, :], ps_a[:, :], ALU.mult)
                    inst = v.tensor_tensor(xn_sb[:, kc, t0:t0 + TT], ln_t1,
                                           ps_b[:, :], ALU.subtract)
                vs.inc(inst)
        else:
            ln_f32 = p.bufs["ln_f32"]
            for kc in range(KC):
                cb_prev = ts.n

                @block.vector
                def _(v):
                    if kc == 0:
                        v.wait_ge(ts.h, tv2)
                    v.wait_ge(ts.h, cb_prev)
                    v.tensor_tensor(ln_t1, x_t[:, kc, :], ps_a[:, :], ALU.mult)
                    v.tensor_tensor(ln_f32[:, :], ln_t1, ps_b[:, :], ALU.subtract)
                    inst = v.tensor_copy(xn_sb[:, kc, t0:t0 + TT], ln_f32[:, :])
                    vs.inc(inst)

                f32_cb(kc, tt, vs.n)


def emit_proj_fm(p, src_sb, w_sb, ps, epilogue, n_kc=KC, oc_list=None, tt_list=None,
                 pre_wait_v=0, pre_wait_ld=0, ld_sem=None):
    block = p.block
    ts, vs, ld = p.sem("ts"), p.sem("vs"), p.sem("ld")
    for oc in (oc_list if oc_list is not None else range(KC)):
        for tt in (tt_list if tt_list is not None else range(NTT)):
            t0 = tt * TT
            evac_prev = vs.n

            @block.tensor
            def _(t):
                t.wait_ge(vs.h, max(evac_prev, pre_wait_v))
                if pre_wait_ld:
                    t.wait_ge((ld_sem or ld).h, pre_wait_ld)
                for kc in range(n_kc):
                    inst = t.matmul(ps[:, :], w_sb[:, kc, oc * 128:(oc + 1) * 128],
                                    src_sb[:, kc, t0:t0 + TT],
                                    start=(kc == 0), stop=(kc == n_kc - 1))
                ts.inc(inst)

            epilogue(oc, tt, ts.inc.__self__.n if False else ts.n)



def emit_attn64(p, qf, kf, v65, av, v_done, chunk_plan, scale):
    block = p.block
    ts, vs, ss = p.sem("ts"), p.sem("vs"), p.sem("ss")
    ps_s, ps_av, ps_r = p.bufs["ps_s"], p.bufs["ps_av"], p.bufs["ps_r"]
    pt = [p.bufs["pt0"], p.bufs["pt1"]]
    shift_sb = p.bufs["shift_sb"]
    onesrow_f32 = p.bufs["onesrow_f32"]
    row_scr = p.bufs["row_scr"]
    scr_f32 = p.bufs["scr_f32"]
    avs = scr_f32[0:64, 0:TT]
    for h in range(16):
        kc_h = h // 2
        prow = (h % 2) * 64
        for qt in range(NTT):
            q0 = qt * TT
            kts = chunk_plan(h, qt)
            av_evac_prev = vs.n
            for i, (kt, mask) in enumerate(kts):
                k0 = kt * 128
                slot = pt[i % 2]
                exp_prev = ss.n

                @block.tensor
                def _(t):
                    t.wait_ge(ss.h, exp_prev)
                    if i == 0:
                        t.wait_ge(vs.h, max(v_done, av_evac_prev))
                    inst = t.matmul(ps_s[:, :],
                                    kf[prow:prow + 64, kc_h, k0:k0 + 128],
                                    qf[prow:prow + 64, kc_h, q0:q0 + TT],
                                    start=True, stop=True)
                    ts.inc(inst)

                sc_done = ts.n
                mask_prev = vs.n

                @block.scalar
                def _(s):
                    s.wait_ge(ts.h, sc_done)
                    if mask is not None:
                        s.wait_ge(vs.h, mask_prev)
                    inst = s.activation(slot[:, :], ps_s[:, :], AF.Exp,
                                        bias=shift_sb[:, 0:1], scale=scale)
                    ss.inc(inst)

                e_done = ss.n
                if mask is not None:
                    @block.vector
                    def _(v):
                        v.wait_ge(ss.h, e_done)
                        if mask[0] == "strip":
                            dd = mask[1]
                            strip = p.bufs["strip"]
                            inst = v.tensor_tensor(slot[:, :], slot[:, :],
                                                   strip[:, 384 + dd:384 + dd + TT],
                                                   ALU.mult)
                        elif mask[0] == "m5":
                            l2m = p.bufs["l2m"]
                            inst = v.tensor_tensor(slot[:, :], slot[:, :],
                                                   l2m[:, mask[1], :], ALU.mult)
                        elif mask[0] == "p16":
                            mask16 = p.bufs["mask16"]
                            inst = v.tensor_scalar(slot[:, :], slot[:, :],
                                                   mask16[:, 0:1], None, ALU.mult)
                        vs.inc(inst)
                    m_done = vs.n
                else:
                    m_done = None

                @block.tensor
                def _(t):
                    if m_done is None:
                        t.wait_ge(ss.h, e_done)
                    else:
                        t.wait_ge(vs.h, m_done)
                    inst = t.matmul(ps_av[:, :], v65[:, kt, h * 65:h * 65 + 65],
                                    slot[:, :], start=(i == 0), stop=(i == len(kts) - 1))
                    ts.inc(inst)

            av_done = ts.n
            r_row = row_scr[0:1, 0:TT]

            @block.vector
            def _(v):
                v.wait_ge(ts.h, av_done)
                inst = v.reciprocal(r_row, ps_av[64:65, :])
                vs.inc(inst)
                v.wait_ge(vs.h, vs.n)

            r_done = vs.n

            @block.tensor
            def _(t):
                t.wait_ge(vs.h, r_done)
                inst = t.matmul(ps_r[:, :], onesrow_f32[0:1, 0:64], r_row,
                                start=True, stop=True)
                ts.inc(inst)

            rb_done = ts.n

            @block.vector
            def _(v):
                v.wait_ge(ts.h, rb_done)
                v.tensor_copy(avs, ps_av[0:64, :])
                inst = v.tensor_tensor(av[prow:prow + 64, kc_h, q0:q0 + TT],
                                       avs, ps_r[:, :], ALU.mult)
                vs.inc(inst)
    return vs.n


def build_l1_attn(shared=None):
    p = Prog(shared)
    x_hbm = p.dram_in("x", [128, KC, T])
    w_hbm = {n: p.dram_in(n, [128, KC, D]) for n in ["wqT", "wkT", "wvT", "woT"]}
    b_hbm = {n: p.dram_in(n, [128, KC]) for n in ["bq", "bk", "bv", "bo"]}
    bvrow_hbm = p.dram_in("bv_row", [1, D])
    ropeC_h = p.dram_in("rope_C", [128, T])
    ropeS2_h = p.dram_in("rope_S2", [128, T])
    pswap_h = p.dram_in("p_swap", [128, 128])
    strip_h = p.dram_in("causal_strip", [128, 896])
    y_hbm = p.dram_out("y", [128, KC, T])

    xn = p.sbuf("xn", [128, KC, T], BF16)
    qf = p.sbuf("qf", [128, KC, T], BF16)
    kf = p.sbuf("kf", [128, KC, T], BF16)
    v65 = p.sbuf("v65", [128, T // 128, 16 * 65], BF16)
    wa = p.sbuf("wa", [128, KC, D], BF16)
    wb = p.sbuf("wb", [128, KC, D], BF16)
    x_t = p.sbuf("x_t", [128, KC, TT], F32)
    b_sb = {n: p.sbuf(n + "_sb", [128, KC], F32) for n in ["bq", "bk", "bv", "bo"]}
    ropeC = p.sbuf("ropeC", [128, T], BF16)
    ropeS2 = p.sbuf("ropeS2", [128, T], BF16)
    pswap = p.sbuf("pswap", [128, 128], BF16)
    strip = p.sbuf("strip", [128, 896], BF16)
    ones_bf = p.sbuf("ones_bf", [128, 1], BF16)
    shift_sb = p.sbuf("shift_sb", [128, 1], F32)
    onesrow_bf = p.sbuf("onesrow_bf", [1, 128], BF16)
    onesrow_f32 = p.sbuf("onesrow_f32", [1, 128], F32)
    bvb = p.sbuf("bvb", [128, D], F32)
    p.bufs["xbf_scr"] = qf[:, :, 0:TT]
    p.bufs["sq_scr"] = kf[:, :, 0:TT]
    scr_f32 = p.sbuf("scr_f32", [128, 2 * TT], F32)
    row_scr = p.sbuf("row_scr", [1, 4 * TT], F32)
    rope_s2 = p.sbuf("rope_s2", [128, TT], BF16)
    pt = [p.sbuf("pt0", [128, TT], BF16), p.sbuf("pt1", [128, TT], BF16)]
    r_bf = p.sbuf("r_bf", [1, TT], BF16)
    xres = x_t[:, 0:2, :]

    ps_s = p.psum("ps_s", [128, TT], F32)
    ps_av = p.psum("ps_av", [65, TT], F32)
    ps_a = p.psum("ps_a", [128, TT], F32)
    ps_b = p.psum("ps_b", [128, TT], F32)
    ps_c = p.psum("ps_c", [33, TT], F32)
    ps_v0 = p.psum("ps_v0", [128, TT], F32)
    ps_v1 = p.psum("ps_v1", [128, TT], F32)
    ps_r = p.psum("ps_r", [64, TT], F32)

    p.start_block()
    block = p.block
    ld, ts, vs, ss, os_ = p.sem("ld"), p.sem("ts"), p.sem("vs"), p.sem("ss"), p.sem("os")
    bvrow = row_scr[0:1, 0:D]

    @block.sync
    def _(s):
        for n in ["bq", "bk", "bv", "bo"]:
            ld.dma(s.dma_start(out=b_sb[n][:, :], in_=b_hbm[n]))
        ld.dma(s.dma_start(out=bvrow, in_=bvrow_hbm))

    lg = p.sem("lg")

    @block.gpsimd
    def _(g):
        lg.dma(g.dma_start(out=wa[:, :, :], in_=w_hbm["wqT"]))
        lg.dma(g.dma_start(out=wb[:, :, :], in_=w_hbm["wkT"]))
        lg.dma(g.dma_start(out=ropeC[:, :], in_=ropeC_h))
        lg.dma(g.dma_start(out=ropeS2[:, :], in_=ropeS2_h))
        lg.dma(g.dma_start(out=pswap[:, :], in_=pswap_h))
        lg.dma(g.dma_start(out=strip[:, :], in_=strip_h))

    all_loaded = ld.n
    all_loaded_g = lg.n

    @block.vector
    def _(v):
        v.wait_ge(ld.h, all_loaded)
        v.wait_ge(lg.h, all_loaded_g)
        v.memset(ones_bf[:, :], 1.0)
        v.memset(shift_sb[:, :], -8.0)
        v.memset(onesrow_bf[:, :], 1.0)
        v.memset(onesrow_f32[:, :], 1.0)
        inst = v.memset(v65[:, :, :], 1.0)
        vs.inc(inst)

    init_done = vs.n

    @block.tensor
    def _(t):
        t.wait_ge(ld.h, all_loaded)
        t.wait_ge(vs.h, init_done)
        t.matmul(ps_v0[:, :], onesrow_f32[0:1, :], bvrow[0:1, 0:512], start=True, stop=True)
        inst = t.matmul(ps_v1[:, :], onesrow_f32[0:1, :], bvrow[0:1, 512:1024],
                        start=True, stop=True)
        ts.inc(inst)

    bvb_mm = ts.n

    @block.vector
    def _(v):
        v.wait_ge(ts.h, bvb_mm)
        v.tensor_copy(bvb[:, 0:512], ps_v0[:, :])
        inst = v.tensor_copy(bvb[:, 512:1024], ps_v1[:, :])
        vs.inc(inst)

    emit_ln_stream(p, x_hbm, xn)
    ln_done = vs.n

    def rope_epilogue(bias_sb, out_sb):
        def epi(oc, tt, mm_done):
            t0 = tt * TT
            rope_s1 = scr_f32[:, 0:TT]
            rope_s3 = scr_f32[:, TT:2 * TT]

            @block.vector
            def _(v):
                v.wait_ge(ts.h, mm_done)
                v.tensor_scalar(rope_s1, ps_s[:, :], bias_sb[:, oc:oc + 1],
                                None, ALU.add)
                inst = v.tensor_tensor(rope_s2[:, :], rope_s1,
                                       ropeS2[:, t0:t0 + TT], ALU.mult)
                vs.inc(inst)

            u_done = vs.n

            @block.tensor
            def _(t):
                t.wait_ge(vs.h, u_done)
                inst = t.matmul(ps_a[:, :], pswap[:, :], rope_s2[:, :],
                                start=True, stop=True)
                ts.inc(inst)

            sw_done = ts.n

            @block.vector
            def _(v):
                v.tensor_tensor(rope_s3, rope_s1, ropeC[:, t0:t0 + TT], ALU.mult)
                v.wait_ge(ts.h, sw_done)
                inst = v.tensor_tensor(out_sb[:, oc, t0:t0 + TT], rope_s3,
                                       ps_a[:, :], ALU.add)
                vs.inc(inst)
        return epi

    emit_proj_fm(p, xn, wa, ps_s, rope_epilogue(b_sb["bq"], qf), pre_wait_v=ln_done)
    q_mm_done = ts.n

    @block.gpsimd
    def _(g):
        g.wait_ge(ts.h, q_mm_done)
        lg.dma(g.dma_start(out=wa[:, :, :], in_=w_hbm["wvT"]))

    wv_loaded = lg.n

    emit_proj_fm(p, xn, wb, ps_s, rope_epilogue(b_sb["bk"], kf), pre_wait_v=ln_done)
    k_mm_done = ts.n

    @block.gpsimd
    def _(g):
        g.wait_ge(ts.h, k_mm_done)
        lg.dma(g.dma_start(out=wb[:, :, :], in_=w_hbm["woT"]))

    wo_loaded = lg.n

    for t128 in range(T // 128):
        evac_prev = vs.n

        @block.tensor
        def _(t):
            t.wait_ge(vs.h, max(evac_prev, ln_done))
            t.wait_ge(lg.h, wv_loaded)
            for kc in range(KC):
                t.matmul(ps_v0[:, :], xn[:, kc, t128 * 128:(t128 + 1) * 128],
                         wa[:, kc, 0:512], start=(kc == 0), stop=(kc == KC - 1))
            for kc in range(KC):
                inst = t.matmul(ps_v1[:, :], xn[:, kc, t128 * 128:(t128 + 1) * 128],
                                wa[:, kc, 512:1024], start=(kc == 0), stop=(kc == KC - 1))
            ts.inc(inst)

        vmm = ts.n

        @block.vector
        def _(v):
            v.wait_ge(ts.h, vmm)
            view = v65[:, t128, :].rearrange("p (h c) -> p h c", c=65)
            vs0 = ps_v0[:, :].rearrange("p (h c) -> p h c", c=64)
            vs1 = ps_v1[:, :].rearrange("p (h c) -> p h c", c=64)
            b0 = bvb[:, 0:512].rearrange("p (h c) -> p h c", c=64)
            b1 = bvb[:, 512:1024].rearrange("p (h c) -> p h c", c=64)
            v.tensor_tensor(view[:, 0:8, 0:64], vs0, b0, ALU.add)
            inst = v.tensor_tensor(view[:, 8:16, 0:64], vs1, b1, ALU.add)
            vs.inc(inst)

    v_done = vs.n
    av = xn

    def l1_chunks(h, qt):
        out = []
        for kt in range(0, min(16, qt * 4 + 4)):
            dd = qt * TT - kt * 128
            out.append((kt, ("strip", dd) if dd < 127 else None))
        return out

    emit_attn64(p, qf, kf, v65, av, v_done, l1_chunks, 0.125)
    attn_done = vs.n

    def out_epilogue(oc, tt, mm_done):
        t0 = tt * TT
        slot = (oc * NTT + tt) % 2
        y_scr = scr_f32[:, slot * TT:(slot + 1) * TT]
        stt_prev2 = max(0, vs.n - 1)

        @block.sync
        def _(s):
            s.wait_ge(vs.h, stt_prev2)
            ld.dma(s.dma_start(out=xres[:, slot, :], in_=x_hbm[:, oc, t0:t0 + TT]))

        x_ld = ld.n
        os_prev2 = max(0, os_.n - 16)

        @block.vector
        def _(v):
            v.wait_ge(ts.h, mm_done)
            v.wait_ge(ld.h, x_ld)
            v.wait_ge(os_.h, os_prev2)
            inst = v.scalar_tensor_tensor(y_scr, ps_s[:, :], b_sb["bo"][:, oc:oc + 1],
                                          xres[:, slot, :], ALU.add, ALU.add)
            vs.inc(inst)

        y_done = vs.n

        @block.sync
        def _(s):
            s.wait_ge(vs.h, y_done)
            os_.dma(s.dma_start(out=y_hbm[:, oc, t0:t0 + TT], in_=y_scr))

    emit_proj_fm(p, av, wb, ps_s, out_epilogue, pre_wait_v=attn_done,
                 pre_wait_ld=wo_loaded, ld_sem=lg)

    @block.sync
    def _(s):
        s.wait_ge(os_.h, os_.n)

    p.close()
    return p


def host_inputs_l1_attn(x_row, w):
    cc = host_consts_common()
    rc = host_consts_rope()
    return {
        "x": fm_x(x_row),
        "wqT": fm_w(np.ascontiguousarray(w["Wq"].T)),
        "wkT": fm_w(np.ascontiguousarray(w["Wk"].T)),
        "wvT": fm_w(np.ascontiguousarray(w["Wv"].T)),
        "woT": fm_w(np.ascontiguousarray(w["Wo"].T)),
        "bq": fm_vec(w["bq"]), "bk": fm_vec(w["bk"]),
        "bv": fm_vec(w["bv"]), "bo": fm_vec(w["bo"]),
        "bv_row": np.ascontiguousarray(w["bv"][None, :]),
        "rope_C": rc["rope_C"], "rope_S2": rc["rope_S2"],
        "p_swap": cc["p_swap"], "causal_strip": cc["causal_strip"],
    }




def build_l2_attn(shared=None):
    p = Prog(shared)
    x_hbm = p.dram_in("x", [128, KC, T])
    w_hbm = {n: p.dram_in(n, [128, KC, D]) for n in ["wqT", "wkT", "wvT", "woT"]}
    l2m_h = p.dram_in("l2_masks", [128, 5, TT])
    mask16_h = p.dram_in("mask16_in", [128, 1])
    y_hbm = p.dram_out("y", [128, KC, T])

    xn = p.sbuf("xn", [128, KC, T], BF16)
    qf = p.sbuf("qf", [128, KC, T], BF16)
    kf = p.sbuf("kf", [128, KC, T], BF16)
    v65 = p.sbuf("v65", [128, T // 128, 16 * 65], BF16)
    wa = p.sbuf("wa", [128, KC, D], BF16)
    wb = p.sbuf("wb", [128, KC, D], BF16)
    x_t = p.sbuf("x_t", [128, KC, TT], F32)
    l2m = p.sbuf("l2m", [128, 5, TT], BF16)
    mask16 = p.sbuf("mask16", [128, 1], F32)
    ones_bf = p.sbuf("ones_bf", [128, 1], BF16)
    shift_sb = p.sbuf("shift_sb", [128, 1], F32)
    onesrow_f32 = p.sbuf("onesrow_f32", [1, 128], F32)
    p.bufs["xbf_scr"] = qf[:, :, 0:TT]
    p.bufs["sq_scr"] = kf[:, :, 0:TT]
    scr_f32 = p.sbuf("scr_f32", [128, 2 * TT], F32)
    row_scr = p.sbuf("row_scr", [1, 4 * TT], F32)
    pt = [p.sbuf("pt0", [128, TT], BF16), p.sbuf("pt1", [128, TT], BF16)]
    xres = x_t[:, 0:2, :]

    ps_s = p.psum("ps_s", [128, TT], F32)
    ps_av = p.psum("ps_av", [65, TT], F32)
    ps_a = p.psum("ps_a", [128, TT], F32)
    ps_b = p.psum("ps_b", [128, TT], F32)
    ps_c = p.psum("ps_c", [33, TT], F32)
    ps_v0 = p.psum("ps_v0", [128, TT], F32)
    ps_v1 = p.psum("ps_v1", [128, TT], F32)
    ps_r = p.psum("ps_r", [64, TT], F32)

    p.start_block()
    block = p.block
    ld, ts, vs, ss, os_ = p.sem("ld"), p.sem("ts"), p.sem("vs"), p.sem("ss"), p.sem("os")
    lg = p.sem("lg")

    @block.sync
    def _(s):
        ld.dma(s.dma_start(out=mask16[:, :], in_=mask16_h))

    @block.gpsimd
    def _(g):
        lg.dma(g.dma_start(out=wa[:, :, :], in_=w_hbm["wqT"]))
        lg.dma(g.dma_start(out=wb[:, :, :], in_=w_hbm["wkT"]))
        lg.dma(g.dma_start(out=l2m[:, :, :], in_=l2m_h))

    all_loaded = ld.n
    all_loaded_g = lg.n

    @block.vector
    def _(v):
        v.wait_ge(ld.h, all_loaded)
        v.wait_ge(lg.h, all_loaded_g)
        v.memset(ones_bf[:, :], 1.0)
        v.memset(shift_sb[:, :], -8.0)
        v.memset(onesrow_f32[:, :], 1.0)
        inst = v.memset(v65[:, :, :], 1.0)
        vs.inc(inst)

    @block.scalar
    def _(s):
        s.wait_ge(ld.h, all_loaded)

    emit_ln_stream(p, x_hbm, xn)
    ln_done = vs.n

    def cast_epilogue(out_sb):
        def epi(oc, tt, mm_done):
            t0 = tt * TT

            @block.vector
            def _(v):
                v.wait_ge(ts.h, mm_done)
                inst = v.tensor_copy(out_sb[:, oc, t0:t0 + TT], ps_s[:, :])
                vs.inc(inst)
        return epi

    emit_proj_fm(p, xn, wa, ps_s, cast_epilogue(qf), pre_wait_v=ln_done)
    q_mm_done = ts.n

    @block.gpsimd
    def _(g):
        g.wait_ge(ts.h, q_mm_done)
        lg.dma(g.dma_start(out=wa[:, :, :], in_=w_hbm["wvT"]))

    wv_loaded = lg.n

    emit_proj_fm(p, xn, wb, ps_s, cast_epilogue(kf), pre_wait_v=ln_done)
    k_mm_done = ts.n

    @block.gpsimd
    def _(g):
        g.wait_ge(ts.h, k_mm_done)
        lg.dma(g.dma_start(out=wb[:, :, :], in_=w_hbm["woT"]))

    wo_loaded = lg.n

    for t128 in range(T // 128):
        evac_prev = vs.n

        @block.tensor
        def _(t):
            t.wait_ge(vs.h, max(evac_prev, ln_done))
            t.wait_ge(lg.h, wv_loaded)
            for kc in range(KC):
                t.matmul(ps_v0[:, :], xn[:, kc, t128 * 128:(t128 + 1) * 128],
                         wa[:, kc, 0:512], start=(kc == 0), stop=(kc == KC - 1))
            for kc in range(KC):
                inst = t.matmul(ps_v1[:, :], xn[:, kc, t128 * 128:(t128 + 1) * 128],
                                wa[:, kc, 512:1024], start=(kc == 0), stop=(kc == KC - 1))
            ts.inc(inst)

        vmm = ts.n

        @block.vector
        def _(v):
            v.wait_ge(ts.h, vmm)
            view = v65[:, t128, :].rearrange("p (h c) -> p h c", c=65)
            vs0 = ps_v0[:, :].rearrange("p (h c) -> p h c", c=64)
            vs1 = ps_v1[:, :].rearrange("p (h c) -> p h c", c=64)
            v.tensor_copy(view[:, 0:8, 0:64], vs0)
            inst = v.tensor_copy(view[:, 8:16, 0:64], vs1)
            vs.inc(inst)

    v_done = vs.n
    av = xn

    def l2_chunks(h, qt):
        if qt == 0:
            return [(0, ("m5", 4)), (1, ("m5", 1)), (2, ("m5", 2)), (3, ("m5", 3))]
        out = [(0, ("p16",))]
        for j in range(4):
            out.append((qt * 4 + j, ("m5", j)))
        return out

    emit_attn64(p, qf, kf, v65, av, v_done, l2_chunks, 0.125)
    attn_done = vs.n

    def out_epilogue(oc, tt, mm_done):
        t0 = tt * TT
        slot = (oc * NTT + tt) % 2
        y_scr = scr_f32[:, slot * TT:(slot + 1) * TT]
        stt_prev2 = max(0, vs.n - 1)

        @block.sync
        def _(s):
            s.wait_ge(vs.h, stt_prev2)
            ld.dma(s.dma_start(out=xres[:, slot, :], in_=x_hbm[:, oc, t0:t0 + TT]))

        x_ld = ld.n
        os_prev2 = max(0, os_.n - 16)

        @block.vector
        def _(v):
            v.wait_ge(ts.h, mm_done)
            v.wait_ge(ld.h, x_ld)
            v.wait_ge(os_.h, os_prev2)
            inst = v.tensor_add(y_scr, ps_s[:, :], xres[:, slot, :])
            vs.inc(inst)

        y_done = vs.n

        @block.sync
        def _(s):
            s.wait_ge(vs.h, y_done)
            os_.dma(s.dma_start(out=y_hbm[:, oc, t0:t0 + TT], in_=y_scr))

    emit_proj_fm(p, av, wb, ps_s, out_epilogue, pre_wait_v=attn_done,
                 pre_wait_ld=wo_loaded, ld_sem=lg)

    @block.sync
    def _(s):
        s.wait_ge(os_.h, os_.n)

    p.close()
    return p


def host_consts_l2():
    kr = np.arange(128)[:, None]
    qc = np.arange(TT)[None, :]
    masks = np.zeros((128, 5, TT), np.float32)
    for j in range(4):
        sb = ((2 * j + kr // 64) == (qc // 64)) & (128 * j + kr <= qc)
        masks[:, j, :] = sb.astype(np.float32)
    glob = (kr < 16) & (kr <= qc)
    masks[:, 4, :] = np.maximum(masks[:, 0, :], glob.astype(np.float32))
    mask16 = (np.arange(128) < 16).astype(np.float32)[:, None]
    return {"l2_masks": masks, "mask16": np.ascontiguousarray(mask16)}


def host_inputs_l2_attn(x_row, w):
    c = host_consts_l2()
    return {
        "x": fm_x(x_row),
        "wqT": fm_w(np.ascontiguousarray(w["Wq"].T)),
        "wkT": fm_w(np.ascontiguousarray(w["Wk"].T)),
        "wvT": fm_w(np.ascontiguousarray(w["Wv"].T)),
        "woT": fm_w(np.ascontiguousarray(w["Wo"].T)),
        "l2_masks": c["l2_masks"], "mask16_in": c["mask16"],
    }




def build_l3_attn(shared=None):
    p = Prog(shared)
    x_hbm = p.dram_in("x", [128, KC, T])
    w_hbm = {n: p.dram_in(n, [128, KC, D]) for n in ["wqT", "wkT", "wvT", "woT"]}
    bo_h = p.dram_in("bo", [128, KC])
    memx_h = p.dram_in("memx_hb", [128, 4, 2, 384])
    betas_h = p.dram_in("betas", [1, 4])
    strip_h = p.dram_in("causal_strip", [128, 896])
    y_hbm = p.dram_out("y", [128, KC, T])

    xn = p.sbuf("xn", [128, KC, T], BF16)
    qf = p.sbuf("qf", [128, KC, T], BF16)
    kf = p.sbuf("kf", [128, KC, T], BF16)
    v_tm = p.sbuf("v_tm", [128, T // 128, D], BF16)
    wa = p.sbuf("wa", [128, KC, D], BF16)
    wb = p.sbuf("wb", [128, KC, D], BF16)
    x_t = p.sbuf("x_t", [128, KC, TT], F32)
    bo_sb = p.sbuf("bo_sb", [128, KC], F32)
    memx = p.sbuf("memx", [128, 4, 2, 384], BF16)
    strip = p.sbuf("strip", [128, 896], BF16)
    g_bc = p.sbuf("g_bc", [128, 4], F32)
    ones_bf = p.sbuf("ones_bf", [128, 1], BF16)
    shift_sb = p.sbuf("shift_sb", [128, 1], F32)
    onesrow_f32 = p.sbuf("onesrow_f32", [1, 128], F32)
    p.bufs["xbf_scr"] = qf[:, :, 0:TT]
    p.bufs["sq_scr"] = kf[:, :, 0:TT]
    scr_f32 = p.sbuf("scr_f32", [128, 2 * TT], F32)
    scr2_f32 = p.sbuf("scr2_f32", [128, 2 * TT], F32)
    row_scr = p.sbuf("row_scr", [1, 4 * TT], F32)
    sq_t = p.sbuf("sq_t", [128, 2, TT], BF16)
    pt = [p.sbuf("pt0", [128, TT], BF16), p.sbuf("pt1", [128, TT], BF16)]
    betarow = p.sbuf("betarow", [1, 4], F32)
    xres = x_t[:, 0:2, :]

    ps_s = p.psum("ps_s", [128, TT], F32)
    ps_av0 = p.psum("ps_av0", [128, TT], F32)
    ps_av1 = p.psum("ps_av1", [128, TT], F32)
    ps_d = p.psum("ps_d", [1, TT], F32)
    ps_a = p.psum("ps_a", [128, TT], F32)
    ps_b = p.psum("ps_b", [128, TT], F32)
    ps_c = p.psum("ps_c", [33, TT], F32)
    ps_r = p.psum("ps_r", [128, TT], F32)

    p.start_block()
    block = p.block
    ld, ts, vs, ss, os_ = p.sem("ld"), p.sem("ts"), p.sem("vs"), p.sem("ss"), p.sem("os")
    lg = p.sem("lg")

    @block.sync
    def _(s):
        ld.dma(s.dma_start(out=bo_sb[:, :], in_=bo_h))
        ld.dma(s.dma_start(out=betarow[:, :], in_=betas_h))

    @block.gpsimd
    def _(g):
        lg.dma(g.dma_start(out=wa[:, :, :], in_=w_hbm["wqT"]))
        lg.dma(g.dma_start(out=wb[:, :, :], in_=w_hbm["wkT"]))
        lg.dma(g.dma_start(out=memx[:, :, :, :], in_=memx_h))
        lg.dma(g.dma_start(out=strip[:, :], in_=strip_h))

    all_loaded = ld.n
    all_loaded_g = lg.n

    @block.vector
    def _(v):
        v.wait_ge(ld.h, all_loaded)
        v.wait_ge(lg.h, all_loaded_g)
        v.memset(ones_bf[:, :], 1.0)
        v.memset(shift_sb[:, :], -20.0)
        inst = v.memset(onesrow_f32[:, :], 1.0)
        vs.inc(inst)

    init_done = vs.n

    @block.scalar
    def _(s):
        s.wait_ge(ld.h, all_loaded)
        s.wait_ge(vs.h, init_done)
        inst = s.activation(betarow[:, :], betarow[:, :], AF.Sigmoid)
        ss.inc(inst)

    sig_done = ss.n

    @block.tensor
    def _(t):
        t.wait_ge(ss.h, sig_done)
        inst = t.matmul(ps_r[:, 0:4], onesrow_f32[0:1, :], betarow[0:1, :],
                        start=True, stop=True)
        ts.inc(inst)

    gb_mm = ts.n

    @block.vector
    def _(v):
        v.wait_ge(ts.h, gb_mm)
        inst = v.tensor_copy(g_bc[:, :], ps_r[:, 0:4])
        vs.inc(inst)

    @block.scalar
    def _(s):
        s.wait_ge(ld.h, all_loaded)

    emit_ln_stream(p, x_hbm, xn)
    ln_done = vs.n

    def cast_epilogue(out_sb):
        def epi(oc, tt, mm_done):
            t0 = tt * TT

            @block.vector
            def _(v):
                v.wait_ge(ts.h, mm_done)
                inst = v.tensor_copy(out_sb[:, oc, t0:t0 + TT], ps_s[:, :])
                vs.inc(inst)
        return epi

    emit_proj_fm(p, xn, wa, ps_s, cast_epilogue(qf), pre_wait_v=ln_done)
    q_mm_done = ts.n

    @block.gpsimd
    def _(g):
        g.wait_ge(ts.h, q_mm_done)
        lg.dma(g.dma_start(out=wa[:, :, :], in_=w_hbm["wvT"]))

    wv_loaded = lg.n

    emit_proj_fm(p, xn, wb, ps_s, cast_epilogue(kf), pre_wait_v=ln_done)
    k_mm_done = ts.n

    @block.gpsimd
    def _(g):
        g.wait_ge(ts.h, k_mm_done)
        lg.dma(g.dma_start(out=wb[:, :, :], in_=w_hbm["woT"]))

    wo_loaded = lg.n

    for t128 in range(T // 128):
        evac_prev = vs.n

        @block.tensor
        def _(t):
            t.wait_ge(vs.h, max(evac_prev, ln_done))
            t.wait_ge(lg.h, wv_loaded)
            for kc in range(KC):
                t.matmul(ps_av0[:, :], xn[:, kc, t128 * 128:(t128 + 1) * 128],
                         wa[:, kc, 0:512], start=(kc == 0), stop=(kc == KC - 1))
            for kc in range(KC):
                inst = t.matmul(ps_av1[:, :], xn[:, kc, t128 * 128:(t128 + 1) * 128],
                                wa[:, kc, 512:1024], start=(kc == 0), stop=(kc == KC - 1))
            ts.inc(inst)

        vmm = ts.n

        @block.vector
        def _(v):
            v.wait_ge(ts.h, vmm)
            v.tensor_copy(v_tm[:, t128, 0:512], ps_av0[:, :])
            inst = v.tensor_copy(v_tm[:, t128, 512:1024], ps_av1[:, :])
            vs.inc(inst)

    v_done = vs.n
    av = xn

    for h in range(4):
        for qt in range(NTT):
            q0 = qt * TT
            kts = list(range(0, min(16, qt * 4 + 4)))
            av_evac_prev = vs.n
            for i, kt in enumerate(kts):
                k0 = kt * 128
                dd = q0 - k0
                partial = dd < 127
                slot = pt[i % 2]
                exp_prev = ss.n

                @block.tensor
                def _(t):
                    t.wait_ge(ss.h, exp_prev)
                    if i == 0:
                        t.wait_ge(vs.h, max(v_done, av_evac_prev))
                    for j in range(2):
                        inst = t.matmul(ps_s[:, :],
                                        kf[:, 2 * h + j, k0:k0 + 128],
                                        qf[:, 2 * h + j, q0:q0 + TT],
                                        start=(j == 0), stop=(j == 1))
                    ts.inc(inst)

                sc_done = ts.n
                mask_prev = vs.n

                @block.scalar
                def _(s):
                    s.wait_ge(ts.h, sc_done)
                    if partial:
                        s.wait_ge(vs.h, mask_prev)
                    inst = s.activation(slot[:, :], ps_s[:, :], AF.Exp,
                                        bias=shift_sb[:, 0:1], scale=1.0)
                    ss.inc(inst)

                e_done = ss.n
                if partial:
                    @block.vector
                    def _(v):
                        v.wait_ge(ss.h, e_done)
                        inst = v.tensor_tensor(slot[:, :], slot[:, :],
                                               strip[:, 384 + dd:384 + dd + TT], ALU.mult)
                        vs.inc(inst)
                    m_done = vs.n
                else:
                    m_done = None

                @block.tensor
                def _(t):
                    if m_done is None:
                        t.wait_ge(ss.h, e_done)
                    else:
                        t.wait_ge(vs.h, m_done)
                    t.matmul(ps_av0[:, :], v_tm[:, kt, h * 256:h * 256 + 128],
                             slot[:, :], start=(i == 0), stop=(i == len(kts) - 1))
                    t.matmul(ps_av1[:, :], v_tm[:, kt, h * 256 + 128:h * 256 + 256],
                             slot[:, :], start=(i == 0), stop=(i == len(kts) - 1))
                    inst = t.matmul(ps_d[:, :], ones_bf[:, 0:1],
                                    slot[:, :], start=(i == 0), stop=(i == len(kts) - 1))
                    ts.inc(inst)

            av_done = ts.n
            r_row = row_scr[0:1, 0:TT]

            @block.vector
            def _(v):
                v.wait_ge(ts.h, av_done)
                inst = v.reciprocal(r_row, ps_d[0:1, :])
                vs.inc(inst)
                v.wait_ge(vs.h, vs.n)

            r_done = vs.n

            @block.tensor
            def _(t):
                t.wait_ge(vs.h, r_done)
                inst = t.matmul(ps_r[:, :], onesrow_f32[0:1, :], r_row,
                                start=True, stop=True)
                ts.inc(inst)

            rb_done = ts.n

            @block.vector
            def _(v):
                v.wait_ge(ts.h, rb_done)
                avs = scr_f32[:, 0:TT]
                v.tensor_copy(avs, ps_av0[:, :])
                v.tensor_tensor(av[:, 2 * h, q0:q0 + TT], avs, ps_r[:, :], ALU.mult)
                v.tensor_copy(avs, ps_av1[:, :])
                inst = v.tensor_tensor(av[:, 2 * h + 1, q0:q0 + TT], avs, ps_r[:, :],
                                       ALU.mult)
                vs.inc(inst)

    attn_done = vs.n

    for h in range(4):
        for tt in range(NTT):
            t0 = tt * TT
            sq_prev = ts.n

            @block.vector
            def _(v):
                v.wait_ge(ts.h, sq_prev)
                v.wait_ge(vs.h, attn_done)
                v.tensor_scalar_min(scr_f32[:, 0:TT],
                                    qf[:, 2 * h, t0:t0 + TT], 0.0)
                inst = v.tensor_scalar_min(scr_f32[:, TT:2 * TT],
                                           qf[:, 2 * h + 1, t0:t0 + TT], 0.0)
                vs.inc(inst)

            min_done = vs.n

            @block.scalar
            def _(s):
                s.wait_ge(vs.h, min_done)
                s.activation(scr2_f32[:, 0:TT], scr_f32[:, 0:TT], AF.Exp)
                inst = s.activation(scr2_f32[:, TT:2 * TT], scr_f32[:, TT:2 * TT], AF.Exp)
                ss.inc(inst)

            exp_done = ss.n

            @block.vector
            def _(v):
                v.wait_ge(ss.h, exp_done)
                for j in range(2):
                    inst = v.scalar_tensor_tensor(sq_t[:, j, :],
                                                  qf[:, 2 * h + j, t0:t0 + TT], 0.0,
                                                  scr2_f32[:, j * TT:(j + 1) * TT],
                                                  ALU.max, ALU.add)
                vs.inc(inst)

            sq_done = vs.n

            @block.tensor
            def _(t):
                t.wait_ge(vs.h, sq_done)
                for ec, psm in [(0, ps_av0), (1, ps_av1), (2, ps_s)]:
                    for j in range(2):
                        inst = t.matmul(psm[:, :], memx[:, h, j, ec * 128:(ec + 1) * 128],
                                        sq_t[:, j, :], start=(j == 0), stop=(j == 1))
                ts.inc(inst)

            am_done = ts.n
            rz_row = row_scr[0:1, TT:2 * TT]

            @block.vector
            def _(v):
                v.wait_ge(ts.h, am_done)
                v.tensor_scalar_add(rz_row, ps_s[0:1, :], 1e-6)
                inst = v.reciprocal(rz_row, rz_row)
                vs.inc(inst)
                v.wait_ge(vs.h, vs.n)

            rz_done = vs.n

            @block.tensor
            def _(t):
                t.wait_ge(vs.h, rz_done)
                inst = t.matmul(ps_r[:, :], onesrow_f32[0:1, :], rz_row,
                                start=True, stop=True)
                ts.inc(inst)

            rzb_done = ts.n

            @block.vector
            def _(v):
                v.wait_ge(ts.h, rzb_done)
                for j, psm in [(0, ps_av0), (1, ps_av1)]:
                    amem = scr_f32[:, 0:TT]
                    dsc = scr2_f32[:, 0:TT]
                    v.tensor_copy(amem, psm[:, :])
                    v.tensor_tensor(amem, amem, ps_r[:, :], ALU.mult)
                    v.tensor_sub(dsc, amem, av[:, 2 * h + j, t0:t0 + TT])
                    inst = v.scalar_tensor_tensor(av[:, 2 * h + j, t0:t0 + TT],
                                                  dsc, g_bc[:, h:h + 1],
                                                  av[:, 2 * h + j, t0:t0 + TT],
                                                  ALU.mult, ALU.add)
                vs.inc(inst)

    comb_done = vs.n

    def out_epilogue(oc, tt, mm_done):
        t0 = tt * TT
        slot = (oc * NTT + tt) % 2
        y_scr = scr_f32[:, slot * TT:(slot + 1) * TT]
        stt_prev2 = max(0, vs.n - 1)

        @block.sync
        def _(s):
            s.wait_ge(vs.h, stt_prev2)
            ld.dma(s.dma_start(out=xres[:, slot, :], in_=x_hbm[:, oc, t0:t0 + TT]))

        x_ld = ld.n
        os_prev2 = max(0, os_.n - 16)

        @block.vector
        def _(v):
            v.wait_ge(ts.h, mm_done)
            v.wait_ge(ld.h, x_ld)
            v.wait_ge(os_.h, os_prev2)
            inst = v.scalar_tensor_tensor(y_scr, ps_s[:, :], bo_sb[:, oc:oc + 1],
                                          xres[:, slot, :], ALU.add, ALU.add)
            vs.inc(inst)

        y_done = vs.n

        @block.sync
        def _(s):
            s.wait_ge(vs.h, y_done)
            os_.dma(s.dma_start(out=y_hbm[:, oc, t0:t0 + TT], in_=y_scr))

    emit_proj_fm(p, av, wb, ps_s, out_epilogue, pre_wait_v=comb_done,
                 pre_wait_ld=wo_loaded, ld_sem=lg)

    @block.sync
    def _(s):
        s.wait_ge(os_.h, os_.n)

    p.close()
    return p


def host_inputs_l3_attn(x_row, w):
    cc = host_consts_common()
    mem = np.asarray(w["mem"])
    z = np.asarray(w["z"])
    memx = np.zeros((4, 256, 384), np.float32)
    memx[:, :, 0:256] = mem
    memx[:, :, 256:257] = z
    memx_l = memx.reshape(4, 2, 128, 384).transpose(2, 0, 1, 3)
    return {
        "x": fm_x(x_row),
        "wqT": fm_w(np.ascontiguousarray(w["Wq"].T)),
        "wkT": fm_w(np.ascontiguousarray(w["Wk"].T)),
        "wvT": fm_w(np.ascontiguousarray(w["Wv"].T)),
        "woT": fm_w(np.ascontiguousarray(w["Wo"].T)),
        "bo": fm_vec(w["bo"]),
        "memx_hb": np.ascontiguousarray(memx_l),
        "betas": np.ascontiguousarray(np.asarray(w["betas"]).reshape(1, 4)),
        "causal_strip": cc["causal_strip"],
    }




def emit_ln_sbuf(p, src_of, out_fn, sq_scr, ps_a, ps_b, ps_c, tts=None, pre_wait_v=0):
    block = p.block
    ts, vs, ss = p.sem("ts"), p.sem("vs"), p.sem("ss")
    ones_bf = p.bufs["ones_bf"]
    onesrow_f32 = p.bufs["onesrow_f32"]
    row = p.bufs["row_scr"]
    for tt in (tts if tts is not None else range(NTT)):
        stats_done_prev = ts.n

        @block.scalar
        def _(s):
            s.wait_ge(ts.h, stats_done_prev)
            s.wait_ge(vs.h, pre_wait_v)
            for kc in range(KC):
                inst = s.activation(sq_scr[:, kc, :], src_of(kc, tt), AF.Square)
            ss.inc(inst)

        s_ready = ss.n
        src_ready = vs.n

        @block.tensor
        def _(t):
            t.wait_ge(ss.h, s_ready)
            t.wait_ge(vs.h, max(src_ready, pre_wait_v))
            for kc in range(KC):
                t.matmul(ps_c[0:1, :], ones_bf[:, 0:1], src_of(kc, tt),
                         start=(kc == 0), stop=(kc == KC - 1))
            for kc in range(KC):
                inst = t.matmul(ps_c[32:33, :], ones_bf[:, 0:1], sq_scr[:, kc, :],
                                start=(kc == 0), stop=(kc == KC - 1))
            ts.inc(inst)

        t_ready = ts.n
        mu = row[0:1, 0:TT]
        var = row[0:1, TT:2 * TT]
        rr = row[0:1, 2 * TT:3 * TT]
        mur = row[0:1, 3 * TT:4 * TT]

        @block.vector
        def _(v):
            v.wait_ge(ts.h, t_ready)
            v.tensor_scalar_mul(mu, ps_c[0:1, :], 1.0 / D)
            v.tensor_scalar_mul(var, ps_c[32:33, :], 1.0 / D)
            v.tensor_tensor(rr, mu, mu, ALU.mult)
            v.tensor_tensor(var, var, rr, ALU.subtract)
            inst = v.tensor_scalar_add(var, var, LN_EPS)
            vs.inc(inst)

        vv = vs.n

        @block.scalar
        def _(s):
            s.wait_ge(vs.h, vv)
            inst = s.activation(rr, var, AF.Sqrt)
            ss.inc(inst)

        sv = ss.n

        @block.vector
        def _(v):
            v.wait_ge(ss.h, sv)
            inst = v.reciprocal(rr, rr)
            vs.inc(inst)
            v.wait_ge(vs.h, vs.n)
            inst = v.tensor_tensor(mur, mu, rr, ALU.mult)
            vs.inc(inst)

        vv2 = vs.n

        @block.tensor
        def _(t):
            t.wait_ge(vs.h, vv2)
            t.matmul(ps_a[:, :], onesrow_f32[0:1, :], rr, start=True, stop=True)
            inst = t.matmul(ps_b[:, :], onesrow_f32[0:1, :], mur, start=True, stop=True)
            ts.inc(inst)

        tv2 = ts.n

        @block.vector
        def _(v):
            v.wait_ge(ts.h, tv2)
            for kc in range(KC):
                out_fn(v, kc, tt, ps_a, ps_b)



def _l0_bail(p, y_hbm, buf):
    block = p.block
    os_ = p.sem("os")
    vs = p.sem("vs")

    @block.sync
    def _(s):
        s.wait_ge(vs.h, vs.n)
        for tt in range(NTT):
            os_.dma(s.dma_start(out=y_hbm[:, :, tt * TT:(tt + 1) * TT], in_=buf))
        s.wait_ge(os_.h, os_.n)


def build_l0_attn(shared=None):
    import os as _os
    STOP = _os.environ.get("L0_STOP", "")
    NORM = 64.0 ** (-0.25)
    EPS_K = 1e-4
    p = Prog(shared)
    x_hbm = p.dram_in("x", [128, KC, T])
    w_hbm = {n: p.dram_in(n, [128, KC, D]) for n in ["wqT", "wkT", "wvT", "woT"]}
    b_hbm = {n: p.dram_in(n, [128, KC]) for n in ["bq", "bk", "bv", "bo",
                                                  "g2", "be2", "g3", "be3", "b2v"]}
    b1_hbm = p.dram_in("b1v", [128, 32])
    bvrow_hbm = p.dram_in("bv_row", [1, D])
    projt2_h = p.dram_in("projT2", [128, 256])
    ident_h = p.dram_in("ident", [128, 128])
    w1T_hbm = p.dram_in("w1T", [128, KC, 4096])
    w2T_hbm = p.dram_in("w2T", [128, 32, D])
    y_hbm = p.dram_out("y", [128, KC, T])

    xn = p.sbuf("xn", [128, KC, T], BF16)
    qf = p.sbuf("qf", [128, KC, T], BF16)
    kf = p.sbuf("kf", [128, KC, T], BF16)
    v_tm = p.sbuf("v_tm", [128, T // 128, D], BF16)
    wa = p.sbuf("wa", [128, KC, D], BF16)
    wb = p.sbuf("wb", [128, KC, D], BF16)
    x_t = p.sbuf("x_t", [128, KC, TT], F32)
    b_sb = {n: p.sbuf(n + "_sb", [128, KC], F32) for n in
            ["bq", "bk", "bv", "bo", "g2", "be2", "g3", "be3", "b2v"]}
    b1_sb = p.sbuf("b1_sb", [128, 32], F32)
    projt2 = p.sbuf("projt2", [128, 256], BF16)
    ident_bf = p.sbuf("ident_bf", [128, 128], BF16)
    ones_bf = p.sbuf("ones_bf", [128, 1], BF16)
    onesrow_f32 = p.sbuf("onesrow_f32", [1, 128], F32)
    bvb = p.sbuf("bvb", [128, D], F32)
    scr_f32 = p.sbuf("scr_f32", [128, 2 * TT], F32)
    row_scr = p.sbuf("row_scr", [1, 4 * TT], F32)
    q2_scr = p.sbuf("q2_scr", [128, T], BF16)
    diag_sc = p.sbuf("diag_sc", [128, 20], F32)
    ksum_bf = p.sbuf("ksum_bf", [128, 2], BF16)
    ctx_sb = p.sbuf("ctx_sb", [128, 2, 64], BF16)
    veps_sb = p.sbuf("veps_sb", [128, 64], F32)
    kmax_sc = p.sbuf("kmax_sc", [128, 18], F32)
    ff_acc = xn.ap().bitcast(F32)[:, :, 0:TT]
    p.bufs["xbf_scr"] = qf[:, :, 0:TT]
    p.bufs["sq_scr"] = kf[:, :, 0:TT]
    xres2 = x_t[:, 0:2, :]

    ps0 = p.psum("ps0", [128, TT], F32)
    ps1 = p.psum("ps1", [128, TT], F32)
    ps_c = p.psum("ps_c", [33, TT], F32)
    ps3 = p.psum("ps3", [1, TT], F32)
    ps4 = p.psum("ps4", [128, 1024], BF16)
    ps5 = p.psum("ps5", [128, TT], F32)
    ps6 = p.psum("ps6", [128, TT], F32)
    ps7 = p.psum("ps7", [128, TT], F32)
    p.bufs["ps_a"] = ps0
    p.bufs["ps_b"] = ps1

    p.start_block()
    block = p.block
    ld, ts, vs, ss, os_ = p.sem("ld"), p.sem("ts"), p.sem("vs"), p.sem("ss"), p.sem("os")
    lg = p.sem("lg")
    bvrow = row_scr[0:1, 0:D]

    @block.sync
    def _(s):
        for n in b_sb:
            ld.dma(s.dma_start(out=b_sb[n][:, :], in_=b_hbm[n]))
        ld.dma(s.dma_start(out=b1_sb[:, :], in_=b1_hbm))
        ld.dma(s.dma_start(out=bvrow, in_=bvrow_hbm))

    @block.gpsimd
    def _(g):
        lg.dma(g.dma_start(out=wa[:, :, :], in_=w_hbm["wqT"]))
        lg.dma(g.dma_start(out=wb[:, :, :], in_=w_hbm["wkT"]))
        lg.dma(g.dma_start(out=projt2[:, :], in_=projt2_h))
        lg.dma(g.dma_start(out=ident_bf[:, :], in_=ident_h))

    all_loaded = ld.n
    all_loaded_g = lg.n

    @block.vector
    def _(v):
        v.wait_ge(ld.h, all_loaded)
        v.wait_ge(lg.h, all_loaded_g)
        v.memset(ones_bf[:, :], 1.0)
        inst = v.memset(onesrow_f32[:, :], 1.0)
        vs.inc(inst)

    init_done = vs.n

    @block.tensor
    def _(t):
        t.wait_ge(ld.h, all_loaded)
        t.wait_ge(vs.h, init_done)
        t.matmul(ps5[:, :], onesrow_f32[0:1, :], bvrow[0:1, 0:512], start=True, stop=True)
        inst = t.matmul(ps6[:, :], onesrow_f32[0:1, :], bvrow[0:1, 512:1024],
                        start=True, stop=True)
        ts.inc(inst)

    bvb_mm = ts.n

    @block.vector
    def _(v):
        v.wait_ge(ts.h, bvb_mm)
        v.tensor_copy(bvb[:, 0:512], ps5[:, :])
        inst = v.tensor_copy(bvb[:, 512:1024], ps6[:, :])
        vs.inc(inst)

    @block.scalar
    def _(s):
        s.wait_ge(ld.h, all_loaded)

    emit_ln_stream(p, x_hbm, xn)
    ln_done = vs.n
    if STOP == "ln":
        _l0_bail(p, y_hbm, ff_acc)
        p.close()
        return p

    def bias_epilogue(bias_sb, out_sb):
        def epi(oc, tt, mm_done):
            t0 = tt * TT

            @block.vector
            def _(v):
                v.wait_ge(ts.h, mm_done)
                inst = v.tensor_scalar(out_sb[:, oc, t0:t0 + TT], ps0[:, :],
                                       bias_sb[:, oc:oc + 1], None, ALU.add)
                vs.inc(inst)
        return epi

    emit_proj_fm(p, xn, wa, ps0, bias_epilogue(b_sb["bq"], qf), pre_wait_v=ln_done)
    q_mm_done = ts.n
    if STOP == "q":
        _l0_bail(p, y_hbm, ff_acc)
        p.close()
        return p

    @block.gpsimd
    def _(g):
        g.wait_ge(ts.h, q_mm_done)
        lg.dma(g.dma_start(out=wa[:, :, :], in_=w_hbm["wvT"]))

    wv_loaded = lg.n

    emit_proj_fm(p, xn, wb, ps0, bias_epilogue(b_sb["bk"], kf), pre_wait_v=ln_done)
    k_mm_done = ts.n

    @block.gpsimd
    def _(g):
        g.wait_ge(ts.h, k_mm_done)
        lg.dma(g.dma_start(out=wb[:, :, :], in_=w_hbm["woT"]))

    wo_loaded = lg.n

    for t128 in range(T // 128):
        evac_prev = vs.n

        @block.tensor
        def _(t):
            t.wait_ge(vs.h, max(evac_prev, ln_done))
            t.wait_ge(lg.h, wv_loaded)
            for kc in range(KC):
                t.matmul(ps5[:, :], xn[:, kc, t128 * 128:(t128 + 1) * 128],
                         wa[:, kc, 0:512], start=(kc == 0), stop=(kc == KC - 1))
            for kc in range(KC):
                inst = t.matmul(ps6[:, :], xn[:, kc, t128 * 128:(t128 + 1) * 128],
                                wa[:, kc, 512:1024], start=(kc == 0), stop=(kc == KC - 1))
            ts.inc(inst)

        vmm = ts.n

        @block.vector
        def _(v):
            v.wait_ge(ts.h, vmm)
            v.tensor_add(v_tm[:, t128, 0:512], ps5[:, :], bvb[:, 0:512])
            inst = v.tensor_add(v_tm[:, t128, 512:1024], ps6[:, :], bvb[:, 512:1024])
            vs.inc(inst)

    v_done = vs.n
    if STOP == "v":
        _l0_bail(p, y_hbm, ff_acc)
        p.close()
        return p

    wa_flat = wa.ap().rearrange("p c d -> p (c d)")
    kp_tm = wa_flat[:, 0:4096].rearrange("p (n m) -> p n m", m=256)
    qpT_buf = wa_flat[:, 4096:8192].rearrange("p (m t) -> p m t", m=2)
    xpk_store = x_t.ap().rearrange("p c t -> p (c t)").rearrange(
        "p (n m) -> p n m", m=256)

    class _StopEmit(Exception):
        pass

    NHEADS = 1 if STOP.startswith("h1") else 16
    try:
        for h in range(NHEADS):
            kc_h = h // 2
            prow = (h % 2) * 64

            for nt in range(16):
                n0 = nt * 128
                prev_use = vs.n

                @block.tensor
                def _(t):
                    t.wait_ge(vs.h, max(v_done, prev_use))
                    inst = t.matmul(ps0[:, 0:256], kf[prow:prow + 64, kc_h, n0:n0 + 128],
                                    projt2[prow:prow + 64, :], start=True, stop=True)
                    ts.inc(inst)

                mmk = ts.n

                @block.vector
                def _(v):
                    v.wait_ge(ts.h, mmk)
                    v.tensor_copy(xpk_store[:, nt, :], ps0[:, 0:256])
                    inst = v.reduce_max(kmax_sc[:, nt:nt + 1], ps0[:, 0:256], axis=AX.X)
                    vs.inc(inst)
                    v.wait_ge(vs.h, vs.n)

            if STOP == "h1kx":
                raise _StopEmit
            @block.vector
            def _(v):
                inst = v.reduce_max(kmax_sc[:, 16:17], kmax_sc[:, 0:16], axis=AX.X)
                vs.inc(inst)
                v.wait_ge(vs.h, vs.n)
                inst = v.tensor_copy(q2_scr[:, 0:1], kmax_sc[:, 16:17])
                vs.inc(inst)

            gk1 = vs.n
            tp_prev = vs.n

            @block.tensor
            def _(t):
                t.wait_ge(vs.h, gk1)
                inst = t.transpose(ps4[0:1, 0:128], q2_scr[:, 0:1], ident_bf[:, :])
                ts.inc(inst)

            gk2 = ts.n

            @block.vector
            def _(v):
                v.wait_ge(ts.h, gk2)
                inst = v.reduce_max(row_scr[0:1, 0:1], ps4[0:1, 0:128], axis=AX.X)
                vs.inc(inst)
                v.wait_ge(vs.h, vs.n)

            gk3 = vs.n

            @block.tensor
            def _(t):
                t.wait_ge(vs.h, gk3)
                inst = t.matmul(ps7[:, 0:1], onesrow_f32[0:1, :], row_scr[0:1, 0:1],
                                start=True, stop=True)
                ts.inc(inst)

            gk4 = ts.n

            @block.vector
            def _(v):
                v.wait_ge(ts.h, gk4)
                inst = v.tensor_scalar_mul(diag_sc[:, 16:17], ps7[:, 0:1], -NORM)
                vs.inc(inst)

            sgneg_done = vs.n
            if STOP == "h1gmax":
                raise _StopEmit

            @block.scalar
            def _(s):
                s.wait_ge(vs.h, sgneg_done)
                inst = s.activation(q2_scr[prow:prow + 64, :], kf[prow:prow + 64, kc_h, :],
                                    AF.Square)
                ss.inc(inst)

            k2_done = ss.n
            for ti in range(NTT):
                ev_prev = vs.n

                @block.tensor
                def _(t):
                    if ti == 0:
                        t.wait_ge(ss.h, k2_done)
                    t.wait_ge(vs.h, ev_prev)
                    inst = t.matmul(ps3[0:1, :], ones_bf[prow:prow + 64, 0:1],
                                    q2_scr[prow:prow + 64, ti * TT:(ti + 1) * TT],
                                    start=True, stop=True)
                    ts.inc(inst)

                tdone = ts.n

                @block.vector
                def _(v):
                    v.wait_ge(ts.h, tdone)
                    inst = v.tensor_scalar_mul(row_scr[0:1, ti * TT:(ti + 1) * TT],
                                               ps3[0:1, :], 0.5 * NORM * NORM)
                    vs.inc(inst)

            diag_rows = vs.n

            @block.tensor
            def _(t):
                t.wait_ge(vs.h, diag_rows)
                for nt in range(16):
                    inst = t.matmul(ps7[:, nt:nt + 1],
                                    row_scr[0:1, nt * 128:(nt + 1) * 128],
                                    onesrow_f32[0:1, 0:1], start=True, stop=True)
                ts.inc(inst)

            diag_cols = ts.n

            @block.vector
            def _(v):
                v.wait_ge(ts.h, diag_cols)
                for nt in range(16):
                    inst = v.scalar_tensor_tensor(diag_sc[:, nt:nt + 1], ps7[:, nt:nt + 1],
                                                  -1.0, diag_sc[:, 16:17],
                                                  ALU.mult, ALU.add)
                vs.inc(inst)

            biask_done = vs.n
            if STOP == "h1diagk":
                raise _StopEmit

            for nt in range(16):
                @block.scalar
                def _(s):
                    if nt == 0:
                        s.wait_ge(vs.h, biask_done)
                    inst = s.activation(kp_tm[:, nt, :], xpk_store[:, nt, :], AF.Exp,
                                        bias=diag_sc[:, nt:nt + 1], scale=NORM)
                    ss.inc(inst)

            kp_done = ss.n
            if STOP == "h1kp":
                raise _StopEmit

            @block.tensor
            def _(t):
                t.wait_ge(ss.h, kp_done)
                for mc in range(2):
                    for nt in range(16):
                        t.matmul(ps7[:, 4 + mc:5 + mc],
                                 kp_tm[:, nt, mc * 128:(mc + 1) * 128],
                                 ones_bf[:, 0:1], start=(nt == 0), stop=(nt == 15))
                for mc in range(2):
                    for nt in range(16):
                        t.matmul(ps1[:, mc * 64:(mc + 1) * 64],
                                 kp_tm[:, nt, mc * 128:(mc + 1) * 128],
                                 v_tm[:, nt, h * 64:h * 64 + 64],
                                 start=(nt == 0), stop=(nt == 15))
                for nt in range(16):
                    inst = t.matmul(ps3[0:1, 0:64], ones_bf[:, 0:1],
                                    v_tm[:, nt, h * 64:h * 64 + 64],
                                    start=(nt == 0), stop=(nt == 15))
                ts.inc(inst)

            kc_mm = ts.n

            @block.vector
            def _(v):
                v.wait_ge(ts.h, kc_mm)
                inst = v.tensor_scalar_add(ksum_bf[:, 0:2], ps7[:, 4:6], T * EPS_K)
                vs.inc(inst)
                inst = v.tensor_scalar_mul(row_scr[0:1, 0:64], ps3[0:1, 0:64], EPS_K)
                vs.inc(inst)

            veps1 = vs.n

            @block.tensor
            def _(t):
                t.wait_ge(vs.h, veps1)
                inst = t.matmul(ps6[:, 0:64], onesrow_f32[0:1, :], row_scr[0:1, 0:64],
                                start=True, stop=True)
                ts.inc(inst)

            veps2 = ts.n

            @block.vector
            def _(v):
                v.wait_ge(ts.h, veps2)
                v.tensor_copy(veps_sb[:, :], ps6[:, 0:64])
                v.tensor_add(ctx_sb[:, 0, :], ps1[:, 0:64], veps_sb[:, :])
                inst = v.tensor_add(ctx_sb[:, 1, :], ps1[:, 64:128], veps_sb[:, :])
                vs.inc(inst)

            ctx_done = vs.n
            if STOP == "h1ctx":
                raise _StopEmit

            @block.scalar
            def _(s):
                s.wait_ge(vs.h, ctx_done)
                inst = s.activation(q2_scr[prow:prow + 64, :], qf[prow:prow + 64, kc_h, :],
                                    AF.Square)
                ss.inc(inst)

            q2_done = ss.n
            for ti in range(NTT):
                ev_prev = vs.n

                @block.tensor
                def _(t):
                    if ti == 0:
                        t.wait_ge(ss.h, q2_done)
                    t.wait_ge(vs.h, ev_prev)
                    inst = t.matmul(ps3[0:1, :], ones_bf[prow:prow + 64, 0:1],
                                    q2_scr[prow:prow + 64, ti * TT:(ti + 1) * TT],
                                    start=True, stop=True)
                    ts.inc(inst)

                tdone = ts.n

                @block.vector
                def _(v):
                    v.wait_ge(ts.h, tdone)
                    inst = v.tensor_scalar_mul(row_scr[0:1, ti * TT:(ti + 1) * TT],
                                               ps3[0:1, :], 0.5 * NORM * NORM)
                    vs.inc(inst)

            dq_rows = vs.n

            @block.tensor
            def _(t):
                t.wait_ge(vs.h, dq_rows)
                for nt in range(16):
                    inst = t.matmul(ps7[:, nt:nt + 1],
                                    row_scr[0:1, nt * 128:(nt + 1) * 128],
                                    onesrow_f32[0:1, 0:1], start=True, stop=True)
                ts.inc(inst)

            dq_cols = ts.n

            @block.vector
            def _(v):
                v.wait_ge(ts.h, dq_cols)
                for nt in range(16):
                    inst = v.tensor_copy(diag_sc[:, nt:nt + 1], ps7[:, nt:nt + 1])
                vs.inc(inst)

            dq_done = vs.n
            if STOP == "h1dq":
                raise _StopEmit

            for nt in range(16):
                n0 = nt * 128

                @block.tensor
                def _(t):
                    if nt == 0:
                        t.wait_ge(vs.h, dq_done)
                    t.wait_ge(ss.h, ss.n)
                    inst = t.matmul(ps0[:, 0:256], qf[prow:prow + 64, kc_h, n0:n0 + 128],
                                    projt2[prow:prow + 64, :], start=True, stop=True)
                    ts.inc(inst)

                mmq = ts.n

                @block.vector
                def _(v):
                    v.wait_ge(ts.h, mmq)
                    inst = v.reduce_max(kmax_sc[:, 17:18], ps0[:, 0:256], axis=AX.X)
                    vs.inc(inst)
                    v.wait_ge(vs.h, vs.n)
                    inst = v.scalar_tensor_tensor(kmax_sc[:, 17:18], kmax_sc[:, 17:18],
                                                  -NORM, diag_sc[:, nt:nt + 1],
                                                  ALU.mult, ALU.subtract)
                    vs.inc(inst)

                bq_done = vs.n

                @block.scalar
                def _(s):
                    s.wait_ge(vs.h, bq_done)
                    inst = s.activation(q2_scr[:, 128:384], ps0[:, 0:256], AF.Exp,
                                        bias=kmax_sc[:, 17:18], scale=NORM)
                    ss.inc(inst)

                qp_exp = ss.n
                qpt_evac_prev = vs.n

                @block.tensor
                def _(t):
                    t.wait_ge(ss.h, qp_exp)
                    t.wait_ge(vs.h, qpt_evac_prev)
                    t.transpose(ps4[:, 256:384], q2_scr[:, 128:256], ident_bf[:, :])
                    inst = t.transpose(ps4[:, 384:512], q2_scr[:, 256:384], ident_bf[:, :])
                    ts.inc(inst)

                qpt_mm = ts.n

                @block.vector
                def _(v):
                    v.wait_ge(ts.h, qpt_mm)
                    v.tensor_scalar_add(qpT_buf[:, 0, n0:n0 + 128], ps4[:, 256:384], EPS_K)
                    inst = v.tensor_scalar_add(qpT_buf[:, 1, n0:n0 + 128], ps4[:, 384:512],
                                               EPS_K)
                    vs.inc(inst)

            qpt_done = vs.n
            if STOP == "h1qpt":
                raise _StopEmit

            for ti in range(NTT):
                t0 = ti * TT
                av_prev = vs.n

                @block.tensor
                def _(t):
                    t.wait_ge(vs.h, max(qpt_done, av_prev))
                    for mc in range(2):
                        t.matmul(ps5[0:64, :], ctx_sb[:, mc, :], qpT_buf[:, mc, t0:t0 + TT],
                                 start=(mc == 0), stop=(mc == 1))
                    for mc in range(2):
                        inst = t.matmul(ps3[0:1, :], ksum_bf[:, mc:mc + 1],
                                        qpT_buf[:, mc, t0:t0 + TT],
                                        start=(mc == 0), stop=(mc == 1))
                    ts.inc(inst)

                av_mm = ts.n
                r_row = row_scr[0:1, 0:TT]

                @block.vector
                def _(v):
                    v.wait_ge(ts.h, av_mm)
                    inst = v.reciprocal(r_row, ps3[0:1, :])
                    vs.inc(inst)
                    v.wait_ge(vs.h, vs.n)

                rcp_done = vs.n

                @block.tensor
                def _(t):
                    t.wait_ge(vs.h, rcp_done)
                    inst = t.matmul(ps6[:, :], onesrow_f32[0:1, :], r_row,
                                    start=True, stop=True)
                    ts.inc(inst)

                rb_done = ts.n

                @block.vector
                def _(v):
                    v.wait_ge(ts.h, rb_done)
                    avs = scr_f32[0:64, 0:TT]
                    v.tensor_copy(avs, ps5[0:64, :])
                    inst = v.tensor_tensor(qf[prow:prow + 64, kc_h, t0:t0 + TT],
                                           avs, ps6[0:64, :], ALU.mult)
                    vs.inc(inst)


    except _StopEmit:
        pass
    attn_done = vs.n
    if STOP.startswith("h1") or STOP == "attn":
        _l0_bail(p, y_hbm, ff_acc)
        p.close()
        return p
    av = qf

    u_buf = kf

    def u_epilogue(oc, tt, mm_done):
        t0 = tt * TT

        @block.vector
        def _(v):
            v.wait_ge(ts.h, mm_done)
            inst = v.scalar_tensor_tensor(u_buf[:, oc, t0:t0 + TT], ps0[:, :],
                                          b_sb["bo"][:, oc:oc + 1],
                                          xn[:, oc, t0:t0 + TT], ALU.add, ALU.add)
            vs.inc(inst)

    emit_proj_fm(p, av, wb, ps0, u_epilogue, pre_wait_v=attn_done,
                 pre_wait_ld=wo_loaded, ld_sem=lg)
    u_done = vs.n
    if STOP == "u":
        _l0_bail(p, y_hbm, ff_acc)
        p.close()
        return p

    x2_buf = qf
    sqs = v_tm.ap().rearrange("p n d -> p (n d)")[:, 0:KC * TT].rearrange(
        "p (c t) -> p c t", c=KC)

    def ln2_out(v, kc, tt, ps_a, ps_b):
        t0 = tt * TT
        v.tensor_tensor(scr_f32[:, 0:TT], u_buf[:, kc, t0:t0 + TT], ps_a[:, :], ALU.mult)
        v.tensor_tensor(scr_f32[:, 0:TT], scr_f32[:, 0:TT], ps_b[:, :], ALU.subtract)
        inst = v.tensor_scalar(x2_buf[:, kc, t0:t0 + TT], scr_f32[:, 0:TT],
                               b_sb["g2"][:, kc:kc + 1], b_sb["be2"][:, kc:kc + 1],
                               ALU.mult, ALU.add)
        vs.inc(inst)

    emit_ln_sbuf(p, lambda kc, tt: u_buf[:, kc, tt * TT:(tt + 1) * TT], ln2_out,
                 sqs, ps0, ps1, ps_c, pre_wait_v=u_done)
    x2_done = vs.n
    if STOP == "x2":
        _l0_bail(p, y_hbm, ff_acc)
        p.close()
        return p

    h_tt = v_tm.ap().rearrange("p n d -> p (n d)").rearrange(
        "p (c t) -> p c t", c=32)
    wbufs = [wa, wb]
    y2_buf = u_buf
    yf = ff_acc

    for tt in range(NTT):
        t0 = tt * TT
        prev_tile_done = vs.n
        for ds in range(4):
            wbuf = wbufs[ds % 2]
            w_prev_mm = ts.n

            @block.gpsimd
            def _(g):
                g.wait_ge(ts.h, w_prev_mm)
                if tt == 0 and ds == 0:
                    g.wait_ge(vs.h, x2_done)
                lg.dma(g.dma_start(out=wbuf[:, :, :],
                                   in_=w1T_hbm[:, :, ds * 1024:(ds + 1) * 1024]))

            w_ld = lg.n
            for ocr in range(8):
                dffc = ds * 8 + ocr
                relu_prev = ss.n

                @block.tensor
                def _(t):
                    t.wait_ge(lg.h, w_ld)
                    t.wait_ge(ss.h, relu_prev)
                    if ds == 0 and ocr == 0:
                        t.wait_ge(vs.h, max(x2_done, prev_tile_done))
                    for kc in range(KC):
                        inst = t.matmul(ps0[:, :], wbuf[:, kc, ocr * 128:(ocr + 1) * 128],
                                        x2_buf[:, kc, t0:t0 + TT],
                                        start=(kc == 0), stop=(kc == KC - 1))
                    ts.inc(inst)

                h_mm = ts.n

                @block.scalar
                def _(s):
                    s.wait_ge(ts.h, h_mm)
                    inst = s.activation(h_tt[:, dffc, :], ps0[:, :], AF.Relu,
                                        bias=b1_sb[:, dffc:dffc + 1], scale=1.0)
                    ss.inc(inst)

        h_done = ss.n

        for ds in range(4):
            wbuf = wbufs[ds % 2]
            w_prev_mm = ts.n

            @block.gpsimd
            def _(g):
                g.wait_ge(ts.h, w_prev_mm)
                lg.dma(g.dma_start(out=wbuf[:, :, :],
                                   in_=w2T_hbm[:, ds * 8:(ds + 1) * 8, :]))

            w_ld = lg.n
            for oc in range(KC):
                acc_prev = vs.n

                @block.tensor
                def _(t):
                    t.wait_ge(lg.h, w_ld)
                    t.wait_ge(vs.h, acc_prev)
                    if ds == 0 and oc == 0:
                        t.wait_ge(ss.h, h_done)
                    for j in range(8):
                        inst = t.matmul(ps1[:, :], wbuf[:, j, oc * 128:(oc + 1) * 128],
                                        h_tt[:, ds * 8 + j, :],
                                        start=(j == 0), stop=(j == 7))
                    ts.inc(inst)

                ff_mm = ts.n

                @block.vector
                def _(v):
                    v.wait_ge(ts.h, ff_mm)
                    if ds == 0:
                        inst = v.tensor_copy(ff_acc[:, oc, :], ps1[:, :])
                    else:
                        inst = v.tensor_add(ff_acc[:, oc, :], ff_acc[:, oc, :], ps1[:, :])
                    vs.inc(inst)

        @block.vector
        def _(v):
            for oc in range(KC):
                inst = v.scalar_tensor_tensor(y2_buf[:, oc, t0:t0 + TT],
                                              ff_acc[:, oc, :],
                                              b_sb["b2v"][:, oc:oc + 1],
                                              x2_buf[:, oc, t0:t0 + TT],
                                              ALU.add, ALU.add)
            vs.inc(inst)

        y2_done = vs.n

        @block.sync
        def _(s):
            s.wait_ge(vs.h, y2_done)
            ld.dma(s.dma_start(out=x_t[:, :, :], in_=x_hbm[:, :, t0:t0 + TT]))

        x_ld = ld.n

        def ln3_out(v, kc, tt2, ps_a, ps_b):
            v.wait_ge(ld.h, x_ld)
            v.tensor_tensor(scr_f32[:, 0:TT], y2_buf[:, kc, t0:t0 + TT],
                            ps_a[:, :], ALU.mult)
            v.tensor_tensor(scr_f32[:, 0:TT], scr_f32[:, 0:TT], ps_b[:, :], ALU.subtract)
            v.tensor_scalar(scr_f32[:, 0:TT], scr_f32[:, 0:TT],
                            b_sb["g3"][:, kc:kc + 1], b_sb["be3"][:, kc:kc + 1],
                            ALU.mult, ALU.add)
            inst = v.tensor_add(yf[:, kc, :], scr_f32[:, 0:TT], x_t[:, kc, :])
            vs.inc(inst)

        emit_ln_sbuf(p, lambda kc, tt2: y2_buf[:, kc, t0:t0 + TT], ln3_out,
                     sqs, ps0, ps1, ps_c, tts=[tt], pre_wait_v=y2_done)
        y_done = vs.n

        @block.sync
        def _(s):
            s.wait_ge(vs.h, y_done)
            os_.dma(s.dma_start(out=y_hbm[:, :, t0:t0 + TT], in_=yf[:, :, :]))

        y_stored = os_.n

        @block.vector
        def _(v):
            v.wait_ge(os_.h, y_stored)

    @block.sync
    def _(s):
        s.wait_ge(os_.h, os_.n)

    p.close()
    return p


def host_inputs_l0_attn(x_row, w):
    proj = np.asarray(w["proj"])
    projT2 = np.zeros((128, 256), np.float32)
    projT2[0:64, :] = proj.T
    projT2[64:128, :] = proj.T
    ident = np.eye(128, dtype=np.float32)
    return {
        "x": fm_x(x_row),
        "wqT": fm_w(np.ascontiguousarray(w["Wq"].T)),
        "wkT": fm_w(np.ascontiguousarray(w["Wk"].T)),
        "wvT": fm_w(np.ascontiguousarray(w["Wv"].T)),
        "woT": fm_w(np.ascontiguousarray(w["Wo"].T)),
        "bq": fm_vec(w["bq"]), "bk": fm_vec(w["bk"]),
        "bv": fm_vec(w["bv"]), "bo": fm_vec(w["bo"]),
        "g2": fm_vec(w["g2"]), "be2": fm_vec(w["be2"]),
        "g3": fm_vec(w["g3"]), "be3": fm_vec(w["be3"]),
        "b2v": fm_vec(w["b2"]),
        "b1v": fm_vec(w["b1"]),
        "bv_row": np.ascontiguousarray(w["bv"][None, :]),
        "projT2": projT2,
        "ident": ident,
        "w1T": fm_w(np.ascontiguousarray(w["W1"].T)),
        "w2T": fm_w(np.ascontiguousarray(w["W2"].T)),
    }




def build_l1_ffn(shared=None):
    E = 16
    IN = 512
    NH = IN // 128
    p = Prog(shared)
    x_hbm = p.dram_in("x", [128, KC, T])
    gate_h = p.dram_in("gateT", [128, KC, E])
    wsin_h = p.dram_in("wsinT", [128, KC, D])
    vsin_h = p.dram_in("vsinT", [128, KC, D])
    wsout_h = p.dram_in("wsoutT", [128, KC, D])
    ewin_h = p.dram_in("ewinT", [128, E, KC, IN])
    evin_h = p.dram_in("evinT", [128, E, KC, IN])
    ewout_h = p.dram_in("ewoutT", [128, E, NH, D])
    iota16_h = p.dram_in("iota16", [128, E])
    ident_h = p.dram_in("ident", [128, 128])
    y_hbm = p.dram_out("y", [128, KC, T])
    h2w_dram = p.nc.dram_tensor(p.prefix + "h2w_scr", [128, E * NH, T], BF16, kind="Internal").ap()

    xn = p.sbuf("xn", [128, KC, T], BF16)
    sh = p.sbuf("sh", [128, KC, T], BF16)
    shh = p.sbuf("shh", [128, KC, T], BF16)
    wa = p.sbuf("wa", [128, KC, D], BF16)
    wb = p.sbuf("wb", [128, KC, D], BF16)
    x_t = p.sbuf("x_t", [128, KC, TT], F32)
    gate_sb = p.sbuf("gate_sb", [128, KC, E], BF16)
    W16 = p.sbuf("W16", [128, T // 128, E], BF16)
    iota16 = p.sbuf("iota16_sb", [128, E], F32)
    ident_bf = p.sbuf("ident_bf", [128, 128], BF16)
    ones_bf = p.sbuf("ones_bf", [128, 1], BF16)
    onesrow_bf = p.sbuf("onesrow_bf", [1, 128], BF16)
    onesrow_f32 = p.sbuf("onesrow_f32", [1, 128], F32)
    scr_f32 = p.sbuf("scr_f32", [128, 2 * TT], F32)
    row_scr = p.sbuf("row_scr", [1, 4 * TT], F32)
    g_sc = p.sbuf("g_sc", [128, E], F32)
    m8_sc = p.sbuf("m8_sc", [128, 8], F32)
    i8_sc = p.sbuf("i8_sc", [128, 8], U32)
    idxf_sc = p.sbuf("idxf_sc", [128, 8], F32)
    col_sc = p.sbuf("col_sc", [128, 8], F32)
    sil_bf = p.sbuf("sil_bf", [128, TT], BF16)
    h2a_bf = p.sbuf("h2a_bf", [128, TT], BF16)
    sil_bf2 = p.sbuf("sil_bf2", [128, TT], BF16)
    h2_bf = p.sbuf("h2_bf", [128, NH, TT], BF16)
    wrow_bf = p.sbuf("wrow_bf", [1, TT], BF16)
    racc = p.sbuf("racc", [128, KC, TT], F32)
    p.bufs["xbf_scr"] = shh[:, :, 0:TT]
    p.bufs["sq_scr"] = sh[:, :, 0:TT]
    h2in = shh.ap().rearrange("p c t -> p (c t)")[:, 0:16 * TT].rearrange(
        "p (c t) -> p c t", c=16)
    xres = x_t[:, 0:2, :]

    ps0 = p.psum("ps0", [128, TT], F32)
    ps1 = p.psum("ps1", [128, TT], F32)
    ps_c = p.psum("ps_c", [33, TT], F32)
    ps3 = p.psum("ps3", [1, TT], F32)
    ps5 = p.psum("ps5", [128, TT], F32)
    ps6 = p.psum("ps6", [128, TT], F32)
    psA1 = p.psum("psA1", [128, TT], F32)
    p.bufs["ps_a"] = ps0
    p.bufs["ps_b"] = ps1

    p.start_block()
    block = p.block
    ld, ts, vs, ss, os_ = p.sem("ld"), p.sem("ts"), p.sem("vs"), p.sem("ss"), p.sem("os")
    lg = p.sem("lg")
    hs = p.sem("hs")

    @block.gpsimd
    def _(g):
        lg.dma(g.dma_start(out=gate_sb[:, :, :], in_=gate_h))
        lg.dma(g.dma_start(out=wa[:, :, :], in_=wsin_h))
        lg.dma(g.dma_start(out=wb[:, :, :], in_=vsin_h))
        lg.dma(g.dma_start(out=ident_bf[:, :], in_=ident_h))

    @block.sync
    def _(s):
        ld.dma(s.dma_start(out=iota16[:, :], in_=iota16_h))

    all_loaded = ld.n
    all_loaded_g = lg.n

    @block.vector
    def _(v):
        v.wait_ge(ld.h, all_loaded)
        v.wait_ge(lg.h, all_loaded_g)
        v.memset(ones_bf[:, :], 1.0)
        v.memset(onesrow_bf[:, :], 1.0)
        inst = v.memset(onesrow_f32[:, :], 1.0)
        vs.inc(inst)

    @block.scalar
    def _(s):
        s.wait_ge(ld.h, all_loaded)
        s.wait_ge(lg.h, all_loaded_g)

    emit_ln_stream(p, x_hbm, xn)
    ln_done = vs.n

    for t128 in range(T // 128):
        t0 = t128 * 128
        evac_prev = vs.n

        @block.tensor
        def _(t):
            t.wait_ge(vs.h, max(evac_prev, ln_done))
            for kc in range(KC):
                inst = t.matmul(ps5[:, 0:E], xn[:, kc, t0:t0 + 128],
                                gate_sb[:, kc, :], start=(kc == 0), stop=(kc == KC - 1))
            ts.inc(inst)

        g_mm = ts.n

        @block.scalar
        def _(s):
            s.wait_ge(ts.h, g_mm)
            inst = s.activation(g_sc[:, :], ps5[:, 0:E], AF.Sigmoid)
            ss.inc(inst)

        sig_done = ss.n

        @block.vector
        def _(v):
            v.wait_ge(ss.h, sig_done)
            def F(inst):
                vs.inc(inst)
                v.wait_ge(vs.h, vs.n)
            F(v.max(out=m8_sc[:, :], in_=g_sc[:, :]))
            F(v.max_index(out=i8_sc[:, :], in_max=m8_sc[:, :], in_values=g_sc[:, :]))
            F(v.tensor_copy(idxf_sc[:, :], i8_sc[:, :]))
            F(v.reduce_sum(col_sc[:, 0:1], m8_sc[:, 0:4], axis=AX.X))
            F(v.reciprocal(col_sc[:, 1:2], col_sc[:, 0:1]))
            F(v.tensor_scalar(col_sc[:, 4:8], m8_sc[:, 0:4], col_sc[:, 1:2], None,
                              ALU.mult))
            F(v.memset(W16[:, t128, :], 0.0))
            for j in range(4):
                F(v.tensor_scalar(scr_f32[:, 0:E], iota16[:, :], idxf_sc[:, j:j + 1],
                                  None, ALU.is_equal))
                F(v.scalar_tensor_tensor(W16[:, t128, :], scr_f32[:, 0:E],
                                         col_sc[:, 4 + j:5 + j], W16[:, t128, :],
                                         ALU.mult, ALU.add))

    w16_done = vs.n

    def silu_epilogue(oc, tt, mm_done):
        t0 = tt * TT

        @block.scalar
        def _(s):
            s.wait_ge(ts.h, mm_done)
            inst = s.activation(sh[:, oc, t0:t0 + TT], ps0[:, :], AF.Silu)
            ss.inc(inst)

        sdone = ss.n

        @block.vector
        def _(v):
            v.wait_ge(ss.h, sdone)
            inst = v.nop()
            vs.inc(inst)

    emit_proj_fm(p, xn, wa, ps0, silu_epilogue, pre_wait_v=ln_done)

    def mult_epilogue(oc, tt, mm_done):
        t0 = tt * TT

        @block.vector
        def _(v):
            v.wait_ge(ts.h, mm_done)
            inst = v.tensor_tensor(shh[:, oc, t0:t0 + TT], sh[:, oc, t0:t0 + TT],
                                   ps0[:, :], ALU.mult)
            vs.inc(inst)

    emit_proj_fm(p, xn, wb, ps0, mult_epilogue, pre_wait_v=ln_done)
    shh_done = vs.n
    vsin_mm_done = ts.n

    @block.gpsimd
    def _(g):
        g.wait_ge(ts.h, vsin_mm_done)
        lg.dma(g.dma_start(out=wa[:, :, :], in_=wsout_h))

    wsout_ld = lg.n

    def shout_epilogue(oc, tt, mm_done):
        t0 = tt * TT

        @block.vector
        def _(v):
            v.wait_ge(ts.h, mm_done)
            inst = v.tensor_copy(sh[:, oc, t0:t0 + TT], ps0[:, :])
            vs.inc(inst)

    emit_proj_fm(p, shh, wa, ps0, shout_epilogue, pre_wait_v=shh_done,
                 pre_wait_ld=wsout_ld, ld_sem=lg)
    sh_done = vs.n

    wbufs = [wa, wb]
    for e in range(E):
        wbuf = wbufs[e % 2]
        w_prev_mm = ts.n

        @block.gpsimd
        def _(g):
            g.wait_ge(ts.h, w_prev_mm)
            lg.dma(g.dma_start(out=wbuf[:, :, 0:IN], in_=ewin_h[:, e, :, :]))
            lg.dma(g.dma_start(out=wbuf[:, :, IN:2 * IN], in_=evin_h[:, e, :, :]))

        w_ld = lg.n
        for tt in range(NTT):
            t0 = tt * TT
            wrow_prev = vs.n

            @block.tensor
            def _(t):
                t.wait_ge(vs.h, max(w16_done, wrow_prev))
                for j in range(4):
                    inst = t.matmul(ps3[0:1, j * 128:(j + 1) * 128],
                                    W16[:, tt * 4 + j, e:e + 1], ident_bf[:, :],
                                    start=True, stop=True)
                ts.inc(inst)

            wrow_mm = ts.n

            @block.vector
            def _(v):
                v.wait_ge(ts.h, wrow_mm)
                inst = v.tensor_copy(wrow_bf[0:1, :], ps3[0:1, :])
                vs.inc(inst)

            wrow_ev = vs.n
            bcast_prev = vs.n

            @block.tensor
            def _(t):
                t.wait_ge(vs.h, wrow_ev)
                inst = t.matmul(ps6[:, :], onesrow_bf[0:1, :], wrow_bf[0:1, :],
                                start=True, stop=True)
                ts.inc(inst)

            wb_mm = ts.n
            h2ev = [vs.n, vs.n]
            for hd in range(NH):
                h2_store_prev = hs.n
                psA = (ps0, ps5)[hd % 2]
                psB = (ps1, psA1)[hd % 2]
                silc = (sil_bf, sil_bf2)[hd % 2]
                h2ev_par = h2ev[hd % 2]

                @block.tensor
                def _(t):
                    t.wait_ge(lg.h, w_ld)
                    t.wait_ge(vs.h, h2ev_par)
                    for kc in range(KC):
                        t.matmul(psA[:, :], wbuf[:, kc, hd * 128:(hd + 1) * 128],
                                 xn[:, kc, t0:t0 + TT],
                                 start=(kc == 0), stop=(kc == KC - 1))
                    for kc in range(KC):
                        inst = t.matmul(psB[:, :],
                                        wbuf[:, kc, IN + hd * 128:IN + (hd + 1) * 128],
                                        xn[:, kc, t0:t0 + TT],
                                        start=(kc == 0), stop=(kc == KC - 1))
                    ts.inc(inst)

                hv_mm = ts.n

                @block.scalar
                def _(s):
                    s.wait_ge(ts.h, hv_mm)
                    inst = s.activation(silc[:, :], psA[:, :], AF.Silu)
                    ss.inc(inst)

                sil_done = ss.n

                @block.vector
                def _(v):
                    v.wait_ge(ss.h, sil_done)
                    v.wait_ge(ts.h, wb_mm)
                    v.wait_ge(hs.h, h2_store_prev)
                    v.tensor_tensor(h2a_bf[:, :], silc[:, :], psB[:, :], ALU.mult)
                    inst = v.tensor_tensor(h2_bf[:, hd, :], h2a_bf[:, :], ps6[:, :],
                                           ALU.mult)
                    vs.inc(inst)
                h2ev[hd % 2] = vs.n

            h2_done = vs.n

            @block.sync
            def _(s):
                s.wait_ge(vs.h, h2_done)
                hs.dma(s.dma_start(out=h2w_dram[:, e * NH:(e + 1) * NH, t0:t0 + TT],
                                   in_=h2_bf[:, :, :]))

    ph_a_done = hs.n

    for tt in range(NTT):
        t0 = tt * TT
        for eg in range(8):
            h2_prev_mm = ts.n

            @block.sync
            def _(s):
                s.wait_ge(hs.h, ph_a_done)
                s.wait_ge(ts.h, h2_prev_mm)
                ld.dma(s.dma_start(out=h2in[:, 0:8, :],
                                   in_=h2w_dram[:, eg * 8:(eg + 1) * 8, t0:t0 + TT]))

            h2_ld = ld.n
            wq_prev_mm = ts.n
            wbuf = wbufs[eg % 2]

            @block.gpsimd
            def _(g):
                g.wait_ge(ts.h, wq_prev_mm)
                lg.dma(g.dma_start(out=wbuf[:, :, :],
                                   in_=ewout_h[:, eg * 2:(eg + 1) * 2, :, :]))

            wq_ld = lg.n
            rev = [vs.n, vs.n]
            for oc in range(KC):
                racc_prev = rev[oc % 2]
                psP = (ps5, psA1)[oc % 2]

                @block.tensor
                def _(t):
                    t.wait_ge(ld.h, h2_ld)
                    t.wait_ge(lg.h, wq_ld)
                    t.wait_ge(vs.h, racc_prev)
                    wv = wbuf.ap().rearrange("p c d -> p (c d)").rearrange(
                        "p (a b) -> p a b", b=D)
                    for j in range(8):
                        inst = t.matmul(psP[:, :], wv[:, j, oc * 128:(oc + 1) * 128],
                                        h2in[:, j, :], start=(j == 0), stop=(j == 7))
                    ts.inc(inst)

                r_mm = ts.n

                @block.vector
                def _(v):
                    v.wait_ge(ts.h, r_mm)
                    if eg == 0:
                        inst = v.tensor_copy(racc[:, oc, :], psP[:, :])
                    else:
                        inst = v.tensor_add(racc[:, oc, :], racc[:, oc, :], psP[:, :])
                    vs.inc(inst)
                rev[oc % 2] = vs.n

        @block.sync
        def _(s):
            s.wait_ge(vs.h, vs.n)
            ld.dma(s.dma_start(out=x_t[:, :, :], in_=x_hbm[:, :, t0:t0 + TT]))

        x_ld = ld.n
        for oc in range(KC):
            y_prev = os_.n

            @block.vector
            def _(v):
                v.wait_ge(ld.h, x_ld)
                if oc == 0:
                    v.wait_ge(os_.h, max(0, os_.n - 16 * 6))
                v.tensor_add(scr_f32[:, 0:TT], racc[:, oc, :], sh[:, oc, t0:t0 + TT])
                inst = v.tensor_add(racc[:, oc, :], scr_f32[:, 0:TT], x_t[:, oc, :])
                vs.inc(inst)

            y_done = vs.n

            @block.sync
            def _(s):
                s.wait_ge(vs.h, y_done)
                os_.dma(s.dma_start(out=y_hbm[:, oc, t0:t0 + TT], in_=racc[:, oc, :]))

    @block.sync
    def _(s):
        s.wait_ge(os_.h, os_.n)

    p.close()
    return p


def host_inputs_l1_ffn(x_row, w):
    E, IN, NH = 16, 512, 4
    ewin = np.asarray(w["e_win"])
    evin = np.asarray(w["e_vin"])
    ewout = np.asarray(w["e_wout"])
    return {
        "x": fm_x(x_row),
        "gateT": fm_w(np.ascontiguousarray(w["gate_w"].T)),
        "wsinT": fm_w(w["ws_in"]),
        "vsinT": fm_w(w["vs_in"]),
        "wsoutT": fm_w(w["ws_out"]),
        "ewinT": np.ascontiguousarray(
            ewin.reshape(E, KC, 128, IN).transpose(2, 0, 1, 3)),
        "evinT": np.ascontiguousarray(
            evin.reshape(E, KC, 128, IN).transpose(2, 0, 1, 3)),
        "ewoutT": np.ascontiguousarray(
            ewout.reshape(E, NH, 128, D).transpose(2, 0, 1, 3)),
        "iota16": np.broadcast_to(np.arange(E, dtype=np.float32)[None, :],
                                  (128, E)).copy(),
        "ident": np.eye(128, dtype=np.float32),
    }




def build_l3_ffn(shared=None):
    E = 32
    IN = 512
    NH = IN // 128
    p = Prog(shared)
    x_hbm = p.dram_in("x", [128, KC, T])
    gw_h = p.dram_in("gwT", [128, KC, E])
    gb_h = p.dram_in("gb_row", [1, E])
    sw1_h = p.dram_in("sw1T", [128, KC, D])
    sw3_h = p.dram_in("sw3T", [128, KC, D])
    sw2_h = p.dram_in("sw2T", [128, KC, D])
    sb1_h = p.dram_in("sb1v", [128, KC])
    sb3_h = p.dram_in("sb3v", [128, KC])
    sb2_h = p.dram_in("sb2v", [128, KC])
    w1_h = p.dram_in("w1T", [128, E, KC, IN])
    w3_h = p.dram_in("w3T", [128, E, KC, IN])
    w2_h = p.dram_in("w2T", [128, E, NH, D])
    b1_h = p.dram_in("b1m", [128, E, NH])
    b3_h = p.dram_in("b3m", [128, E, NH])
    b2_h = p.dram_in("b2m", [32, D])
    iota32_h = p.dram_in("iota32", [128, E])
    iota8_h = p.dram_in("iota8", [128, 8])
    ident_h = p.dram_in("ident", [128, 128])
    y_hbm = p.dram_out("y", [128, KC, T])
    h2w_dram = p.nc.dram_tensor(p.prefix + "h2w_scr", [128, E * NH, T], BF16, kind="Internal").ap()

    xn = p.sbuf("xn", [128, KC, T], BF16)
    sh = p.sbuf("sh", [128, KC, T], BF16)
    shh = p.sbuf("shh", [128, KC, T], BF16)
    wa = p.sbuf("wa", [128, KC, D], BF16)
    wb = p.sbuf("wb", [128, KC, D], BF16)
    x_t = p.sbuf("x_t", [128, KC, TT], F32)
    gw_sb = p.sbuf("gw_sb", [128, KC, E], BF16)
    W32 = p.sbuf("W32", [128, T // 128, E], BF16)
    W32T = p.sbuf("W32T", [32, T], BF16)
    b2_sb = p.sbuf("b2_sb", [32, D], BF16)
    b1_sb = p.sbuf("b1_sb", [128, E, NH], F32)
    b3_sb = p.sbuf("b3_sb", [128, E, NH], F32)
    sb1_sb = p.sbuf("sb1_sb", [128, KC], F32)
    sb3_sb = p.sbuf("sb3_sb", [128, KC], F32)
    sb2_sb = p.sbuf("sb2_sb", [128, KC], F32)
    gb_bc = p.sbuf("gb_bc", [128, E], F32)
    iota32 = p.sbuf("iota32_sb", [128, E], F32)
    iota8 = p.sbuf("iota8_sb", [128, 8], F32)
    ident_bf = p.sbuf("ident_bf", [128, 128], BF16)
    ones_bf = p.sbuf("ones_bf", [128, 1], BF16)
    onesrow_bf = p.sbuf("onesrow_bf", [1, 128], BF16)
    onesrow_f32 = p.sbuf("onesrow_f32", [1, 128], F32)
    scr_f32 = p.sbuf("scr_f32", [128, 2 * TT], F32)
    row_scr = p.sbuf("row_scr", [1, 4 * TT], F32)
    e32_sc = p.sbuf("e32_sc", [128, E], F32)
    sg_sc = p.sbuf("sg_sc", [128, 8], F32)
    m8_sc = p.sbuf("m8_sc", [128, 8], F32)
    i8_sc = p.sbuf("i8_sc", [128, 8], U32)
    idxf_sc = p.sbuf("idxf_sc", [128, 8], F32)
    col_sc = p.sbuf("col_sc", [128, 8], F32)
    km_sc = p.sbuf("km_sc", [128, E], F32)
    sil_bf = p.sbuf("sil_bf", [128, TT], BF16)
    h2a_bf = p.sbuf("h2a_bf", [128, TT], BF16)
    sil_bf2 = p.sbuf("sil_bf2", [128, TT], BF16)
    h2_bf = p.sbuf("h2_bf", [128, NH, TT], BF16)
    wrow_bf = p.sbuf("wrow_bf", [1, TT], BF16)
    racc = p.sbuf("racc", [128, KC, TT], F32)
    p.bufs["xbf_scr"] = shh[:, :, 0:TT]
    p.bufs["sq_scr"] = sh[:, :, 0:TT]
    h2in = shh.ap().rearrange("p c t -> p (c t)")[:, 0:8 * TT].rearrange(
        "p (c t) -> p c t", c=8)
    xres = x_t[:, 0:2, :]

    ps0 = p.psum("ps0", [128, TT], F32)
    ps1 = p.psum("ps1", [128, TT], F32)
    ps_c = p.psum("ps_c", [33, TT], F32)
    ps3 = p.psum("ps3", [1, TT], F32)
    ps4 = p.psum("ps4", [128, 1024], BF16)
    ps5 = p.psum("ps5", [128, TT], F32)
    ps6 = p.psum("ps6", [128, TT], F32)
    psA1 = p.psum("psA1", [128, TT], F32)
    p.bufs["ps_a"] = ps0
    p.bufs["ps_b"] = ps1

    p.start_block()
    block = p.block
    ld, ts, vs, ss, os_ = p.sem("ld"), p.sem("ts"), p.sem("vs"), p.sem("ss"), p.sem("os")
    lg = p.sem("lg")
    hs = p.sem("hs")

    @block.gpsimd
    def _(g):
        lg.dma(g.dma_start(out=gw_sb[:, :, :], in_=gw_h))
        lg.dma(g.dma_start(out=wa[:, :, :], in_=sw1_h))
        lg.dma(g.dma_start(out=wb[:, :, :], in_=sw3_h))
        lg.dma(g.dma_start(out=ident_bf[:, :], in_=ident_h))
        lg.dma(g.dma_start(out=b2_sb[:, :], in_=b2_h))

    @block.sync
    def _(s):
        ld.dma(s.dma_start(out=iota32[:, :], in_=iota32_h))
        ld.dma(s.dma_start(out=iota8[:, :], in_=iota8_h))
        ld.dma(s.dma_start(out=b1_sb[:, :, :], in_=b1_h))
        ld.dma(s.dma_start(out=b3_sb[:, :, :], in_=b3_h))
        ld.dma(s.dma_start(out=sb1_sb[:, :], in_=sb1_h))
        ld.dma(s.dma_start(out=sb3_sb[:, :], in_=sb3_h))
        ld.dma(s.dma_start(out=sb2_sb[:, :], in_=sb2_h))
        ld.dma(s.dma_start(out=row_scr[0:1, 0:E], in_=gb_h))

    all_loaded = ld.n
    all_loaded_g = lg.n

    @block.vector
    def _(v):
        v.wait_ge(ld.h, all_loaded)
        v.wait_ge(lg.h, all_loaded_g)
        v.memset(ones_bf[:, :], 1.0)
        v.memset(onesrow_bf[:, :], 1.0)
        inst = v.memset(onesrow_f32[:, :], 1.0)
        vs.inc(inst)

    init_done = vs.n

    @block.tensor
    def _(t):
        t.wait_ge(vs.h, init_done)
        inst = t.matmul(ps5[:, 0:E], onesrow_f32[0:1, :], row_scr[0:1, 0:E],
                        start=True, stop=True)
        ts.inc(inst)

    gbm = ts.n

    @block.vector
    def _(v):
        v.wait_ge(ts.h, gbm)
        inst = v.tensor_copy(gb_bc[:, :], ps5[:, 0:E])
        vs.inc(inst)

    @block.scalar
    def _(s):
        s.wait_ge(ld.h, all_loaded)
        s.wait_ge(lg.h, all_loaded_g)

    emit_ln_stream(p, x_hbm, xn)
    ln_done = vs.n

    for t128 in range(T // 128):
        t0 = t128 * 128
        evac_prev = vs.n

        @block.tensor
        def _(t):
            t.wait_ge(vs.h, max(evac_prev, ln_done))
            for kc in range(KC):
                inst = t.matmul(ps5[:, 0:E], xn[:, kc, t0:t0 + 128],
                                gw_sb[:, kc, :], start=(kc == 0), stop=(kc == KC - 1))
            ts.inc(inst)

        g_mm = ts.n

        @block.vector
        def _(v):
            v.wait_ge(ts.h, g_mm)
            inst = v.tensor_tensor(e32_sc[:, :], ps5[:, 0:E], gb_bc[:, :], ALU.add)
            vs.inc(inst)
            v.wait_ge(vs.h, vs.n)
            inst = v.reduce_max(col_sc[:, 0:1], e32_sc[:, :], axis=AX.X)
            vs.inc(inst)
            v.wait_ge(vs.h, vs.n)
            inst = v.tensor_scalar_mul(col_sc[:, 1:2], col_sc[:, 0:1], -1.0)
            vs.inc(inst)

        sm1 = vs.n

        @block.scalar
        def _(s):
            s.wait_ge(vs.h, sm1)
            inst = s.activation(e32_sc[:, :], e32_sc[:, :], AF.Exp,
                                bias=col_sc[:, 1:2], scale=1.0)
            ss.inc(inst)

        sm2 = ss.n

        @block.vector
        def _(v):
            v.wait_ge(ss.h, sm2)
            def F(inst):
                vs.inc(inst)
                v.wait_ge(vs.h, vs.n)
            F(v.reduce_sum(col_sc[:, 2:3], e32_sc[:, :], axis=AX.X))
            F(v.reciprocal(col_sc[:, 3:4], col_sc[:, 2:3]))
            F(v.tensor_scalar(e32_sc[:, :], e32_sc[:, :], col_sc[:, 3:4], None,
                              ALU.mult))
            sg = e32_sc[:, :].rearrange("p (g f) -> p g f", f=4)
            first = True
            for i in range(4):
                for j in range(i + 1, 4):
                    F(v.tensor_tensor(scr_f32[:, 0:8], sg[:, :, i], sg[:, :, j],
                                      ALU.add))
                    if first:
                        F(v.tensor_copy(sg_sc[:, :], scr_f32[:, 0:8]))
                        first = False
                    else:
                        F(v.tensor_tensor(sg_sc[:, :], sg_sc[:, :], scr_f32[:, 0:8],
                                          ALU.max))
            F(v.max(out=m8_sc[:, :], in_=sg_sc[:, :]))
            F(v.max_index(out=i8_sc[:, :], in_max=m8_sc[:, :], in_values=sg_sc[:, :]))
            F(v.tensor_copy(idxf_sc[:, :], i8_sc[:, :]))
            F(v.memset(km_sc[:, 0:8], 0.0))
            for j in range(4):
                F(v.tensor_scalar(scr_f32[:, 0:8], iota8[:, :], idxf_sc[:, j:j + 1],
                                  None, ALU.is_equal))
                F(v.tensor_tensor(km_sc[:, 0:8], km_sc[:, 0:8], scr_f32[:, 0:8],
                                  ALU.add))
            kmb = km_sc[:, 0:8].unsqueeze(2).to_broadcast([128, 8, 4])
            F(v.tensor_copy(scr_f32[:, 8:40].rearrange("p (g f) -> p g f", f=4), kmb))
            F(v.tensor_scalar_add(scr_f32[:, 40:72], e32_sc[:, :], 10.0))
            F(v.tensor_tensor(scr_f32[:, 40:72], scr_f32[:, 40:72], scr_f32[:, 8:40],
                              ALU.mult))
            F(v.tensor_scalar_add(scr_f32[:, 40:72], scr_f32[:, 40:72], -10.0))
            F(v.max(out=m8_sc[:, :], in_=scr_f32[:, 40:72]))
            F(v.max_index(out=i8_sc[:, :], in_max=m8_sc[:, :],
                          in_values=scr_f32[:, 40:72]))
            F(v.tensor_copy(idxf_sc[:, :], i8_sc[:, :]))
            F(v.memset(W32[:, t128, :], 0.0))
            for j in range(4):
                F(v.tensor_scalar(scr_f32[:, 0:E], iota32[:, :], idxf_sc[:, j:j + 1],
                                  None, ALU.is_equal))
                F(v.scalar_tensor_tensor(W32[:, t128, :], scr_f32[:, 0:E],
                                         m8_sc[:, j:j + 1], W32[:, t128, :],
                                         ALU.mult, ALU.add))

        w_done = vs.n

        @block.tensor
        def _(t):
            t.wait_ge(vs.h, w_done)
            inst = t.transpose(ps4[0:32, 0:128], W32[:, t128, :], ident_bf[:, :])
            ts.inc(inst)

        wt_mm = ts.n

        @block.vector
        def _(v):
            v.wait_ge(ts.h, wt_mm)
            inst = v.tensor_copy(W32T[:, t0:t0 + 128], ps4[0:32, 0:128])
            vs.inc(inst)

    w32_done = vs.n

    def silu_epilogue(oc, tt, mm_done):
        t0 = tt * TT

        @block.scalar
        def _(s):
            s.wait_ge(ts.h, mm_done)
            inst = s.activation(sh[:, oc, t0:t0 + TT], ps0[:, :], AF.Silu,
                                bias=sb1_sb[:, oc:oc + 1], scale=1.0)
            ss.inc(inst)

        sdone = ss.n

        @block.vector
        def _(v):
            v.wait_ge(ss.h, sdone)
            inst = v.nop()
            vs.inc(inst)

    emit_proj_fm(p, xn, wa, ps0, silu_epilogue, pre_wait_v=ln_done)

    def mult_epilogue(oc, tt, mm_done):
        t0 = tt * TT

        @block.vector
        def _(v):
            v.wait_ge(ts.h, mm_done)
            inst = v.scalar_tensor_tensor(shh[:, oc, t0:t0 + TT], ps0[:, :],
                                          sb3_sb[:, oc:oc + 1],
                                          sh[:, oc, t0:t0 + TT], ALU.add, ALU.mult)
            vs.inc(inst)

    emit_proj_fm(p, xn, wb, ps0, mult_epilogue, pre_wait_v=ln_done)
    shh_done = vs.n
    sw3_mm_done = ts.n

    @block.gpsimd
    def _(g):
        g.wait_ge(ts.h, sw3_mm_done)
        lg.dma(g.dma_start(out=wa[:, :, :], in_=sw2_h))

    sw2_ld = lg.n

    def shout_epilogue(oc, tt, mm_done):
        t0 = tt * TT

        @block.vector
        def _(v):
            v.wait_ge(ts.h, mm_done)
            inst = v.tensor_scalar(sh[:, oc, t0:t0 + TT], ps0[:, :],
                                   sb2_sb[:, oc:oc + 1], None, ALU.add)
            vs.inc(inst)

    emit_proj_fm(p, shh, wa, ps0, shout_epilogue, pre_wait_v=shh_done,
                 pre_wait_ld=sw2_ld, ld_sem=lg)
    sh_done = vs.n

    wbufs = [wa, wb]
    for e in range(E):
        wbuf = wbufs[e % 2]
        w_prev_mm = ts.n

        @block.gpsimd
        def _(g):
            g.wait_ge(ts.h, w_prev_mm)
            lg.dma(g.dma_start(out=wbuf[:, :, 0:IN], in_=w1_h[:, e, :, :]))
            lg.dma(g.dma_start(out=wbuf[:, :, IN:2 * IN], in_=w3_h[:, e, :, :]))

        w_ld = lg.n
        for tt in range(NTT):
            t0 = tt * TT
            wrow_prev = vs.n

            @block.tensor
            def _(t):
                t.wait_ge(vs.h, max(w32_done, wrow_prev))
                for j in range(4):
                    inst = t.matmul(ps3[0:1, j * 128:(j + 1) * 128],
                                    W32[:, tt * 4 + j, e:e + 1], ident_bf[:, :],
                                    start=True, stop=True)
                ts.inc(inst)

            wrow_mm = ts.n

            @block.vector
            def _(v):
                v.wait_ge(ts.h, wrow_mm)
                inst = v.tensor_copy(wrow_bf[0:1, :], ps3[0:1, :])
                vs.inc(inst)

            wrow_ev = vs.n

            @block.tensor
            def _(t):
                t.wait_ge(vs.h, wrow_ev)
                inst = t.matmul(ps6[:, :], onesrow_bf[0:1, :], wrow_bf[0:1, :],
                                start=True, stop=True)
                ts.inc(inst)

            wb_mm = ts.n
            h2ev = [vs.n, vs.n]
            for hd in range(NH):
                h2_store_prev = hs.n
                psA = (ps0, ps5)[hd % 2]
                psB = (ps1, psA1)[hd % 2]
                silc = (sil_bf, sil_bf2)[hd % 2]
                h2ev_par = h2ev[hd % 2]

                @block.tensor
                def _(t):
                    t.wait_ge(lg.h, w_ld)
                    t.wait_ge(vs.h, h2ev_par)
                    for kc in range(KC):
                        t.matmul(psA[:, :], wbuf[:, kc, hd * 128:(hd + 1) * 128],
                                 xn[:, kc, t0:t0 + TT],
                                 start=(kc == 0), stop=(kc == KC - 1))
                    for kc in range(KC):
                        inst = t.matmul(psB[:, :],
                                        wbuf[:, kc, IN + hd * 128:IN + (hd + 1) * 128],
                                        xn[:, kc, t0:t0 + TT],
                                        start=(kc == 0), stop=(kc == KC - 1))
                    ts.inc(inst)

                hv_mm = ts.n

                @block.scalar
                def _(s):
                    s.wait_ge(ts.h, hv_mm)
                    inst = s.activation(silc[:, :], psA[:, :], AF.Silu,
                                        bias=b1_sb[:, e, hd:hd + 1], scale=1.0)
                    ss.inc(inst)

                sil_done = ss.n

                @block.vector
                def _(v):
                    v.wait_ge(ss.h, sil_done)
                    v.wait_ge(ts.h, wb_mm)
                    v.wait_ge(hs.h, h2_store_prev)
                    v.scalar_tensor_tensor(h2a_bf[:, :], psB[:, :],
                                           b3_sb[:, e, hd:hd + 1], silc[:, :],
                                           ALU.add, ALU.mult)
                    inst = v.tensor_tensor(h2_bf[:, hd, :], h2a_bf[:, :], ps6[:, :],
                                           ALU.mult)
                    vs.inc(inst)
                h2ev[hd % 2] = vs.n

            h2_done = vs.n

            @block.sync
            def _(s):
                s.wait_ge(vs.h, h2_done)
                hs.dma(s.dma_start(out=h2w_dram[:, e * NH:(e + 1) * NH, t0:t0 + TT],
                                   in_=h2_bf[:, :, :]))

    ph_a_done = hs.n

    for tt in range(NTT):
        t0 = tt * TT
        for eg in range(16):
            h2_prev_mm = ts.n

            @block.sync
            def _(s):
                s.wait_ge(hs.h, ph_a_done)
                s.wait_ge(ts.h, h2_prev_mm)
                ld.dma(s.dma_start(out=h2in[:, 0:8, :],
                                   in_=h2w_dram[:, eg * 8:(eg + 1) * 8, t0:t0 + TT]))

            h2_ld = ld.n
            wq_prev_mm = ts.n
            wbuf = wbufs[eg % 2]

            @block.gpsimd
            def _(g):
                g.wait_ge(ts.h, wq_prev_mm)
                lg.dma(g.dma_start(out=wbuf[:, :, :],
                                   in_=w2_h[:, eg * 2:(eg + 1) * 2, :, :]))

            wq_ld = lg.n
            rev = [vs.n, vs.n]
            for oc in range(KC):
                racc_prev = rev[oc % 2]
                psP = (ps5, psA1)[oc % 2]

                @block.tensor
                def _(t):
                    t.wait_ge(ld.h, h2_ld)
                    t.wait_ge(lg.h, wq_ld)
                    t.wait_ge(vs.h, racc_prev)
                    wv = wbuf.ap().rearrange("p c d -> p (c d)").rearrange(
                        "p (a b) -> p a b", b=D)
                    if eg == 0:
                        t.matmul(psP[:, :], b2_sb[:, oc * 128:(oc + 1) * 128],
                                 W32T[:, t0:t0 + TT], start=True, stop=False)
                    for j in range(8):
                        inst = t.matmul(psP[:, :], wv[:, j, oc * 128:(oc + 1) * 128],
                                        h2in[:, j, :],
                                        start=(j == 0 and eg != 0), stop=(j == 7))
                    ts.inc(inst)

                r_mm = ts.n

                @block.vector
                def _(v):
                    v.wait_ge(ts.h, r_mm)
                    if eg == 0:
                        inst = v.tensor_copy(racc[:, oc, :], psP[:, :])
                    else:
                        inst = v.tensor_add(racc[:, oc, :], racc[:, oc, :], psP[:, :])
                    vs.inc(inst)
                rev[oc % 2] = vs.n

        @block.sync
        def _(s):
            s.wait_ge(vs.h, vs.n)
            ld.dma(s.dma_start(out=x_t[:, :, :], in_=x_hbm[:, :, t0:t0 + TT]))

        x_ld = ld.n
        for oc in range(KC):
            @block.vector
            def _(v):
                v.wait_ge(ld.h, x_ld)
                if oc == 0:
                    v.wait_ge(os_.h, max(0, os_.n - 16 * 6))
                v.tensor_add(scr_f32[:, 0:TT], racc[:, oc, :], sh[:, oc, t0:t0 + TT])
                inst = v.tensor_add(racc[:, oc, :], scr_f32[:, 0:TT], x_t[:, oc, :])
                vs.inc(inst)

            y_done = vs.n

            @block.sync
            def _(s):
                s.wait_ge(vs.h, y_done)
                os_.dma(s.dma_start(out=y_hbm[:, oc, t0:t0 + TT], in_=racc[:, oc, :]))

    @block.sync
    def _(s):
        s.wait_ge(os_.h, os_.n)

    p.close()
    return p


def host_inputs_l3_ffn(x_row, w):
    E, IN, NH = 32, 512, 4
    w1 = np.asarray(w["w1"])
    w3 = np.asarray(w["w3"])
    w2 = np.asarray(w["w2"])
    b1 = np.asarray(w["b1"])
    b3 = np.asarray(w["b3"])
    return {
        "x": fm_x(x_row),
        "gwT": fm_w(w["gw"]),
        "gb_row": np.ascontiguousarray(np.asarray(w["gb"])[None, :]),
        "sw1T": fm_w(w["sw1"]),
        "sw3T": fm_w(w["sw3"]),
        "sw2T": fm_w(w["sw2"]),
        "sb1v": fm_vec(w["sb1"]), "sb3v": fm_vec(w["sb3"]), "sb2v": fm_vec(w["sb2"]),
        "w1T": np.ascontiguousarray(w1.reshape(E, KC, 128, IN).transpose(2, 0, 1, 3)),
        "w3T": np.ascontiguousarray(w3.reshape(E, KC, 128, IN).transpose(2, 0, 1, 3)),
        "w2T": np.ascontiguousarray(w2.reshape(E, NH, 128, D).transpose(2, 0, 1, 3)),
        "b1m": np.ascontiguousarray(b1.reshape(E, NH, 128).transpose(2, 0, 1)),
        "b3m": np.ascontiguousarray(b3.reshape(E, NH, 128).transpose(2, 0, 1)),
        "b2m": np.ascontiguousarray(np.asarray(w["b2"])),
        "iota32": np.broadcast_to(np.arange(E, dtype=np.float32)[None, :],
                                  (128, E)).copy(),
        "iota8": np.broadcast_to(np.arange(8, dtype=np.float32)[None, :],
                                 (128, 8)).copy(),
        "ident": np.eye(128, dtype=np.float32),
    }




def build_l0_ffn(shared=None):
    import os as _os
    E = 8
    IN = 1024
    NH = IN // 128
    CAP = T
    p = Prog(shared)
    x_hbm = p.dram_in("x", [128, KC, T])
    wr_h = p.dram_in("wrT_f32", [128, KC, E])
    br_h = p.dram_in("br_row", [1, E])
    w1_h = p.dram_in("w1T", [128, E, KC, IN])
    w2_h = p.dram_in("w2T", [128, E, NH, D])
    b1_h = p.dram_in("b1m", [128, E, NH])
    b2_h = p.dram_in("b2m", [E, D])
    iota8_h = p.dram_in("iota8", [128, 8])
    ident_h = p.dram_in("ident", [128, 128])
    identf_h = p.dram_in("identf", [128, 128])
    sel_h = p.dram_in("sel_prior", [64, E])
    y_hbm = p.dram_out("y", [128, KC, T])
    h2w_dram = p.nc.dram_tensor(p.prefix + "h2w_scr", [128, E * NH, T], BF16, kind="Internal").ap()
    cc_in = p.nc.dram_tensor(p.prefix + "cc_in", [E, 1], F32).ap()
    cc_out = p.nc.dram_tensor(p.prefix + "cc_out", [8 * E, 1], F32).ap()

    xn = p.sbuf("xn", [128, KC, T], BF16)
    wa = p.sbuf("wa", [128, KC, D], BF16)
    wb = p.sbuf("wb", [128, KC, D], BF16)
    x_t = p.sbuf("x_t", [128, KC, TT], F32)
    wr_sb = p.sbuf("wr_sb", [128, KC, E], F32)
    br_bc = p.sbuf("br_bc", [128, E], F32)
    W8 = p.sbuf("W8", [128, T // 128, E], BF16)
    W8T = p.sbuf("W8T", [E, T], BF16)
    b2_sb = p.sbuf("b2_sb", [E, D], BF16)
    b1_sb = p.sbuf("b1_sb", [128, E, NH], F32)
    sel_sb = p.sbuf("sel_sb", [64, E], F32)
    oh_tm = p.sbuf("oh_tm", [128, T // 128, E], BF16)
    oh_em = p.sbuf("oh_em", [E, T], F32)
    ones8t = p.sbuf("ones8t", [E, T], F32)
    pos_em = p.sbuf("pos_em", [E, T], F32)
    cts_sb = p.sbuf("cts_sb", [64, 1], F32)
    cnt_sb = p.sbuf("cnt_sb", [E, 1], F32)
    prior_sb = p.sbuf("prior_sb", [E, 1], F32)
    iota8 = p.sbuf("iota8_sb", [128, 8], F32)
    ident_bf = p.sbuf("ident_bf", [128, 128], BF16)
    ident_f = p.sbuf("ident_f", [128, 128], F32)
    ones_bf = p.sbuf("ones_bf", [128, 1], BF16)
    onesrow_bf = p.sbuf("onesrow_bf", [1, 128], BF16)
    onesrow_f32 = p.sbuf("onesrow_f32", [1, 128], F32)
    scr_f32 = p.sbuf("scr_f32", [128, 2 * TT], F32)
    row_scr = p.sbuf("row_scr", [1, 4 * TT], F32)
    ln_f32 = p.sbuf("ln_f32", [128, TT], F32)
    p.bufs["ln_f32"] = ln_f32
    gate_acc = p.sbuf("gate_acc", [128, 128], F32)
    e8_sc = p.sbuf("e8_sc", [128, 8], F32)
    m8_sc = p.sbuf("m8_sc", [128, 8], F32)
    i8_sc = p.sbuf("i8_sc", [128, 8], U32)
    idxf_sc = p.sbuf("idxf_sc", [128, 8], F32)
    col_sc = p.sbuf("col_sc", [128, 8], F32)
    sil_bf = p.sbuf("sil_bf", [128, TT], BF16)
    sil_bf2 = p.sbuf("sil_bf2", [128, TT], BF16)
    h2_bf = p.sbuf("h2_bf", [128, NH, TT], BF16)
    h2in = p.sbuf("h2in", [128, NH, TT], BF16)
    wrow_bf = p.sbuf("wrow_bf", [1, TT], BF16)
    wbc_sb = p.sbuf("wbc_sb", [128, TT], F32)
    racc = p.sbuf("racc", [128, KC, TT], F32)
    p.bufs["xbf_scr"] = xn[:, :, 0:TT]
    xbf_scr = p.sbuf("xbf_scr2", [128, KC, TT], BF16)
    sq_scr = p.sbuf("sq_scr2", [128, KC, TT], BF16)
    p.bufs["xbf_scr"] = xbf_scr
    p.bufs["sq_scr"] = sq_scr
    xres = x_t[:, 0:2, :]

    ps0 = p.psum("ps0", [128, TT], F32)
    ps1 = p.psum("ps1", [128, TT], F32)
    ps_c = p.psum("ps_c", [33, TT], F32)
    ps3 = p.psum("ps3", [1, TT], F32)
    ps4 = p.psum("ps4", [128, 1024], BF16)
    ps5 = p.psum("ps5", [128, TT], F32)
    ps6 = p.psum("ps6", [128, TT], F32)
    ps_g = p.psum("ps_g", [128, 128], F32)
    p.bufs["ps_a"] = ps0
    p.bufs["ps_b"] = ps1

    p.start_block()
    block = p.block
    ld, ts, vs, ss, os_ = p.sem("ld"), p.sem("ts"), p.sem("vs"), p.sem("ss"), p.sem("os")
    lg = p.sem("lg")
    hs = p.sem("hs")
    cs = p.sem("cs")

    @block.gpsimd
    def _(g):
        lg.dma(g.dma_start(out=ident_bf[:, :], in_=ident_h))
        lg.dma(g.dma_start(out=b2_sb[:, :], in_=b2_h))

    @block.sync
    def _(s):
        ld.dma(s.dma_start(out=wr_sb[:, :, :], in_=wr_h))
        ld.dma(s.dma_start(out=iota8[:, :], in_=iota8_h))
        ld.dma(s.dma_start(out=ident_f[:, :], in_=identf_h))
        ld.dma(s.dma_start(out=sel_sb[:, :], in_=sel_h))
        ld.dma(s.dma_start(out=b1_sb[:, :, :], in_=b1_h))
        ld.dma(s.dma_start(out=row_scr[0:1, 0:E], in_=br_h))

    all_loaded = ld.n
    all_loaded_g = lg.n

    @block.vector
    def _(v):
        v.wait_ge(ld.h, all_loaded)
        v.wait_ge(lg.h, all_loaded_g)
        v.memset(ones_bf[:, :], 1.0)
        v.memset(onesrow_bf[:, :], 1.0)
        v.memset(onesrow_f32[:, :], 1.0)
        inst = v.memset(ones8t[:, :], 1.0)
        vs.inc(inst)

    init_done = vs.n

    @block.tensor
    def _(t):
        t.wait_ge(vs.h, init_done)
        inst = t.matmul(ps5[:, 0:E], onesrow_f32[0:1, :], row_scr[0:1, 0:E],
                        start=True, stop=True)
        ts.inc(inst)

    brm = ts.n

    @block.vector
    def _(v):
        v.wait_ge(ts.h, brm)
        inst = v.tensor_copy(br_bc[:, :], ps5[:, 0:E])
        vs.inc(inst)

    @block.scalar
    def _(s):
        s.wait_ge(ld.h, all_loaded)
        s.wait_ge(lg.h, all_loaded_g)

    def gate_cb(kc, tt, f32_ready):
        @block.tensor
        def _(t):
            t.wait_ge(vs.h, f32_ready)
            for q in range(4):
                inst = t.matmul(ps_g[:, q * 8:(q + 1) * 8],
                                ln_f32[:, q * 128:(q + 1) * 128],
                                wr_sb[:, kc, :], start=True, stop=True)
            ts.inc(inst)

        gm = ts.n

        @block.vector
        def _(v):
            v.wait_ge(ts.h, gm)
            if kc == 0:
                inst = v.tensor_copy(gate_acc[:, tt * 32:(tt + 1) * 32], ps_g[:, 0:32])
            else:
                inst = v.tensor_add(gate_acc[:, tt * 32:(tt + 1) * 32],
                                    gate_acc[:, tt * 32:(tt + 1) * 32], ps_g[:, 0:32])
            vs.inc(inst)

    emit_ln_stream(p, x_hbm, xn, f32_cb=gate_cb)
    ln_done = vs.n
    gates_mm = ts.n

    for t128 in range(T // 128):
        @block.vector
        def _(v):
            v.wait_ge(ts.h, gates_mm)
            v.tensor_tensor(e8_sc[:, :], gate_acc[:, t128 * 8:(t128 + 1) * 8],
                            br_bc[:, :], ALU.add)
            inst = v.reduce_max(col_sc[:, 0:1], e8_sc[:, :], axis=AX.X)
            vs.inc(inst)
            v.wait_ge(vs.h, vs.n)
            inst = v.tensor_scalar_mul(col_sc[:, 1:2], col_sc[:, 0:1], -1.0)
            vs.inc(inst)

        sm1 = vs.n

        @block.scalar
        def _(s):
            s.wait_ge(vs.h, sm1)
            inst = s.activation(e8_sc[:, :], e8_sc[:, :], AF.Exp,
                                bias=col_sc[:, 1:2], scale=1.0)
            ss.inc(inst)

        sm2 = ss.n

        @block.vector
        def _(v):
            v.wait_ge(ss.h, sm2)
            inst = v.reduce_sum(col_sc[:, 2:3], e8_sc[:, :], axis=AX.X)
            vs.inc(inst)
            v.wait_ge(vs.h, vs.n)
            inst = v.reciprocal(col_sc[:, 3:4], col_sc[:, 2:3])
            vs.inc(inst)
            v.wait_ge(vs.h, vs.n)
            inst = v.tensor_scalar(e8_sc[:, :], e8_sc[:, :], col_sc[:, 3:4],
                                   None, ALU.mult)
            vs.inc(inst)
            v.wait_ge(vs.h, vs.n)
            inst = v.max(out=m8_sc[:, :], in_=e8_sc[:, :])
            vs.inc(inst)
            v.wait_ge(vs.h, vs.n)
            inst = v.max_index(out=i8_sc[:, :], in_max=m8_sc[:, :], in_values=e8_sc[:, :])
            vs.inc(inst)
            v.wait_ge(vs.h, vs.n)
            inst = v.tensor_copy(idxf_sc[:, 0:1], i8_sc[:, 0:1])
            vs.inc(inst)
            v.tensor_copy(col_sc[:, 4:5], m8_sc[:, 0:1])
            inst = v.tensor_scalar(oh_tm[:, t128, :], iota8[:, :], idxf_sc[:, 0:1],
                                   None, ALU.is_equal)
            vs.inc(inst)

        oh_done = vs.n

        @block.tensor
        def _(t):
            t.wait_ge(vs.h, oh_done)
            inst = t.transpose(ps4[0:E, 0:128], oh_tm[:, t128, :], ident_bf[:, :])
            ts.inc(inst)

        oht_mm = ts.n

        @block.vector
        def _(v):
            v.wait_ge(ts.h, oht_mm)
            inst = v.tensor_copy(oh_em[:, t128 * 128:(t128 + 1) * 128], ps4[0:E, 0:128])
            vs.inc(inst)

        @block.vector
        def _(v):
            inst = v.tensor_copy(W8[:, t128, :], oh_tm[:, t128, :])
            vs.inc(inst)
            inst = v.tensor_copy(scr_f32[:, t128:t128 + 1], col_sc[:, 4:5])
            vs.inc(inst)

    oh_all = vs.n

    @block.vector
    def _(v):
        v.wait_ge(vs.h, oh_all)
        inst = v.reduce_sum(cnt_sb[:, 0:1], oh_em[:, :], axis=AX.X)
        vs.inc(inst)
        v.wait_ge(vs.h, vs.n)

    cnt_done = vs.n

    @block.gpsimd
    def _(g):
        g.wait_ge(vs.h, cnt_done)
        lg.dma(g.dma_start(out=cc_in, in_=cnt_sb[:, :]))
        g.wait_ge(lg.h, lg.n)
        g.collective_compute(
            "AllGather", ALU.bypass, replica_groups=[list(range(8))],
            ins=[cc_in], outs=[cc_out]).then_inc(cs.h, 1)
        cs.n += 1
        g.wait_ge(cs.h, cs.n)
        lg.dma(g.dma_start(out=cts_sb[:, :], in_=cc_out))

    cts_ld = lg.n

    @block.tensor
    def _(t):
        t.wait_ge(lg.h, cts_ld)
        inst = t.matmul(ps3[0:E, 0:1] if False else ps5[0:E, 0:1],
                        sel_sb[:, :], cts_sb[:, :], start=True, stop=True)
        ts.inc(inst)

    prior_mm = ts.n

    @block.vector
    def _(v):
        v.wait_ge(ts.h, prior_mm)
        inst = v.tensor_copy(prior_sb[:, :], ps5[0:E, 0:1])
        vs.inc(inst)
        v.wait_ge(vs.h, vs.n)
        inst = v.tensor_tensor_scan(pos_em[:, :], ones8t[:, :], oh_em[:, :],
                                    prior_sb[:, :], ALU.mult, ALU.add)
        vs.inc(inst)
        v.wait_ge(vs.h, vs.n)

    pos_done = vs.n

    for t128 in range(T // 128):
        t0 = t128 * 128
        keep_prev = vs.n

        @block.tensor
        def _(t):
            t.wait_ge(vs.h, max(pos_done, keep_prev))
            inst = t.transpose(ps6[0:128, 0:E], pos_em[:, t0:t0 + 128], ident_f[0:E, 0:E])
            ts.inc(inst)

        pt_mm = ts.n

        @block.vector
        def _(v):
            v.wait_ge(ts.h, pt_mm)
            def F(inst):
                vs.inc(inst)
                v.wait_ge(vs.h, vs.n)
            F(v.tensor_tensor(e8_sc[:, :], W8[:, t128, :], ps6[0:128, 0:E], ALU.mult))
            F(v.reduce_sum(col_sc[:, 5:6], e8_sc[:, :], axis=AX.X))
            F(v.tensor_scalar(col_sc[:, 6:7], col_sc[:, 5:6], float(CAP - 1), None,
                              ALU.is_le))
            F(v.tensor_tensor(col_sc[:, 7:8], scr_f32[:, t128:t128 + 1],
                              col_sc[:, 6:7], ALU.mult))
            F(v.tensor_scalar(W8[:, t128, :], W8[:, t128, :], col_sc[:, 7:8],
                              None, ALU.mult))

        w8_done = vs.n

        @block.tensor
        def _(t):
            t.wait_ge(vs.h, w8_done)
            inst = t.transpose(ps4[0:E, 128:256], W8[:, t128, :], ident_bf[:, :])
            ts.inc(inst)

        wt_mm = ts.n

        @block.vector
        def _(v):
            v.wait_ge(ts.h, wt_mm)
            inst = v.tensor_copy(W8T[:, t0:t0 + 128], ps4[0:E, 128:256])
            vs.inc(inst)

    w8_all = vs.n

    wbufs = [wa, wb]
    for e in range(E):
        wbuf = wbufs[e % 2]
        w_prev_mm = ts.n

        @block.gpsimd
        def _(g):
            g.wait_ge(ts.h, w_prev_mm)
            lg.dma(g.dma_start(out=wbuf[:, :, :], in_=w1_h[:, e, :, :]))

        w_ld = lg.n
        for tt in range(NTT):
            t0 = tt * TT
            wrow_prev = vs.n

            @block.tensor
            def _(t):
                t.wait_ge(vs.h, max(w8_all, wrow_prev))
                for j in range(4):
                    inst = t.matmul(ps3[0:1, j * 128:(j + 1) * 128],
                                    W8[:, tt * 4 + j, e:e + 1], ident_bf[:, :],
                                    start=True, stop=True)
                ts.inc(inst)

            wrow_mm = ts.n

            @block.vector
            def _(v):
                v.wait_ge(ts.h, wrow_mm)
                inst = v.tensor_copy(wrow_bf[0:1, :], ps3[0:1, :])
                vs.inc(inst)

            wrow_ev = vs.n

            @block.tensor
            def _(t):
                t.wait_ge(vs.h, wrow_ev)
                inst = t.matmul(ps6[:, :], onesrow_bf[0:1, :], wrow_bf[0:1, :],
                                start=True, stop=True)
                ts.inc(inst)

            wb_mm = ts.n

            @block.vector
            def _(v):
                v.wait_ge(ts.h, wb_mm)
                inst = v.tensor_copy(wbc_sb[:, :], ps6[:, :])
                vs.inc(inst)

            wbc_done = vs.n
            h2ev = [vs.n, vs.n]
            for hd in range(NH):
                h2_store_prev = hs.n
                psA = (ps0, ps1)[hd % 2]
                silc = (sil_bf, sil_bf2)[hd % 2]
                scrc = scr_f32[:, 0:TT] if hd % 2 == 0 else scr_f32[:, TT:2 * TT]
                h2ev_par = h2ev[hd % 2]

                @block.tensor
                def _(t):
                    t.wait_ge(lg.h, w_ld)
                    t.wait_ge(vs.h, h2ev_par)
                    for kc in range(KC):
                        inst = t.matmul(psA[:, :], wbuf[:, kc, hd * 128:(hd + 1) * 128],
                                        xn[:, kc, t0:t0 + TT],
                                        start=(kc == 0), stop=(kc == KC - 1))
                    ts.inc(inst)

                hv_mm = ts.n

                @block.vector
                def _(v):
                    v.wait_ge(ts.h, hv_mm)
                    v.wait_ge(vs.h, wbc_done)
                    inst = v.tensor_tensor(scrc, psA[:, :], wbc_sb[:, :],
                                           ALU.mult)
                    vs.inc(inst)

                u_done = vs.n

                @block.scalar
                def _(s):
                    s.wait_ge(vs.h, u_done)
                    inst = s.activation(silc[:, :], scrc, AF.Relu,
                                        bias=b1_sb[:, e, hd:hd + 1], scale=1.0)
                    ss.inc(inst)

                sil_done = ss.n

                @block.vector
                def _(v):
                    v.wait_ge(ss.h, sil_done)
                    v.wait_ge(hs.h, h2_store_prev)
                    inst = v.tensor_tensor(h2_bf[:, hd, :], silc[:, :], wbc_sb[:, :],
                                           ALU.mult)
                    vs.inc(inst)
                h2ev[hd % 2] = vs.n

            h2_done = vs.n

            @block.sync
            def _(s):
                s.wait_ge(vs.h, h2_done)
                hs.dma(s.dma_start(out=h2w_dram[:, e * NH:(e + 1) * NH, t0:t0 + TT],
                                   in_=h2_bf[:, :, :]))

    ph_a_done = hs.n

    if _os.environ.get("L0F_DBG"):
        dbg_w8 = p.dram_out("dbg_w8", [128, T // 128, E])
        dbg_h2 = p.dram_out("dbg_h2", [128, NH, T])
        dbg_ga = p.dram_out("dbg_ga", [128, 128])
        dbg_pos = p.dram_out("dbg_pos", [8, T])
        dbg_oh = p.dram_out("dbg_oh", [8, T])
        dbg_pri = p.dram_out("dbg_pri", [8, 1])
        dbg_ohtm = p.dram_out("dbg_ohtm", [128, T // 128, 8])
        dbg_scr = p.dram_out("dbg_scr", [128, 8])
        dbg_m8 = p.dram_out("dbg_m8", [128, 8])
        dbg_i8 = p.dram_out("dbg_i8", [128, 8], U32)
        dbg_idxf = p.dram_out("dbg_idxf", [128, 8])
        dbg_col = p.dram_out("dbg_col", [128, 8])

        @block.gpsimd
        def _(g):
            g.wait_ge(hs.h, ph_a_done)
            lg.dma(g.dma_start(out=dbg_h2, in_=h2w_dram[:, 0:NH, :]))
            lg.dma(g.dma_start(out=dbg_w8, in_=W8[:, :, :]))
            lg.dma(g.dma_start(out=dbg_ga, in_=gate_acc[:, :]))
            lg.dma(g.dma_start(out=dbg_pos, in_=pos_em[:, :]))
            lg.dma(g.dma_start(out=dbg_oh, in_=oh_em[:, :]))
            lg.dma(g.dma_start(out=dbg_pri, in_=prior_sb[:, :]))
            lg.dma(g.dma_start(out=dbg_ohtm, in_=oh_tm[:, :, :]))
            lg.dma(g.dma_start(out=dbg_scr, in_=e8_sc[:, :]))
            lg.dma(g.dma_start(out=dbg_m8, in_=m8_sc[:, :]))
            lg.dma(g.dma_start(out=dbg_i8, in_=i8_sc[:, :]))
            lg.dma(g.dma_start(out=dbg_idxf, in_=idxf_sc[:, :]))
            lg.dma(g.dma_start(out=dbg_col, in_=col_sc[:, :]))

    for tt in range(NTT):
        t0 = tt * TT
        for e in range(E):
            h2_prev_mm = ts.n

            @block.sync
            def _(s):
                s.wait_ge(hs.h, ph_a_done)
                s.wait_ge(ts.h, h2_prev_mm)
                ld.dma(s.dma_start(out=h2in[:, :, :],
                                   in_=h2w_dram[:, e * NH:(e + 1) * NH, t0:t0 + TT]))

            h2_ld = ld.n
            wq_prev_mm = ts.n
            wbuf = wbufs[e % 2]

            @block.gpsimd
            def _(g):
                g.wait_ge(ts.h, wq_prev_mm)
                lg.dma(g.dma_start(out=wbuf[:, :, :], in_=w2_h[:, e, :, :]))

            wq_ld = lg.n
            for oc in range(KC):
                racc_prev = vs.n

                @block.tensor
                def _(t):
                    t.wait_ge(ld.h, h2_ld)
                    t.wait_ge(lg.h, wq_ld)
                    t.wait_ge(vs.h, racc_prev)
                    if e == 0:
                        t.matmul(ps5[:, :], b2_sb[:, oc * 128:(oc + 1) * 128],
                                 W8T[:, t0:t0 + TT], start=True, stop=False)
                    for j in range(NH):
                        inst = t.matmul(ps5[:, :], wbuf[:, j, oc * 128:(oc + 1) * 128],
                                        h2in[:, j, :],
                                        start=(j == 0 and e != 0), stop=(j == NH - 1))
                    ts.inc(inst)

                r_mm = ts.n

                @block.vector
                def _(v):
                    v.wait_ge(ts.h, r_mm)
                    if e == 0:
                        inst = v.tensor_copy(racc[:, oc, :], ps5[:, :])
                    else:
                        inst = v.tensor_add(racc[:, oc, :], racc[:, oc, :], ps5[:, :])
                    vs.inc(inst)

        @block.sync
        def _(s):
            s.wait_ge(vs.h, vs.n)
            ld.dma(s.dma_start(out=x_t[:, :, :], in_=x_hbm[:, :, t0:t0 + TT]))

        x_ld = ld.n
        for oc in range(KC):
            @block.vector
            def _(v):
                v.wait_ge(ld.h, x_ld)
                if oc == 0:
                    v.wait_ge(os_.h, max(0, os_.n - 16 * 6))
                inst = v.tensor_add(racc[:, oc, :], racc[:, oc, :], x_t[:, oc, :])
                vs.inc(inst)

            y_done = vs.n

            @block.sync
            def _(s):
                s.wait_ge(vs.h, y_done)
                os_.dma(s.dma_start(out=y_hbm[:, oc, t0:t0 + TT], in_=racc[:, oc, :]))

    @block.sync
    def _(s):
        s.wait_ge(os_.h, os_.n)

    p.close()
    return p


def host_inputs_l0_ffn(x_row, w, core_id):
    E, IN, NH = 8, 1024, 8
    w1 = np.asarray(w["W1"])
    w2 = np.asarray(w["W2"])
    b1 = np.asarray(w["b1"])
    sel = np.zeros((64, E), np.float32)
    for r in range(core_id):
        for e in range(E):
            sel[r * E + e, e] = 1.0
    return {
        "x": fm_x(x_row),
        "wrT_f32": fm_w(w["Wr"]),
        "br_row": np.ascontiguousarray(np.asarray(w["br"])[None, :]),
        "w1T": np.ascontiguousarray(w1.reshape(E, KC, 128, IN).transpose(2, 0, 1, 3)),
        "w2T": np.ascontiguousarray(w2.reshape(E, NH, 128, D).transpose(2, 0, 1, 3)),
        "b1m": np.ascontiguousarray(b1.reshape(E, NH, 128).transpose(2, 0, 1)),
        "b2m": np.ascontiguousarray(np.asarray(w["b2"])),
        "iota8": np.broadcast_to(np.arange(8, dtype=np.float32)[None, :],
                                 (128, 8)).copy(),
        "ident": np.eye(128, dtype=np.float32),
        "identf": np.eye(128, dtype=np.float32),
        "sel_prior": sel,
    }




def build_l2_ffn(shared=None):
    import os as _os2
    H8 = 8
    NOC = 16
    EPS_BN = 1e-5
    p = Prog(shared)
    x_hbm = p.dram_in("x", [128, KC, T])
    wq_h = p.dram_in("wqT", [128, KC, 2048])
    bq_h = p.dram_in("bqm", [128, NOC])
    gam_h = p.dram_in("bn_g", [128, 2])
    bet_h = p.dram_in("bn_b", [128, 2])
    k0_h = p.dram_in("k0", [32, 128])
    k1_h = p.dram_in("k1", [32, 128])
    ed_h = p.dram_in("edT", [128, KC, D])
    eu_h = p.dram_in("euT", [128, KC, D])
    iota32t_h = p.dram_in("iota32t", [128, 16, 32])
    ident_h = p.dram_in("ident", [128, 128])
    y_hbm = p.dram_out("y", [128, KC, T])
    q_dram = p.nc.dram_tensor(p.prefix + "q_scr", [128, NOC, T], BF16, kind="Internal").ap()
    cc_in = p.nc.dram_tensor(p.prefix + "cc_in", [128, 4], F32).ap()
    cc_out = p.nc.dram_tensor(p.prefix + "cc_out", [128, 4], F32).ap()

    xn = p.sbuf("xn", [128, KC, T], BF16)
    wa = p.sbuf("wa", [128, KC, D], BF16)
    wb = p.sbuf("wb", [128, KC, D], BF16)
    x_t = p.sbuf("x_t", [128, KC, TT], F32)
    xbf_scr = p.sbuf("xbf_scr2", [128, KC, TT], BF16)
    sq_scr = p.sbuf("sq_scr2", [128, KC, TT], BF16)
    p.bufs["xbf_scr"] = xbf_scr
    p.bufs["sq_scr"] = sq_scr
    bq_sb = p.sbuf("bq_sb", [128, NOC], F32)
    gam_sb = p.sbuf("gam_sb", [128, 2], F32)
    bet_sb = p.sbuf("bet_sb", [128, 2], F32)
    q_t = p.sbuf("q_t", [128, 2, TT], BF16)
    rs_cols = p.sbuf("rs_cols", [128, NOC, NTT], F32)
    rq_cols = p.sbuf("rq_cols", [128, NOC, NTT], F32)
    stats = p.sbuf("stats", [128, 4], F32)
    bnp = p.sbuf("bnp", [128, 8], F32)
    k01 = p.sbuf("k01", [32, 128], F32)
    k0nT = p.sbuf("k0nT", [128, 32], BF16)
    k1nT = p.sbuf("k1nT", [128, 32], BF16)
    kcol = p.sbuf("kcol", [32, 2], F32)
    qbn = p.sbuf("qbn", [128, T], BF16)
    q2s = p.sbuf("q2s", [128, T], BF16)
    rcol = p.sbuf("rcol", [128, 16], F32)
    m_all = p.sbuf("m_all", [128, 16, H8, 2, 32], BF16)
    sums = p.sbuf("sums", [128, 16, H8, 2], F32)
    m8a = p.sbuf("m8a", [128, 16, 8], F32)
    idxa = p.sbuf("idxa", [128, 16, 8], F32)
    e4a = p.sbuf("e4a", [128, 16, 4], F32)
    rz_all = p.sbuf("rz_all", [128, 16, H8], F32)
    S_tm = p.sbuf("S_tm", [128, 1024], F32)
    S_t2 = p.sbuf("S_t2", [128, 1024], F32)
    G_t = p.sbuf("G_t", [128, 1024], BF16)
    SG_t = p.sbuf("SG_t", [128, 1024], BF16)
    SGT = p.sbuf("SGT", [128, 8, 128], BF16)
    iota32t = p.sbuf("iota32t_sb", [128, 16, 32], F32)
    ident_bf = p.sbuf("ident_bf", [128, 128], BF16)
    ones_bf = p.sbuf("ones_bf", [128, 1], BF16)
    onesrow_f32 = p.sbuf("onesrow_f32", [1, 128], F32)
    scr_f32 = p.sbuf("scr_f32", [128, 2 * TT], F32)
    row_scr = p.sbuf("row_scr", [1, 4 * TT], F32)
    m8_sc = p.sbuf("m8_sc", [128, 8], F32)
    i8_sc = p.sbuf("i8_sc", [128, 8], U32)
    xres8 = p.sbuf("xres8", [128, KC, 128], F32)
    y8 = p.sbuf("y8", [128, KC, 128], F32)

    ps0 = p.psum("ps0", [128, TT], F32)
    ps1 = p.psum("ps1", [128, TT], F32)
    ps_c = p.psum("ps_c", [33, TT], F32)
    ps3 = p.psum("ps3", [1, TT], F32)
    ps4 = p.psum("ps4", [128, 1024], BF16)
    ps5 = p.psum("ps5", [128, TT], F32)
    ps6 = p.psum("ps6", [128, TT], F32)
    p.bufs["ps_a"] = ps0
    p.bufs["ps_b"] = ps1

    p.start_block()
    block = p.block
    ld, ts, vs, ss, os_ = p.sem("ld"), p.sem("ts"), p.sem("vs"), p.sem("ss"), p.sem("os")
    lg = p.sem("lg")
    cs = p.sem("cs")
    qs = p.sem("qs")

    @block.gpsimd
    def _(g):
        lg.dma(g.dma_start(out=wa[:, :, :], in_=wq_h[:, :, 0:1024]))
        lg.dma(g.dma_start(out=wb[:, :, :], in_=wq_h[:, :, 1024:2048]))
        lg.dma(g.dma_start(out=ident_bf[:, :], in_=ident_h))

    @block.sync
    def _(s):
        ld.dma(s.dma_start(out=bq_sb[:, :], in_=bq_h))
        ld.dma(s.dma_start(out=gam_sb[:, :], in_=gam_h))
        ld.dma(s.dma_start(out=bet_sb[:, :], in_=bet_h))
        ld.dma(s.dma_start(out=iota32t[:, :, :], in_=iota32t_h))

    all_loaded = ld.n
    all_loaded_g = lg.n

    @block.vector
    def _(v):
        v.wait_ge(ld.h, all_loaded)
        v.wait_ge(lg.h, all_loaded_g)
        v.memset(ones_bf[:, :], 1.0)
        inst = v.memset(onesrow_f32[:, :], 1.0)
        vs.inc(inst)

    @block.scalar
    def _(s):
        s.wait_ge(ld.h, all_loaded)
        s.wait_ge(lg.h, all_loaded_g)

    emit_ln_stream(p, x_hbm, xn)
    ln_done = vs.n

    for oc in range(NOC):
        wsel = wa if oc < 8 else wb
        ocr = oc % 8
        for tt in range(NTT):
            t0 = tt * TT
            evac_prev = ss.n

            @block.tensor
            def _(t):
                t.wait_ge(ss.h, evac_prev)
                t.wait_ge(vs.h, ln_done)
                for kc in range(KC):
                    inst = t.matmul(ps0[:, :], wsel[:, kc, ocr * 128:(ocr + 1) * 128],
                                    xn[:, kc, t0:t0 + TT],
                                    start=(kc == 0), stop=(kc == KC - 1))
                ts.inc(inst)

            q_mm = ts.n
            slot = (oc * NTT + tt) % 2
            st_prev = qs.n

            @block.scalar
            def _(s):
                s.wait_ge(ts.h, q_mm)
                s.wait_ge(qs.h, max(0, st_prev - 16))
                s.activation(q_t[:, slot, :], ps0[:, :], AF.Identity,
                             bias=bq_sb[:, oc:oc + 1], scale=1.0,
                             accum_out=rs_cols[:, oc, tt:tt + 1])
                inst = s.activation(q2s[:, 0:TT], ps0[:, :], AF.Square,
                                    bias=bq_sb[:, oc:oc + 1], scale=1.0,
                                    accum_out=rq_cols[:, oc, tt:tt + 1])
                ss.inc(inst)

            q_ev = ss.n

            @block.sync
            def _(s):
                s.wait_ge(ss.h, q_ev)
                qs.dma(s.dma_start(out=q_dram[:, oc, t0:t0 + TT], in_=q_t[:, slot, :]))

    q_all = ss.n

    wq_mm_done = ts.n

    @block.gpsimd
    def _(g):
        g.wait_ge(ts.h, wq_mm_done)
        lg.dma(g.dma_start(out=wa[:, :, :], in_=ed_h))
        lg.dma(g.dma_start(out=wb[:, :, :], in_=eu_h))

    edu_ld = lg.n

    @block.vector
    def _(v):
        v.wait_ge(ss.h, q_all)
        def F(inst):
            vs.inc(inst)
            v.wait_ge(vs.h, vs.n)
        F(v.reduce_sum(stats[:, 0:1], rs_cols[:, 0:NOC:2, :], axis=AX.XY))
        F(v.reduce_sum(stats[:, 1:2], rs_cols[:, 1:NOC:2, :], axis=AX.XY))
        F(v.reduce_sum(stats[:, 2:3], rq_cols[:, 0:NOC:2, :], axis=AX.XY))
        F(v.reduce_sum(stats[:, 3:4], rq_cols[:, 1:NOC:2, :], axis=AX.XY))

    st_done = vs.n

    @block.gpsimd
    def _(g):
        g.wait_ge(vs.h, st_done)
        lg.dma(g.dma_start(out=cc_in, in_=stats[:, :]))
        g.wait_ge(lg.h, lg.n)
        g.collective_compute("AllReduce", ALU.add, replica_groups=[list(range(8))],
                             ins=[cc_in], outs=[cc_out]).then_inc(cs.h, 1)
        cs.n += 1
        g.wait_ge(cs.h, cs.n)
        lg.dma(g.dma_start(out=stats[:, :], in_=cc_out))

    st_ld = lg.n
    NBN = float(8 * T * H8)

    @block.vector
    def _(v):
        v.wait_ge(lg.h, st_ld)
        def F(inst):
            vs.inc(inst)
            v.wait_ge(vs.h, vs.n)
        F(v.tensor_scalar_mul(bnp[:, 0:2], stats[:, 0:2], 1.0 / NBN))
        F(v.tensor_scalar_mul(bnp[:, 2:4], stats[:, 2:4], 1.0 / NBN))
        F(v.tensor_tensor(bnp[:, 4:6], bnp[:, 0:2], bnp[:, 0:2], ALU.mult))
        F(v.tensor_tensor(bnp[:, 2:4], bnp[:, 2:4], bnp[:, 4:6], ALU.subtract))
        F(v.tensor_scalar_add(bnp[:, 2:4], bnp[:, 2:4], EPS_BN))

    bn1 = vs.n

    @block.scalar
    def _(s):
        s.wait_ge(vs.h, bn1)
        inst = s.activation(bnp[:, 2:4], bnp[:, 2:4], AF.Sqrt)
        ss.inc(inst)

    bn2 = ss.n

    @block.vector
    def _(v):
        v.wait_ge(ss.h, bn2)
        inst = v.reciprocal(bnp[:, 2:4], bnp[:, 2:4])
        vs.inc(inst)
        v.wait_ge(vs.h, vs.n)

    bn_done = vs.n

    if _os2.environ.get("L2F_DBG"):
        dbg_st = p.dram_out("dbg_st", [128, 4])
        dbg_bnp = p.dram_out("dbg_bnp", [128, 8])
        dbg_rs = p.dram_out("dbg_rs", [128, NOC, NTT])
        dbg_rq = p.dram_out("dbg_rq", [128, NOC, NTT])

        @block.gpsimd
        def _(g):
            g.wait_ge(vs.h, bn_done)
            lg.dma(g.dma_start(out=dbg_st, in_=stats[:, :]))
            lg.dma(g.dma_start(out=dbg_bnp, in_=bnp[:, :]))
            lg.dma(g.dma_start(out=dbg_rs, in_=rs_cols[:, :, :]))
            lg.dma(g.dma_start(out=dbg_rq, in_=rq_cols[:, :, :]))

    for which, kh, kT in [(0, k0_h, k0nT), (1, k1_h, k1nT)]:
        @block.sync
        def _(s):
            s.wait_ge(vs.h, bn_done if which == 0 else vs.n)
            ld.dma(s.dma_start(out=k01[:, :], in_=kh))

        k_ld = ld.n

        @block.scalar
        def _(s):
            s.wait_ge(ld.h, k_ld)
            inst = s.activation(scr_f32[0:32, 0:128], k01[:, :], AF.Square)
            ss.inc(inst)

        k_sq = ss.n

        @block.vector
        def _(v):
            v.wait_ge(ss.h, k_sq)
            inst = v.reduce_sum(kcol[:, 0:1], scr_f32[0:32, 0:128], axis=AX.X)
            vs.inc(inst)
            v.wait_ge(vs.h, vs.n)

        k_rs = vs.n

        @block.scalar
        def _(s):
            s.wait_ge(vs.h, k_rs)
            inst = s.activation(kcol[:, 0:1], kcol[:, 0:1], AF.Sqrt)
            ss.inc(inst)

        k_sq2 = ss.n

        @block.vector
        def _(v):
            v.wait_ge(ss.h, k_sq2)
            inst = v.tensor_scalar_max(kcol[:, 0:1], kcol[:, 0:1], 1e-12)
            vs.inc(inst)
            v.wait_ge(vs.h, vs.n)
            inst = v.reciprocal(kcol[:, 1:2], kcol[:, 0:1])
            vs.inc(inst)
            v.wait_ge(vs.h, vs.n)
            inst = v.tensor_scalar(q2s[0:32, 0:128], k01[:, :], kcol[:, 1:2],
                                   None, ALU.mult)
            vs.inc(inst)

        k_n = vs.n

        @block.tensor
        def _(t):
            t.wait_ge(vs.h, k_n)
            inst = t.transpose(ps4[:, 0:32], q2s[0:32, 0:128], ident_bf[0:32, 0:32])
            ts.inc(inst)

        k_t = ts.n

        @block.vector
        def _(v):
            v.wait_ge(ts.h, k_t)
            inst = v.tensor_copy(kT[:, :], ps4[:, 0:32])
            vs.inc(inst)

    keys_done = vs.n

    for h in range(H8):
        for half in range(2):
            oc = 2 * h + half
            chunk_prev = vs.n

            @block.sync
            def _(s):
                s.wait_ge(vs.h, max(keys_done, chunk_prev))
                s.wait_ge(qs.h, qs.n)
                ld.dma(s.dma_start(out=qbn[:, :], in_=q_dram[:, oc, :]))

            qc_ld = ld.n

            @block.vector
            def _(v):
                v.wait_ge(ld.h, qc_ld)
                v.tensor_scalar(qbn[:, :], qbn[:, :], bnp[:, half:half + 1],
                                bnp[:, 2 + half:3 + half], ALU.subtract, ALU.mult)
                inst = v.tensor_scalar(qbn[:, :], qbn[:, :], gam_sb[:, half:half + 1],
                                       bet_sb[:, half:half + 1], ALU.mult, ALU.add)
                vs.inc(inst)

            bn_ap = vs.n

            @block.scalar
            def _(s):
                s.wait_ge(vs.h, bn_ap)
                inst = s.activation(q2s[:, :], qbn[:, :], AF.Square)
                ss.inc(inst)

            sq_done = ss.n

            for ti in range(NTT):
                ev_prev = vs.n

                @block.tensor
                def _(t):
                    if ti == 0:
                        t.wait_ge(ss.h, sq_done)
                    t.wait_ge(vs.h, ev_prev)
                    inst = t.matmul(ps3[0:1, :], ones_bf[:, 0:1],
                                    q2s[:, ti * TT:(ti + 1) * TT],
                                    start=True, stop=True)
                    ts.inc(inst)

                n_mm = ts.n

                @block.vector
                def _(v):
                    v.wait_ge(ts.h, n_mm)
                    inst = v.tensor_copy(row_scr[0:1, ti * TT:(ti + 1) * TT], ps3[0:1, :])
                    vs.inc(inst)

            nr_done = vs.n

            @block.scalar
            def _(s):
                s.wait_ge(vs.h, nr_done)
                inst = s.activation(row_scr[0:1, 0:T], row_scr[0:1, 0:T], AF.Sqrt)
                ss.inc(inst)

            nsq = ss.n

            @block.vector
            def _(v):
                v.wait_ge(ss.h, nsq)
                inst = v.tensor_scalar_max(row_scr[0:1, 0:T], row_scr[0:1, 0:T],
                                           1e-12)
                vs.inc(inst)
                v.wait_ge(vs.h, vs.n)
                inst = v.reciprocal(row_scr[0:1, 0:T], row_scr[0:1, 0:T])
                vs.inc(inst)
                v.wait_ge(vs.h, vs.n)

            nrec = vs.n

            @block.tensor
            def _(t):
                t.wait_ge(vs.h, nrec)
                for q in range(16):
                    inst = t.matmul(ps6[:, q:q + 1], row_scr[0:1, q * 128:(q + 1) * 128],
                                    onesrow_f32[0:1, 0:1], start=True, stop=True)
                ts.inc(inst)

            rc_mm = ts.n

            @block.vector
            def _(v):
                v.wait_ge(ts.h, rc_mm)
                inst = v.tensor_copy(rcol[:, :], ps6[:, 0:16])
                vs.inc(inst)

            rc_done = vs.n
            kT = k0nT if half == 0 else k1nT

            for t128 in range(16):
                t0 = t128 * 128
                sc_prev = vs.n

                @block.tensor
                def _(t):
                    t.wait_ge(vs.h, max(rc_done, sc_prev))
                    inst = t.matmul(ps5[:, 0:32], qbn[:, t0:t0 + 128], kT[:, :],
                                    start=True, stop=True)
                    ts.inc(inst)

                s_mm = ts.n

                @block.vector
                def _(v):
                    v.wait_ge(ts.h, s_mm)
                    inst = v.tensor_scalar(scr_f32[:, 0:32], ps5[:, 0:32],
                                           rcol[:, t128:t128 + 1], None, ALU.mult)
                    vs.inc(inst)
                    v.wait_ge(vs.h, vs.n)
                    inst = v.max(out=m8_sc[:, :], in_=scr_f32[:, 0:32])
                    vs.inc(inst)
                    v.wait_ge(vs.h, vs.n)
                    inst = v.max_index(out=i8_sc[:, :], in_max=m8_sc[:, :],
                                       in_values=scr_f32[:, 0:32])
                    vs.inc(inst)
                    v.wait_ge(vs.h, vs.n)
                    inst = v.tensor_copy(m8a[:, t128, :], m8_sc[:, :])
                    vs.inc(inst)
                    v.wait_ge(vs.h, vs.n)
                    inst = v.tensor_copy(idxa[:, t128, :], i8_sc[:, :])
                    vs.inc(inst)
                    v.wait_ge(vs.h, vs.n)

            tops_done = vs.n

            @block.vector
            def _(v):
                v.wait_ge(vs.h, tops_done)
                mx = m8a[:, :, 0:1].to_broadcast([128, 16, 4])
                inst = v.tensor_tensor(e4a[:, :, :], m8a[:, :, 0:4], mx, ALU.subtract)
                vs.inc(inst)

            e4s = vs.n

            @block.scalar
            def _(s):
                s.wait_ge(vs.h, e4s)
                inst = s.activation(e4a[:, :, :], e4a[:, :, :], AF.Exp)
                ss.inc(inst)

            e4e = ss.n

            @block.vector
            def _(v):
                v.wait_ge(ss.h, e4e)
                inst = v.reduce_sum(sums[:, :, h, half:half + 1], e4a[:, :, :],
                                    axis=AX.X)
                vs.inc(inst)
                v.wait_ge(vs.h, vs.n)
                mv = m_all[:, :, h, half, :]
                v.memset(mv, 0.0)
                for j in range(4):
                    idb = idxa[:, :, j:j + 1].to_broadcast([128, 16, 32])
                    v.tensor_tensor(scr_f32[:, 0:TT].rearrange(
                        "p (a b) -> p a b", b=32), iota32t[:, :, :], idb, ALU.is_equal)
                    e4b = e4a[:, :, j:j + 1].to_broadcast([128, 16, 32])
                    v.tensor_tensor(scr_f32[:, TT:2 * TT].rearrange(
                        "p (a b) -> p a b", b=32),
                        scr_f32[:, 0:TT].rearrange("p (a b) -> p a b", b=32),
                        e4b, ALU.mult)
                    inst = v.tensor_tensor(mv, mv, scr_f32[:, TT:2 * TT].rearrange(
                        "p (a b) -> p a b", b=32), ALU.add)
                vs.inc(inst)

    mvec_done = vs.n

    @block.vector
    def _(v):
        v.wait_ge(vs.h, mvec_done)
        inst = v.tensor_tensor(rz_all[:, :, :], sums[:, :, :, 0], sums[:, :, :, 1],
                               ALU.mult)
        vs.inc(inst)
        v.wait_ge(vs.h, vs.n)
        inst = v.reciprocal(rz_all[:, :, :], rz_all[:, :, :])
        vs.inc(inst)
        v.wait_ge(vs.h, vs.n)

    rz_done = vs.n

    for t128 in range(16):
        t0 = t128 * 128

        @block.vector
        def _(v):
            if t128 == 0:
                v.wait_ge(vs.h, rz_done)
            for h in range(H8):
                m0b = m_all[:, t128, h, 0, :].unsqueeze(2).to_broadcast([128, 32, 32])
                m1b = m_all[:, t128, h, 1, :].unsqueeze(1).to_broadcast([128, 32, 32])
                Sv = S_tm[:, :].rearrange("p (a b) -> p a b", b=32)
                Tv = S_t2[:, :].rearrange("p (a b) -> p a b", b=32)
                v.tensor_tensor(Tv, m0b, m1b, ALU.mult)
                if h == 0:
                    inst = v.tensor_scalar(Sv, Tv, rz_all[:, t128, h:h + 1], None,
                                           ALU.mult)
                else:
                    inst = v.scalar_tensor_tensor(Sv, Tv, rz_all[:, t128, h:h + 1],
                                                  Sv, ALU.mult, ALU.add)
            vs.inc(inst)

        s_done = vs.n

        gel_prev = ss.n

        @block.tensor
        def _(t):
            t.wait_ge(lg.h, edu_ld)
            t.wait_ge(ss.h, gel_prev)
            for kc in range(KC):
                t.matmul(ps0[:, :], xn[:, kc, t0:t0 + 128], wa[:, kc, 0:512],
                         start=(kc == 0), stop=(kc == KC - 1))
            for kc in range(KC):
                inst = t.matmul(ps1[:, :], xn[:, kc, t0:t0 + 128], wa[:, kc, 512:1024],
                                start=(kc == 0), stop=(kc == KC - 1))
            ts.inc(inst)

        eh_mm = ts.n

        @block.scalar
        def _(s):
            s.wait_ge(ts.h, eh_mm)
            s.activation(G_t[:, 0:512], ps0[:, 0:128].rearrange("p q -> p q")
                         if False else ps0[:, :], AF.Gelu)
            inst = s.activation(G_t[:, 512:1024], ps1[:, :], AF.Gelu)
            ss.inc(inst)

        g_done = ss.n

        @block.vector
        def _(v):
            v.wait_ge(ss.h, g_done)
            v.wait_ge(vs.h, s_done)
            inst = v.tensor_tensor(SG_t[:, :], S_tm[:, :], G_t[:, :], ALU.mult)
            vs.inc(inst)

        sg_done = vs.n

        @block.tensor
        def _(t):
            t.wait_ge(vs.h, sg_done)
            for ec in range(8):
                inst = t.transpose(ps4[:, ec * 128:(ec + 1) * 128],
                                   SG_t[:, ec * 128:(ec + 1) * 128], ident_bf[:, :])
            ts.inc(inst)

        sgt_mm = ts.n

        @block.vector
        def _(v):
            v.wait_ge(ts.h, sgt_mm)
            inst = v.tensor_copy(SGT[:, :, :],
                                 ps4[:, :].rearrange("p (a b) -> p a b", b=128))
            vs.inc(inst)

        sgt_done = vs.n

        @block.sync
        def _(s):
            s.wait_ge(vs.h, sgt_done)
            ld.dma(s.dma_start(out=xres8[:, :, :], in_=x_hbm[:, :, t0:t0 + 128]))

        x_ld = ld.n
        y_prev = os_.n
        for oc in range(KC):
            y_ev_prev = vs.n

            @block.tensor
            def _(t):
                if oc == 0:
                    t.wait_ge(vs.h, sgt_done)
                t.wait_ge(vs.h, y_ev_prev)
                for ec in range(8):
                    inst = t.matmul(ps5[:, 0:128], wb[:, ec, oc * 128:(oc + 1) * 128],
                                    SGT[:, ec, :], start=(ec == 0), stop=(ec == 7))
                ts.inc(inst)

            u_mm = ts.n

            @block.vector
            def _(v):
                v.wait_ge(ts.h, u_mm)
                v.wait_ge(ld.h, x_ld)
                if oc == 0:
                    v.wait_ge(os_.h, y_prev)
                inst = v.tensor_add(y8[:, oc, :], ps5[:, 0:128], xres8[:, oc, :])
                vs.inc(inst)

        yt_done = vs.n

        @block.sync
        def _(s):
            s.wait_ge(vs.h, yt_done)
            os_.dma(s.dma_start(out=y_hbm[:, :, t0:t0 + 128], in_=y8[:, :, :]))

    @block.sync
    def _(s):
        s.wait_ge(os_.h, os_.n)

    p.close()
    return p


def host_inputs_l2_ffn(x_row, w):
    iota32t = np.broadcast_to(np.arange(32, dtype=np.float32)[None, None, :],
                              (128, 16, 32)).copy()
    return {
        "x": fm_x(x_row),
        "wqT": fm_w(w["Wq"]),
        "bqm": fm_vec(w["bq"]),
        "bn_g": np.ascontiguousarray(np.asarray(w["bn_gamma"]).reshape(2, 128).T),
        "bn_b": np.ascontiguousarray(np.asarray(w["bn_beta"]).reshape(2, 128).T),
        "k0": np.ascontiguousarray(w["sub_key0"]),
        "k1": np.ascontiguousarray(w["sub_key1"]),
        "edT": fm_w(np.ascontiguousarray(np.asarray(w["expert_down"]).T)),
        "euT": fm_w(np.asarray(w["expert_up"])),
        "iota32t": iota32t,
        "ident": np.eye(128, dtype=np.float32),
    }


_SLOTS = [
    ("l0a_", "build_l0_attn"),
    ("l0f_", "build_l0_ffn"),
    ("l1a_", "build_l1_attn"),
    ("l1f_", "build_l1_ffn"),
    ("l2a_", "build_l2_attn"),
    ("l2f_", "build_l2_ffn"),
    ("l3a_", "build_l3_attn"),
    ("l3f_", "build_l3_ffn"),
]


def build_fused():
    root = Prog()
    nc = root.nc
    x_in = nc.dram_tensor("x", [128, KC, T], F32, kind="ExternalInput").ap()
    y_out = nc.dram_tensor("y", [128, KC, T], F32, kind="ExternalOutput").ap()
    states = [x_in]
    for i in range(7):
        states.append(nc.dram_tensor(f"xs{i}", [128, KC, T], F32,
                                     kind="ExternalOutput").ap())
    states.append(y_out)
    g = globals()
    for i, (prefix, fn) in enumerate(_SLOTS):
        shared = {"nc": nc, "sems": root.sems, "prefix": prefix,
                  "x_ap": states[i], "y_ap": states[i + 1],
                  "root_ctx": root.root_ctx}
        g[fn](shared)
    return root


_PROG_CACHE = {}


def _prog(name, build_fn):
    if name not in _PROG_CACHE:
        _PROG_CACHE[name] = build_fn()
    return _PROG_CACHE[name]


def _slot_host_inputs(inp, core_id):
    g = lambda n: inp[n]
    w0a = {k: g("L0_attn_" + k) for k in
           ["Wq", "bq", "Wk", "bk", "Wv", "bv", "Wo", "bo", "proj",
            "W1", "b1", "W2", "b2", "g2", "be2", "g3", "be3"]}
    w0f = {"Wr": g("L0_ffn_Wr"), "br": g("L0_ffn_br"), "W1": g("L0_ffn_W1"),
           "b1": g("L0_ffn_b1"), "W2": g("L0_ffn_W2"), "b2": g("L0_ffn_b2")}
    w1a = {k: g("L1_attn_" + k) for k in
           ["Wq", "bq", "Wk", "bk", "Wv", "bv", "Wo", "bo"]}
    w1f = {k: g("L1_ffn_" + k) for k in
           ["gate_w", "ws_in", "vs_in", "ws_out", "e_win", "e_vin", "e_wout"]}
    w2a = {k: g("L2_attn_" + k) for k in ["Wq", "Wk", "Wv", "Wo"]}
    w2f = {k: g("L2_ffn_" + k) for k in
           ["Wq", "bq", "bn_gamma", "bn_beta", "sub_key0", "sub_key1",
            "expert_down", "expert_up"]}
    w3a = {k: g("L3_attn_" + k) for k in
           ["Wq", "Wk", "Wv", "Wo", "bo", "betas", "mem", "z"]}
    w3f = {k: g("L3_ffn_" + k) for k in
           ["gw", "gb", "w1", "b1", "w3", "b3", "w2", "b2",
            "sw1", "sb1", "sw3", "sb3", "sw2", "sb2"]}
    parts = {
        "l0a_": host_inputs_l0_attn(None, w0a),
        "l0f_": host_inputs_l0_ffn(None, w0f, core_id),
        "l1a_": host_inputs_l1_attn(None, w1a),
        "l1f_": host_inputs_l1_ffn(None, w1f),
        "l2a_": host_inputs_l2_attn(None, w2a),
        "l2f_": host_inputs_l2_ffn(None, w2f),
        "l3a_": host_inputs_l3_attn(None, w3a),
        "l3f_": host_inputs_l3_ffn(None, w3f),
    }
    out = {}
    for prefix, d in parts.items():
        for k, v in d.items():
            if v is not None:
                out[prefix + k] = v
    return out


def kernel(**inp):
    ncores = 8
    if "fused" not in _PROG_CACHE:
        _PROG_CACHE["fused"] = build_fused()
    root = _PROG_CACHE["fused"]
    base = _slot_host_inputs(inp, 0)
    in_maps = []
    for c in range(ncores):
        im = dict(base) if c == 0 else {**base, **{
            k: v for k, v in _slot_host_inputs_core(inp, c).items()}}
        im["x"] = fm_x(inp["L0_attn_video"][c])
        in_maps.append(im)
    res = run_bass_kernel_spmd(root.nc, in_maps, core_ids=list(range(ncores)))
    return np.stack([fm_x_inv(np.asarray(res.results[c]["y"]))
                     for c in range(ncores)]).astype(np.float32)


def _slot_host_inputs_core(inp, core_id):
    w0f = {"Wr": inp["L0_ffn_Wr"], "br": inp["L0_ffn_br"], "W1": inp["L0_ffn_W1"],
           "b1": inp["L0_ffn_b1"], "W2": inp["L0_ffn_W2"], "b2": inp["L0_ffn_b2"]}
    d = host_inputs_l0_ffn(None, w0f, core_id)
    return {"l0f_" + k: v for k, v in d.items() if v is not None}
